# version 02
# baseline (speedup 1.0000x reference)
import math
from contextlib import ExitStack

import numpy as np
import concourse.bass as bass
import concourse.mybir as mybir
from concourse.bass_utils import run_bass_kernel_spmd
from concourse.bass import IndirectOffsetOnAxis

F32 = mybir.dt.float32
BF16 = mybir.dt.bfloat16
I32 = mybir.dt.int32
U32 = mybir.dt.uint32
AF = mybir.ActivationFunctionType
ALU = mybir.AluOpType
AX = mybir.AxisListType

ENGINES = ("sync", "tensor", "vector", "scalar", "gpsimd")
DEBUG_TAPS = []

DEBUG_ON = False
DEBUG_HBM = False
SIZES = {}
PARANOID = False


def tapd(P, A, name, key, ap_fn):
    if DEBUG_ON:
        ap = ap_fn()
        d = P.nc.dram_tensor("tap_" + name, list(ap.shape), ap.dtype, kind="ExternalOutput").ap()
        P.emit("gpsimd", lambda e: e.dma_start(out=d, in_=ap), reads=[key],
               writes=["tap_" + name], dma=True)


def tap(P, A, name, key, ap_fn):
    if DEBUG_ON:
        ap = ap_fn()
        tk = "tapt_" + name
        t = A.sb(tk, [ap.shape[0], int(np.prod(ap.shape[1:]))], ap.dtype)
        P.emit("vector", lambda e: e.tensor_copy(t[:], ap), reads=[key], writes=[tk])
        d = P.nc.dram_tensor("tap_" + name, list(t[:].shape), ap.dtype, kind="ExternalOutput").ap()
        P.emit("gpsimd", lambda e: e.dma_start(out=d, in_=t[:]), reads=[tk],
               writes=["tap_" + name], dma=True)

D_MODEL = 1024
SEQ = 4096
N_CORES = 8
NDM = D_MODEL // 128


class Prog:
    def __init__(self, nc):
        self.nc = nc
        self.ops = []
        self.lastw = {}
        self.readers = {}
        self.eng_last = {}

    def emit(self, eng, fn, reads=(), writes=(), sreads=(), dma=False, tag=""):
        i = len(self.ops)
        deps = set()
        sdep = set()
        for k in sreads:
            j = self.lastw.get(k)
            if j is not None:
                deps.add(j)
                sdep.add(j)
        for k in reads:
            j = self.lastw.get(k)
            if j is not None:
                deps.add(j)
        for k in writes:
            j = self.lastw.get(k)
            if j is not None:
                deps.add(j)
            deps |= self.readers.get(k, set())
        pruned = set()
        for j in deps:
            oj = self.ops[j]
            if (not PARANOID and oj["eng"] == eng and not oj["dma"] and not dma
                    and j not in sdep and not oj["small"]):
                continue
            pruned.add(j)
        small = any(SIZES.get(k, 1 << 30) <= 16 for k in writes)
        self.ops.append(dict(eng=eng, fn=fn, dma=dma, deps=pruned, tag=tag, small=small))
        self.eng_last[eng] = i
        for k in reads:
            self.readers.setdefault(k, set()).add(i)
        for k in sreads:
            self.readers.setdefault(k, set()).add(i)
        for k in writes:
            self.lastw[k] = i
            self.readers[k] = set()
        return i

    def barrier_read(self, eng, keys, tag="final"):
        self.emit(eng, lambda e: e.nop(), reads=tuple(keys), tag=tag)

    def fence(self):
        lasts = dict(self.eng_last)
        for eng in ENGINES:
            deps = {j for e2, j in lasts.items() if e2 != eng}
            i = len(self.ops)
            self.ops.append(dict(eng=eng, fn=lambda e: e.nop(), dma=False,
                                 deps=deps, tag="fence", small=False))
            self.eng_last[eng] = i

    def materialize(self, block, sems):
        needed = set()
        for op in self.ops:
            needed |= op["deps"]
        sig_at = {}
        counters = {k: 0 for k in sems}
        for i, op in enumerate(self.ops):
            if op["dma"] or i in needed:
                k = (op["eng"], "d" if op["dma"] else "c")
                counters[k] += 16 if op["dma"] else 1
                sig_at[i] = (k, counters[k])
        prewaits = [None] * len(self.ops)
        bound = {}
        for i, op in enumerate(self.ops):
            if not op["dma"]:
                continue
            k = (op["eng"], "d")
            b = bound.get(k)
            if b is not None:
                prewaits[i] = b
            if i in needed:
                bound[k] = sig_at[i][1]
        waits = []
        marks = {eng: {} for eng in ENGINES}
        idxs = {eng: [] for eng in ENGINES}
        for i, op in enumerate(self.ops):
            w = {}
            for j in op["deps"]:
                k, v = sig_at[j]
                if w.get(k, -1) < v:
                    w[k] = v
            eng = op["eng"]
            if prewaits[i] is not None:
                kq = (eng, "d")
                if w.get(kq, -1) < prewaits[i]:
                    w[kq] = prewaits[i]
            ww = []
            for k, v in w.items():
                if marks[eng].get(k, -1) >= v:
                    continue
                marks[eng][k] = v
                ww.append((k, v))
            waits.append(ww)
            idxs[eng].append(i)

        def mk(eng):
            def body(e):
                for i in idxs[eng]:
                    for (k, v) in waits[i]:
                        e.wait_ge(sems[k], v)
                    inst = self.ops[i]["fn"](e)
                    if i in sig_at:
                        k, _ = sig_at[i]
                        inst.then_inc(sems[k], 16 if k[1] == "d" else 1)
            return body

        for eng in ENGINES:
            getattr(block, eng)(mk(eng))
        return counters


class Alloc:
    def __init__(self, nc, stack):
        self.nc = nc
        self.stack = stack
        self.bytes = 0
        self.t = {}
        self.cms = {}
        self.rorder = []

    def sb(self, name, shape, dtype, side=None):
        assert name not in self.t, name
        SIZES[name] = int(np.prod(shape[1:]))
        self.colbytes = getattr(self, "colbytes", {})
        self.colbytes[name] = (int(np.prod(shape[1:])) * mybir.dt.size(dtype), side)
        self.bytes += 128 * int(np.prod(shape[1:])) * mybir.dt.size(dtype)
        self.uid = getattr(self, "uid", 0) + 1
        bname = f"{name}_u{self.uid}"
        cm = self.nc.sbuf_tensor(bname, shape, dtype, side=side) if side else self.nc.sbuf_tensor(bname, shape, dtype)
        h = cm.__enter__()
        self.t[name] = h
        self.cms[name] = cm
        if side == "right":
            self.rorder.append(name)
        return h

    def mark(self):
        return len(self.rorder)

    def free_left(self, *names):
        for name in reversed(names):
            self.cms.pop(name).__exit__(None, None, None)
            del self.t[name]

    def free_to(self, mark):
        while len(self.rorder) > mark:
            name = self.rorder.pop()
            self.cms.pop(name).__exit__(None, None, None)
            del self.t[name]

    def ps(self, name, shape, dtype):
        h = self.stack.enter_context(self.nc.psum_tensor(name, shape, dtype))
        self.t[name] = h
        return h


def build_common(P, A):
    g = {}
    g["ident_bf"] = A.sb("ident_bf", [128, 128], BF16)
    g["ident_f"] = A.sb("ident_f", [128, 128], F32)
    g["ones_col_f"] = A.sb("ones_col_f", [128, 1], F32)
    g["ones_col_bf"] = A.sb("ones_col_bf", [128, 1], BF16)
    g["ones_row_f"] = A.sb("ones_row_f", [1, 128], F32)
    g["ones_row_bf"] = A.sb("ones_row_bf", [1, 128], BF16)
    g["iota_i"] = A.sb("iota_i", [128, 128], I32)
    g["iota_f"] = A.sb("iota_f", [128, 128], F32)
    g["pid_f"] = A.sb("pid_f", [128, 1], F32)
    ii, pf, io = g["iota_i"], g["pid_f"], g["iota_f"]
    P.emit("gpsimd", lambda e: e.iota(ii[:], pattern=[[1, 128]], base=0, channel_multiplier=0),
           writes=["iota_i"])
    P.emit("gpsimd", lambda e: e.iota(pf[:], pattern=[[0, 1]], base=0, channel_multiplier=1,
                                      allow_small_or_imprecise_dtypes=True), writes=["pid_f"])
    P.emit("vector", lambda e: e.tensor_copy(io[:], ii[:]), reads=["iota_i"], writes=["iota_f"])
    ibf, iff = g["ident_bf"], g["ident_f"]
    P.emit("vector", lambda e: e.tensor_scalar(ibf[:], io[:], pf[:], None, op0=ALU.is_equal),
           reads=["iota_f"], sreads=["pid_f"], writes=["ident_bf"])
    P.emit("vector", lambda e: e.tensor_scalar(iff[:], io[:], pf[:], None, op0=ALU.is_equal),
           reads=["iota_f"], sreads=["pid_f"], writes=["ident_f"])
    g["ones_all_f"] = A.sb("ones_all_f", [128, 128], F32)
    A.sb("ln_sqs0", [128, 512], F32)
    A.sb("ln_sqs1", [128, 512], F32)
    A.sb("lnrow_m", [1, 512], F32)
    A.sb("lnrow_v", [1, 512], F32)
    A.sb("lnrow_r", [1, 512], F32)
    A.sb("ln_t", [128, 512], F32)
    for n in ("ones_col_f", "ones_col_bf", "ones_row_f", "ones_row_bf", "ones_all_f"):
        h = g[n]
        P.emit("vector", lambda e, h=h: e.memset(h[:], 1.0), writes=[n])
    return g


def load_weight_bf16(P, A, name, ap, shape, eng="sync", stage=None, side=None):
    if stage is None:
        f = A.sb(name + "_f", shape, F32)
        fkey = name + "_f"
    else:
        f = A.t[stage]
        fkey = stage
    b = A.sb(name, shape, BF16, side=side)
    stw = int(f[:].shape[1])
    for c0 in range(0, shape[1], stw):
        c1 = min(c0 + stw, shape[1])
        P.emit(eng, lambda e, c0=c0, c1=c1: e.dma_start(out=f[0:shape[0], 0:c1 - c0], in_=ap[:, c0:c1]),
               writes=[fkey], dma=True)
        P.emit("scalar", lambda e, c0=c0, c1=c1: e.activation(out=b[:, c0:c1], in_=f[0:shape[0], 0:c1 - c0], func=AF.Copy),
               reads=[fkey], writes=[name])
    return b


def ln_chunk(P, A, g, xk, tc, dst_names, phS="ph3", phR="ph4", rev=False,
             dst_key_suffix="", width=512):
    if "ln_sqs0" not in A.t:
        A.sb("ln_sqs0", [128, 512], F32)
        A.sb("ln_sqs1", [128, 512], F32)
        A.sb("lnrow_m", [1, 512], F32)
        A.sb("lnrow_v", [1, 512], F32)
        A.sb("lnrow_r", [1, 512], F32)
        A.sb("ln_t", [128, 512], F32)
    s = slice(tc * width, (tc + 1) * width)
    pS, pR = A.t[phS], A.t[phR]
    m, v, r = A.t["lnrow_m"], A.t["lnrow_v"], A.t["lnrow_r"]
    oc = g["ones_col_f"]
    orf = g["ones_row_f"]
    for i in range(NDM):
        sqn = f"ln_sqs{i % 2}"
        sq = A.t[sqn]
        xi = A.t[xk[i]]
        P.emit("scalar", lambda e, sq=sq, xi=xi, s=s: e.activation(out=sq[0:128, 0:width], in_=xi[:, s], func=AF.Square),
               reads=[xk[i]], writes=[sqn])
        P.emit("tensor", lambda e, xi=xi, s=s, st=(i == 0): e.matmul(
            pS[0:1, 0:width], oc[:], xi[:, s], start=st, stop=(i == NDM - 1)),
            reads=[xk[i], "ones_col_f"], writes=[phS])
        P.emit("tensor", lambda e, sq=sq, st=(i == 0), sp=(i == NDM - 1): e.matmul(
            pR[0:1, 0:width], oc[:], sq[0:128, 0:width], start=st, stop=sp),
            reads=[sqn, "ones_col_f"], writes=[phR])
    P.emit("vector", lambda e: e.tensor_scalar(m[0:1, 0:width], pS[0:1, 0:width], 1.0 / D_MODEL, None, op0=ALU.mult),
           reads=[phS], writes=["lnrow_m"])
    P.emit("vector", lambda e: e.tensor_tensor(v[0:1, 0:width], m[0:1, 0:width], m[0:1, 0:width], op=ALU.mult),
           reads=["lnrow_m"], writes=["lnrow_v"])
    P.emit("vector", lambda e: e.scalar_tensor_tensor(
        v[0:1, 0:width], pR[0:1, 0:width], 1.0 / D_MODEL, v[0:1, 0:width], op0=ALU.mult, op1=ALU.subtract),
        reads=[phR, "lnrow_v"], writes=["lnrow_v"])
    P.emit("vector", lambda e: e.tensor_scalar(v[0:1, 0:width], v[0:1, 0:width], 1e-5, None, op0=ALU.add),
           reads=["lnrow_v"], writes=["lnrow_v"])
    P.emit("scalar", lambda e: e.activation(out=v[0:1, 0:width], in_=v[0:1, 0:width], func=AF.Ln),
           reads=["lnrow_v"], writes=["lnrow_v"])
    P.emit("vector", lambda e: e.tensor_scalar(v[0:1, 0:width], v[0:1, 0:width], -0.5, None, op0=ALU.mult),
           reads=["lnrow_v"], writes=["lnrow_v"])
    P.emit("scalar", lambda e: e.activation(out=r[0:1, 0:width], in_=v[0:1, 0:width], func=AF.Exp),
           reads=["lnrow_v"], writes=["lnrow_r"])
    P.emit("tensor", lambda e: e.matmul(pS[:, 0:width], orf[0:1, :], m[0:1, 0:width], start=True, stop=True),
           reads=["lnrow_m", "ones_row_f"], writes=[phS])
    P.emit("tensor", lambda e: e.matmul(pR[:, 0:width], orf[0:1, :], r[0:1, 0:width], start=True, stop=True),
           reads=["lnrow_r", "ones_row_f"], writes=[phR])
    t = A.t["ln_t"]
    for i in range(NDM):
        xi = A.t[xk[i]]
        d = A.t[dst_names[i]]
        dk = dst_names[i] + dst_key_suffix
        P.emit("vector", lambda e, xi=xi, s=s: e.tensor_tensor(t[:, 0:width], xi[:, s], pS[:, 0:width], op=ALU.subtract),
               reads=[xk[i], phS], writes=["ln_t"])
        if rev:
            P.emit("vector", lambda e, d=d: e.tensor_tensor(d[:, width - 1::-1], t[:, 0:width], pR[:, 0:width], op=ALU.mult),
                   reads=["ln_t", phR], writes=[dk])
        else:
            P.emit("vector", lambda e, d=d: e.tensor_tensor(d[:, 0:width], t[:, 0:width], pR[:, 0:width], op=ALU.mult),
                   reads=["ln_t", phR], writes=[dk])


def sl_l1ffn(P, A, g, W, xk, first_call=True):
    CLAMP = float(np.log(1.0 / 0.01))
    sfx = "a" if first_call else "b"
    if first_call:
        if "wstage" not in A.t:
            A.sb("wstage", [128, 1024], F32)
        for i in range(NDM):
            load_weight_bf16(P, A, f"l1WT{i}", W["l1_WT"][i * 128:(i + 1) * 128, :], [128, 1024], stage="wstage")
            load_weight_bf16(P, A, f"l1WpT{i}", W["l1_WpT"][i * 128:(i + 1) * 128, :], [128, 256], stage="wstage")
            load_weight_bf16(P, A, f"l1WdT{i}", W["l1_WdT"][i * 128:(i + 1) * 128, :], [128, 15], stage="wstage")
        load_weight_bf16(P, A, "l1WuT", W["l1_WuT"][:, :], [15, 1024], stage="wstage")
        bp = A.sb("l1bp", [128, 2], F32)
        P.emit("sync", lambda e: e.dma_start(out=bp[:], in_=W["l1_bp"][:, :]), writes=["l1bp"], dma=True)
        simf = A.sb("l1simf", [128, 8], F32)
        P.emit("sync", lambda e: e.dma_start(out=simf[:], in_=W["l1_sim"][:, :]), writes=["l1simf"], dma=True)
        tempf = A.sb("l1temp", [1, 1], F32)
        P.emit("sync", lambda e: e.dma_start(out=tempf[:], in_=W["l1_temp"][:, :]), writes=["l1temp"], dma=True)
        load_weight_bf16(P, A, "l1ea", W["l1_ea"][:, :], [4, 15], stage="wstage")
        A.sb("l1_xd", [15, 512], F32)
        A.sb("l1_xds", [15, 512], BF16)
        A.sb("l1_proj", [128, 1024], F32)
        A.sb("l1_projn", [128, 1024], BF16)
        A.sb("l1_psq", [128, 512], F32)
        A.sb("l1_rstd", [1, 512], F32)
        A.sb("l1_logit", [4, 512], F32)
        A.sb("l1_tokg", [128, 4], F32)
        A.sb("l1_retT", [4, 512], BF16)
        A.sb("l1_s4", [1, 1], F32)
        A.sb("l1_zero4", [128, 4], F32)
        A.sb("l1_nm", [128, 1], F32)
        A.sb("l1_sum", [128, 1], F32)
        A.sb("l1_rec", [128, 1], F32)
        A.sb("l1_mx2", [128, 1], F32)
        A.sb("l1_cum", [128, 4], F32)
        A.sb("l1_msk", [128, 4], F32)
        A.sb("l1_lw", [128, 4], F32)
        A.sb("l1_rettm", [128, 4], BF16)
        A.sb("l1_nrm", [1, 8], F32)
        z4 = A.t["l1_zero4"]
        P.emit("vector", lambda e: e.memset(z4[:], 0.0), writes=["l1_zero4"])
        for i in range(NDM):
            A.sb(f"l1xc{i}", [128, 512], BF16)

    simn = A.sb(f"l1simn{sfx}", [128, 8], BF16)
    ssq = A.t["l1_psq"]
    simf = A.t["l1simf"]
    tempf = A.t["l1temp"]
    s4 = A.t["l1_s4"]
    ph0, ph1, ph2, ph3 = A.t["ph0"], A.t["ph1"], A.t["ph2"], A.t["ph3"]
    phbf = A.t["phbf"]
    ocf = g["ones_col_f"]
    orf = g["ones_row_f"]
    nrm = A.t["l1_nrm"]

    P.emit("vector", lambda e: e.tensor_scalar(s4[:], tempf[:], CLAMP, None, op0=ALU.min),
           reads=["l1temp"], writes=["l1_s4"])
    P.emit("scalar", lambda e: e.activation(out=s4[:], in_=s4[:], func=AF.Exp),
           reads=["l1_s4"], writes=["l1_s4"])
    P.emit("scalar", lambda e: e.activation(out=ssq[:, 0:8], in_=simf[:], func=AF.Square),
           reads=["l1simf"], writes=["l1_psq"])
    P.emit("tensor", lambda e: e.matmul(ph2[0:1, 0:8], ocf[:], ssq[:, 0:8], start=True, stop=True),
           reads=["l1_psq", "ones_col_f"], writes=["ph2"])
    P.emit("vector", lambda e: e.tensor_copy(nrm[0:1, 0:8], ph2[0:1, 0:8]),
           reads=["ph2"], writes=["l1_nrm"])
    P.emit("vector", lambda e: e.tensor_tensor(nrm[0:1, 0:4], nrm[0:1, 0:4], nrm[0:1, 4:8], op=ALU.add),
           reads=["l1_nrm"], writes=["l1_nrm"])
    P.emit("vector", lambda e: e.tensor_scalar(nrm[0:1, 0:4], nrm[0:1, 0:4], 1e-24, None, op0=ALU.max),
           reads=["l1_nrm"], writes=["l1_nrm"])
    P.emit("scalar", lambda e: e.activation(out=nrm[0:1, 0:4], in_=nrm[0:1, 0:4], func=AF.Ln),
           reads=["l1_nrm"], writes=["l1_nrm"])
    P.emit("vector", lambda e: e.tensor_scalar(nrm[0:1, 0:4], nrm[0:1, 0:4], -0.5, None, op0=ALU.mult),
           reads=["l1_nrm"], writes=["l1_nrm"])
    P.emit("scalar", lambda e: e.activation(out=nrm[0:1, 0:4], in_=nrm[0:1, 0:4], func=AF.Exp),
           reads=["l1_nrm"], writes=["l1_nrm"])
    P.emit("vector", lambda e: e.tensor_copy(nrm[0:1, 4:8], nrm[0:1, 0:4]),
           reads=["l1_nrm"], writes=["l1_nrm"])
    P.emit("vector", lambda e: e.tensor_scalar(nrm[0:1, :], nrm[0:1, :], s4[:], None, op0=ALU.mult),
           reads=["l1_nrm"], sreads=["l1_s4"], writes=["l1_nrm"])
    P.emit("tensor", lambda e: e.matmul(ph2[:, 8:16], orf[0:1, :], nrm[0:1, :], start=True, stop=True),
           reads=["l1_nrm", "ones_row_f"], writes=["ph2"])
    P.emit("vector", lambda e: e.tensor_tensor(simn[:], simf[:], ph2[:, 8:16], op=ALU.mult),
           reads=["l1simf", "ph2"], writes=[f"l1simn{sfx}"])

    xlnc = [f"l1xc{i}" for i in range(NDM)]
    xd = A.t["l1_xd"]
    xds = A.t["l1_xds"]
    proj = A.t["l1_proj"]
    projn = A.t["l1_projn"]
    psq = A.t["l1_psq"]
    rstd = A.t["l1_rstd"]
    logit = A.t["l1_logit"]
    retT = A.t["l1_retT"]
    ea = A.t["l1ea"]
    bp = A.t["l1bp"]
    z4 = A.t["l1_zero4"]

    for tc in range(8):
        ln_chunk(P, A, g, xk, tc, xlnc)
        for i in range(NDM):
            wd = A.t[f"l1WdT{i}"]
            xc = A.t[xlnc[i]]
            P.emit("tensor", lambda e, wd=wd, xc=xc, st=(i == 0), sp=(i == NDM - 1):
                   e.matmul(ph1[0:15, 0:512], wd[:], xc[:], start=st, stop=sp),
                   reads=[f"l1WdT{i}", xlnc[i]], writes=["ph1"])
        P.emit("vector", lambda e: e.tensor_copy(xd[:], ph1[0:15, 0:512]),
               reads=["ph1"], writes=["l1_xd"])
        for pc in range(2):
            for i in range(NDM):
                wp = A.t[f"l1WpT{i}"]
                xc = A.t[xlnc[i]]
                P.emit("tensor", lambda e, wp=wp, xc=xc, pc=pc, st=(i == 0), sp=(i == NDM - 1):
                       e.matmul(ph2[:, 0:512], wp[:, pc * 128:(pc + 1) * 128], xc[:], start=st, stop=sp),
                       reads=[f"l1WpT{i}", xlnc[i]], writes=["ph2"])
            P.emit("vector", lambda e, pc=pc: e.tensor_scalar(
                proj[:, pc * 512:(pc + 1) * 512], ph2[:, 0:512], bp[:, pc:pc + 1], None, op0=ALU.add),
                reads=["ph2"], sreads=["l1bp"], writes=["l1_proj"])
        for pc in range(2):
            P.emit("scalar", lambda e, pc=pc: e.activation(out=psq[:], in_=proj[:, pc * 512:(pc + 1) * 512], func=AF.Square),
                   reads=["l1_proj"], writes=["l1_psq"])
            P.emit("tensor", lambda e, st=(pc == 0), sp=(pc == 1):
                   e.matmul(ph2[0:1, 0:512], ocf[:], psq[:], start=st, stop=sp),
                   reads=["l1_psq", "ones_col_f"], writes=["ph2"])
        P.emit("vector", lambda e: e.tensor_scalar(rstd[:], ph2[0:1, 0:512], 1e-24, None, op0=ALU.max),
               reads=["ph2"], writes=["l1_rstd"])
        P.emit("scalar", lambda e: e.activation(out=rstd[:], in_=rstd[:], func=AF.Ln),
               reads=["l1_rstd"], writes=["l1_rstd"])
        P.emit("vector", lambda e: e.tensor_scalar(rstd[:], rstd[:], -0.5, None, op0=ALU.mult),
               reads=["l1_rstd"], writes=["l1_rstd"])
        P.emit("scalar", lambda e: e.activation(out=rstd[:], in_=rstd[:], func=AF.Exp),
               reads=["l1_rstd"], writes=["l1_rstd"])
        P.emit("tensor", lambda e: e.matmul(ph2[:, 0:512], orf[0:1, :], rstd[0:1, :], start=True, stop=True),
               reads=["l1_rstd", "ones_row_f"], writes=["ph2"])
        for pc in range(2):
            P.emit("vector", lambda e, pc=pc: e.tensor_tensor(
                projn[:, pc * 512:(pc + 1) * 512], proj[:, pc * 512:(pc + 1) * 512], ph2[:, 0:512], op=ALU.mult),
                reads=["l1_proj", "ph2"], writes=["l1_projn"])
        for pc in range(2):
            P.emit("tensor", lambda e, pc=pc, st=(pc == 0), sp=(pc == 1):
                   e.matmul(ph2[0:4, 0:512], simn[:, pc * 4:(pc + 1) * 4], projn[:, pc * 512:(pc + 1) * 512], start=st, stop=sp),
                   reads=[f"l1simn{sfx}", "l1_projn"], writes=["ph2"])
        P.emit("vector", lambda e: e.tensor_copy(logit[:], ph2[0:4, 0:512]),
               reads=["ph2"], writes=["l1_logit"])
        tokg = A.t["l1_tokg"]
        nm, sm, rec, mx2 = A.t["l1_nm"], A.t["l1_sum"], A.t["l1_rec"], A.t["l1_mx2"]
        cum, msk, lw, rtm = A.t["l1_cum"], A.t["l1_msk"], A.t["l1_lw"], A.t["l1_rettm"]
        for sb in range(4):
            cs = slice(sb * 128, (sb + 1) * 128)
            P.emit("tensor", lambda e, cs=cs: e.transpose(ph2[:, 16:20], logit[:, cs], g["ident_f"][0:4, 0:4]),
                   reads=["l1_logit", "ident_f"], writes=["ph2"])
            P.emit("vector", lambda e: e.tensor_reduce(nm[:], ph2[:, 16:20], axis=AX.X, op=ALU.max, negate=True),
                   reads=["ph2"], writes=["l1_nm"])
            P.emit("scalar", lambda e: e.activation(out=tokg[:], in_=ph2[:, 16:20], func=AF.Exp, bias=nm[:]),
                   reads=["ph2"], sreads=["l1_nm"], writes=["l1_tokg"])
            P.emit("vector", lambda e: e.tensor_reduce(sm[:], tokg[:], axis=AX.X, op=ALU.add),
                   reads=["l1_tokg"], writes=["l1_sum"])
            P.emit("vector", lambda e: e.reciprocal(rec[:], sm[:]), reads=["l1_sum"], writes=["l1_rec"])
            P.emit("vector", lambda e: e.tensor_scalar(lw[:], tokg[:], rec[:], None, op0=ALU.mult),
                   reads=["l1_tokg"], sreads=["l1_rec"], writes=["l1_lw"])
            P.emit("vector", lambda e: e.tensor_reduce(mx2[:], lw[:], axis=AX.X, op=ALU.max),
                   reads=["l1_lw"], writes=["l1_mx2"])
            P.emit("vector", lambda e: e.tensor_scalar(msk[:], lw[:], mx2[:], None, op0=ALU.is_equal),
                   reads=["l1_lw"], sreads=["l1_mx2"], writes=["l1_msk"])
            P.emit("vector", lambda e: e.tensor_tensor_scan(cum[:], z4[:], msk[:], 0.0, op0=ALU.add, op1=ALU.add),
                   reads=["l1_zero4", "l1_msk"], writes=["l1_cum"])
            P.emit("vector", lambda e: e.tensor_scalar(cum[:], cum[:], 1.0, None, op0=ALU.is_equal),
                   reads=["l1_cum"], writes=["l1_cum"])
            P.emit("vector", lambda e: e.tensor_tensor(cum[:], cum[:], msk[:], op=ALU.mult),
                   reads=["l1_cum", "l1_msk"], writes=["l1_cum"])
            P.emit("vector", lambda e: e.tensor_tensor(rtm[:], cum[:], lw[:], op=ALU.mult),
                   reads=["l1_cum", "l1_lw"], writes=["l1_rettm"])
            if tc == 0 and sb == 0 and DEBUG_ON:
                for nm2, src2 in [("lw", lw), ("msk", msk), ("cum", cum), ("tokg", tokg), ("rtm", rtm)]:
                    cp = A.sb("dbg_" + nm2, [128, 4], F32)
                    P.emit("vector", lambda e, cp=cp, src2=src2: e.tensor_copy(cp[:], src2[:]),
                           reads=["l1_" + nm2 if nm2 != "rtm" else "l1_rettm"], writes=["dbg_" + nm2])
                    tap(P, A, nm2, "dbg_" + nm2, lambda cp=cp: cp[:])
            P.emit("tensor", lambda e: e.transpose(phbf[0:4, 0:128], rtm[:], g["ident_bf"][:]),
                   reads=["l1_rettm", "ident_bf"], writes=["phbf"])
            P.emit("vector", lambda e, cs=cs: e.tensor_copy(retT[0:4, cs], phbf[0:4, 0:128]),
                   reads=["phbf"], writes=["l1_retT"])
        P.emit("tensor", lambda e: e.matmul(ph1[0:15, 0:512], ea[:], retT[:], start=True, stop=True),
               reads=["l1ea", "l1_retT"], writes=["ph1"])
        P.emit("vector", lambda e: e.tensor_tensor(xds[:], ph1[0:15, 0:512], xd[:], op=ALU.mult),
               reads=["ph1", "l1_xd"], writes=["l1_xds"])
        if tc == 0:
            tap(P, A, "xd", "l1_xd", lambda: A.t["l1_xd"][:])
            tap(P, A, "xds", "l1_xds", lambda: A.t["l1_xds"][:])
            tap(P, A, "retT", "l1_retT", lambda: A.t["l1_retT"][:])
            tap(P, A, "logit", "l1_logit", lambda: A.t["l1_logit"][:])
            tap(P, A, "proj", "l1_proj", lambda: A.t["l1_proj"][:])
            tap(P, A, "projn", "l1_projn", lambda: A.t["l1_projn"][:])
            tap(P, A, "simn", f"l1simn{sfx}", lambda: A.t[f"l1simn{sfx}"][:])
            tap(P, A, "xln0", "l1xc0", lambda: A.t["l1xc0"][:])
        for oc in range(NDM):
            for i in range(NDM):
                wt = A.t[f"l1WT{i}"]
                xc = A.t[xlnc[i]]
                P.emit("tensor", lambda e, wt=wt, xc=xc, oc=oc, st=(i == 0):
                       e.matmul(ph0[:, 0:512], wt[:, oc * 128:(oc + 1) * 128], xc[:], start=st, stop=False),
                       reads=[f"l1WT{i}", xlnc[i]], writes=["ph0"])
            wu = A.t["l1WuT"]
            P.emit("tensor", lambda e, oc=oc: e.matmul(ph0[:, 0:512], wu[:, oc * 128:(oc + 1) * 128], xds[:], start=False, stop=True),
                   reads=["l1WuT", "l1_xds"], writes=["ph0"])
            xo = A.t[xk[oc]]
            s = slice(tc * 512, (tc + 1) * 512)
            P.emit("vector", lambda e, xo=xo, s=s: e.scalar_tensor_tensor(
                xo[:, s], ph0[:, 0:512], 0.5, xo[:, s], op0=ALU.mult, op1=ALU.add),
                reads=["ph0", xk[oc]], writes=[xk[oc]])


def prep_l1ffn_weights(inputs):
    W = {}
    W["l1_WT"] = np.ascontiguousarray(inputs["L1_ffn_W"].T)
    W["l1_WpT"] = np.ascontiguousarray(inputs["L1_ffn_Wp"].T)
    W["l1_bp"] = np.ascontiguousarray(np.asarray(inputs["L1_ffn_bp"]).reshape(2, 128).T)
    W["l1_sim"] = np.ascontiguousarray(
        np.asarray(inputs["L1_ffn_sim"]).reshape(2, 128, 4).transpose(1, 0, 2).reshape(128, 8))
    W["l1_temp"] = np.ascontiguousarray(np.asarray(inputs["L1_ffn_temp"]).reshape(1, 1))
    wd = np.concatenate([np.asarray(inputs[f"L1_ffn_Wd{i}"]) for i in range(4)], axis=0)
    W["l1_WdT"] = np.ascontiguousarray(wd.T)
    wu = np.concatenate([np.asarray(inputs[f"L1_ffn_Wu{i}"]) for i in range(4)], axis=1)
    W["l1_WuT"] = np.ascontiguousarray(wu.T)
    ea = np.zeros((4, 15), np.float32)
    for e_i, (a, b) in enumerate([(0, 1), (1, 3), (3, 7), (7, 15)]):
        ea[e_i, a:b] = 1.0
    W["l1_ea"] = ea
    return W


L1FFN_W_SHAPES = dict(
    l1_WT=[1024, 1024], l1_WpT=[1024, 256], l1_bp=[128, 2], l1_sim=[128, 8],
    l1_temp=[1, 1], l1_WdT=[1024, 15], l1_WuT=[15, 1024], l1_ea=[4, 15])


def finalize(P, nc, A):
    with nc.Block() as block, ExitStack() as semstack:
        sems = {}
        for eng in ENGINES:
            for kk in ("c", "d"):
                sems[(eng, kk)] = semstack.enter_context(nc.semaphore(f"s_{eng}_{kk}"))
        P.materialize(block, sems)


def build_sl4_standalone():
    nc = bass.Bass("TRN2", target_bir_lowering=False)
    xin = nc.dram_tensor("xT", [D_MODEL, SEQ], F32, kind="ExternalInput").ap()
    xout = nc.dram_tensor("yT", [D_MODEL, SEQ], F32, kind="ExternalOutput").ap()
    Wd = {k: nc.dram_tensor(k, sh, F32, kind="ExternalInput").ap()
          for k, sh in L1FFN_W_SHAPES.items()}
    with ExitStack() as stack:
        A = Alloc(nc, stack)
        P = Prog(nc)
        g = build_common(P, A)
        xk = [f"x{i}" for i in range(NDM)]
        for i in range(NDM):
            x = A.sb(xk[i], [128, SEQ], F32)
            P.emit("sync", lambda e, x=x, i=i: e.dma_start(out=x[:], in_=xin[i * 128:(i + 1) * 128, :]),
                   writes=[xk[i]], dma=True)
        for pn in ["ph0", "ph1", "ph2", "ph3", "ph4"]:
            A.ps(pn, [128, 512], F32)
        A.ps("phbf", [128, 1024], BF16)
        sl_l1ffn(P, A, g, Wd, xk, first_call=True)
        for nm, tk in DEBUG_TAPS:
            t = A.t[tk]
            dt = nc.dram_tensor("tap_" + nm, list(t[:].shape), t[:].dtype, kind="ExternalOutput").ap()
            P.emit("gpsimd", lambda e, dt=dt, t=t: e.dma_start(out=dt, in_=t[:]),
                   reads=[tk], writes=["tapdma_" + nm], dma=True)
        for i in range(NDM):
            x = A.t[xk[i]]
            P.emit("gpsimd", lambda e, x=x, i=i: e.dma_start(out=xout[i * 128:(i + 1) * 128, :], in_=x[:]),
                   reads=[xk[i]], writes=[f"outdma{i}"], dma=True)
        P.barrier_read("gpsimd", [f"outdma{i}" for i in range(NDM)])
        finalize(P, nc, A)
        print(f"SBUF bytes: {A.bytes/1e6:.2f}MB, ops: {len(P.ops)}", flush=True)
    return nc




def cast_to_hbm_blocked(P, A, name, src_ap, rows, cols, dst_ap, stage):
    st_ = A.t[stage]
    bsn = stage + "_bf"
    bs = A.t[bsn]
    for rc in range(rows // 128):
        for cc in range(cols // 1024):
            P.emit("sync", lambda e, rc=rc, cc=cc: e.dma_start(
                out=st_[:, 0:1024], in_=src_ap[rc * 128:(rc + 1) * 128, cc * 1024:(cc + 1) * 1024]),
                writes=[stage], dma=True)
            P.emit("scalar", lambda e: e.activation(out=bs[:, 0:1024], in_=st_[:, 0:1024], func=AF.Copy),
                   reads=[stage], writes=[bsn])
            for j in range(8):
                mc = cc * 8 + j
                P.emit("gpsimd", lambda e, mc=mc, rc=rc, j=j: e.dma_start(
                    out=dst_ap[mc, rc, :, :], in_=bs[:, j * 128:(j + 1) * 128]),
                    reads=[bsn], writes=[name + "_blk"], dma=True)


def sl_l0attn(P, A, g, W, xk):
    nc = P.nc
    NH = 16
    ph0, ph1, ph2, ph3, ph4, ph5 = [A.t[f"ph{i}"] for i in range(6)]
    ocf, orf = g["ones_col_f"], g["ones_row_f"]
    kind = "ExternalOutput" if DEBUG_HBM else "Internal"
    hq = nc.dram_tensor("hbm_q", [1024, SEQ], BF16, kind=kind).ap()
    hk = nc.dram_tensor("hbm_k", [1024, SEQ], BF16, kind=kind).ap()
    hv = nc.dram_tensor("hbm_v", [32, 16, 128, 64], BF16, kind=kind).ap()
    havn = nc.dram_tensor("hbm_avn", [1024, SEQ], BF16, kind=kind).ap()
    if DEBUG_HBM:
        hpodbg = nc.dram_tensor("hbm_podbg", [8, 8, 128, 512], BF16, kind="ExternalOutput").ap()
        hpsdbg = nc.dram_tensor("hbm_psdbg", [8, 8, 128, 512], F32, kind="ExternalOutput").ap()
    else:
        hpodbg = hpsdbg = None
    hwq = nc.dram_tensor("hbm_wq", [24, 8, 128, 128], BF16).ap()
    hwqs = nc.dram_tensor("hbm_wqs", [16, 8, 128, 128], BF16).ap()
    if "wstage1k" not in A.t:
        A.sb("wstage1k", [128, 1024], F32)
        A.sb("wstage1k_bf", [128, 1024], BF16)
    pa = A.sb("att_pa_sb", [1, 32], F32)
    P.emit("sync", lambda e: e.dma_start(out=pa[:], in_=W["att_pa"][:, :]), writes=["att_pa"], dma=True)
    P.emit("vector", lambda e: e.tensor_scalar(pa[0:1, 0:32], pa[0:1, 0:32], 0.01, None, op0=ALU.max),
           reads=["att_pa"], writes=["att_pa"])
    P.emit("vector", lambda e: e.tensor_scalar(pa[0:1, 0:16], pa[0:1, 0:16], -1.0, None, op0=ALU.mult),
           reads=["att_pa"], writes=["att_pa"])
    fr = A.sb("att_fr_sb", [128, 1], F32)
    P.emit("sync", lambda e: e.dma_start(out=fr[:], in_=W["att_freqs"][:, :]), writes=["att_fr"], dma=True)
    sg = A.sb("att_sg_sb", [128, 1], F32)
    P.emit("sync", lambda e: e.dma_start(out=sg[:], in_=W["att_sgn"][:, :]), writes=["att_sg"], dma=True)
    cast_to_hbm_blocked(P, A, "awq", W["att_qkvT"], 1024, 3072, hwq, "wstage1k")
    cast_to_hbm_blocked(P, A, "awqs", W["att_qkvTs"], 1024, 2048, hwqs, "wstage1k")
    mkA = A.mark()
    crep = A.sb("att_crep", [128, SEQ], BF16, side="right")
    srep = A.sb("att_srep", [128, SEQ], BF16, side="right")
    angi = A.sb("att_angi", [128, 512], I32, side="right")
    tmpa = A.sb("att_tmpa", [128, 512], F32, side="right")
    tmpb = A.sb("att_tmpb", [128, 512], F32, side="right")
    for i in range(NDM):
        A.sb(f"att_xc{i}", [128, 512], BF16, side="right")
    for pp in range(2):
        for i in range(NDM):
            A.sb(f"att_wc{pp}_{i}", [128, 128], BF16, side="right")
            A.sb(f"att_ws{pp}_{i}", [128, 128], BF16, side="right")
    for pp in range(2):
        A.sb(f"att_vwc{pp}", [128, 512], BF16, side="right")
    ro1 = A.sb("att_ro1", [128, 512], F32, side="right")
    ro2 = A.sb("att_ro2", [128, 512], F32, side="right")
    qkb = A.sb("att_qk", [128, 512], BF16, side="right")
    vsb = A.sb("att_vsb", [128, 512], BF16, side="right")
    for tc in range(8):
        for (dst, dk, phase) in ((srep, "att_srep", 0.0), (crep, "att_crep", math.pi / 2)):
            P.emit("gpsimd", lambda e, tc=tc: e.iota(angi[:], pattern=[[1, 512]], base=tc * 512, channel_multiplier=0),
                   writes=["att_angi"])
            P.emit("vector", lambda e: e.tensor_copy(tmpa[:], angi[:]), reads=["att_angi"], writes=["att_tmpa"])
            P.emit("vector", lambda e: e.tensor_scalar(tmpa[:], tmpa[:], fr[:], None, op0=ALU.mult),
                   reads=["att_tmpa"], sreads=["att_fr"], writes=["att_tmpa"])
            P.emit("vector", lambda e, phase=phase: e.tensor_scalar(
                tmpa[:], tmpa[:], 1.0 / (2 * math.pi), phase / (2 * math.pi), op0=ALU.mult, op1=ALU.add),
                reads=["att_tmpa"], writes=["att_tmpa"])
            P.emit("vector", lambda e: e.tensor_copy(angi[:], tmpa[:]), reads=["att_tmpa"], writes=["att_angi"])
            P.emit("vector", lambda e: e.tensor_copy(tmpb[:], angi[:]), reads=["att_angi"], writes=["att_tmpb"])
            P.emit("vector", lambda e: e.tensor_tensor(tmpa[:], tmpa[:], tmpb[:], op=ALU.subtract),
                   reads=["att_tmpa", "att_tmpb"], writes=["att_tmpa"])
            P.emit("vector", lambda e: e.tensor_scalar(tmpb[:], tmpa[:], 0.0, None, op0=ALU.is_lt),
                   reads=["att_tmpa"], writes=["att_tmpb"])
            P.emit("vector", lambda e: e.tensor_tensor(tmpa[:], tmpa[:], tmpb[:], op=ALU.add),
                   reads=["att_tmpa", "att_tmpb"], writes=["att_tmpa"])
            P.emit("vector", lambda e: e.tensor_scalar(tmpa[:], tmpa[:], 2 * math.pi, None, op0=ALU.mult),
                   reads=["att_tmpa"], writes=["att_tmpa"])
            P.emit("vector", lambda e: e.tensor_scalar(tmpb[:], tmpa[:], math.pi, None, op0=ALU.is_gt),
                   reads=["att_tmpa"], writes=["att_tmpb"])
            P.emit("vector", lambda e: e.scalar_tensor_tensor(tmpa[:], tmpb[:], -2 * math.pi, tmpa[:],
                                                              op0=ALU.mult, op1=ALU.add),
                   reads=["att_tmpb", "att_tmpa"], writes=["att_tmpa"])
            P.emit("scalar", lambda e, dst=dst, tc=tc: e.activation(out=dst[:, tc * 512:(tc + 1) * 512], in_=tmpa[:], func=AF.Sin),
                   reads=["att_tmpa"], writes=[dk])
    P.emit("vector", lambda e: e.tensor_scalar(srep[:], srep[:], sg[:], None, op0=ALU.mult),
           reads=["att_srep"], sreads=["att_sg"], writes=["att_srep"])
    xlnc = [f"att_xc{i}" for i in range(NDM)]
    for tc in range(8):
        s = slice(tc * 512, (tc + 1) * 512)
        ln_chunk(P, A, g, xk, tc, xlnc)
        for mc in range(16):
            pp = mc % 2
            for i in range(NDM):
                wc = A.t[f"att_wc{pp}_{i}"]
                ws = A.t[f"att_ws{pp}_{i}"]
                P.emit("sync", lambda e, wc=wc, mc=mc, i=i: e.dma_start(out=wc[:], in_=hwq[mc, i, :, :]),
                       reads=["awq_blk"], writes=[f"att_wc{pp}_{i}"], dma=True)
                P.emit("gpsimd", lambda e, ws=ws, mc=mc, i=i: e.dma_start(out=ws[:], in_=hwqs[mc, i, :, :]),
                       reads=["awqs_blk"], writes=[f"att_ws{pp}_{i}"], dma=True)
            for i in range(NDM):
                wc = A.t[f"att_wc{pp}_{i}"]
                ws = A.t[f"att_ws{pp}_{i}"]
                xc = A.t[xlnc[i]]
                P.emit("tensor", lambda e, wc=wc, xc=xc, st=(i == 0), sp=(i == NDM - 1):
                       e.matmul(ph0[:, 0:512], wc[:], xc[:], start=st, stop=sp),
                       reads=[f"att_wc{pp}_{i}", xlnc[i]], writes=["ph0"])
                P.emit("tensor", lambda e, ws=ws, xc=xc, st=(i == 0), sp=(i == NDM - 1):
                       e.matmul(ph1[:, 0:512], ws[:], xc[:], start=st, stop=sp),
                       reads=[f"att_ws{pp}_{i}", xlnc[i]], writes=["ph1"])
            scl = 0.125 if mc < 8 else 1.0
            P.emit("vector", lambda e, s=s, scl=scl: e.scalar_tensor_tensor(
                ro1[:], ph0[:, 0:512], scl, crep[:, s], op0=ALU.mult, op1=ALU.mult),
                reads=["ph0", "att_crep"], writes=["att_ro1"])
            P.emit("vector", lambda e, s=s, scl=scl: e.scalar_tensor_tensor(
                ro2[:], ph1[:, 0:512], scl, srep[:, s], op0=ALU.mult, op1=ALU.mult),
                reads=["ph1", "att_srep"], writes=["att_ro2"])
            P.emit("vector", lambda e: e.tensor_tensor(qkb[:], ro1[:], ro2[:], op=ALU.add),
                   reads=["att_ro1", "att_ro2"], writes=["att_qk"])
            dst = hq if mc < 8 else hk
            row = (mc % 8) * 128
            P.emit("gpsimd", lambda e, dst=dst, row=row, s=s: e.dma_start(
                out=dst[row:row + 128, s], in_=qkb[:]),
                reads=["att_qk"], writes=["hbm_qk"], dma=True)
        for vc in range(2):
            for i in range(NDM):
                pp = i % 2
                vw = A.t[f"att_vwc{pp}"]
                P.emit("sync", lambda e, vw=vw, vc=vc, i=i: e.dma_start(
                    out=vw[:].rearrange("p (j c) -> p j c", j=4),
                    in_=hwq[16 + vc * 4:16 + vc * 4 + 4, i, :, :].rearrange("j p c -> p j c")),
                    reads=["awq_blk"], writes=[f"att_vwc{pp}"], dma=True)
                for sub in range(4):
                    xc = A.t[xlnc[i]]
                    P.emit("tensor", lambda e, vw=vw, xc=xc, sub=sub, st=(i == 0), sp=(i == NDM - 1):
                           e.matmul(ph2[:, 0:512] if sub == 0 else (ph4[:, 0:512] if sub == 1 else (ph5[:, 0:512] if sub == 2 else ph3[:, 0:512])),
                                    xc[:, sub * 128:(sub + 1) * 128], vw[:], start=st, stop=sp),
                           reads=[f"att_vwc{pp}", xlnc[i]],
                           writes=[["ph2", "ph4", "ph5", "ph3"][sub]])
            for sub in range(4):
                ktc = tc * 4 + sub
                pht_v = [ph2, ph4, ph5, ph3][sub]
                pk = ["ph2", "ph4", "ph5", "ph3"][sub]
                P.emit("scalar", lambda e, pht_v=pht_v: e.activation(out=vsb[:], in_=pht_v[:, 0:512], func=AF.Copy),
                       reads=[pk], writes=["att_vsb"])
                for j in range(8):
                    h = vc * 8 + j
                    P.emit("sync" if j % 2 == 0 else "gpsimd",
                           lambda e, ktc=ktc, h=h, j=j: e.dma_start(
                               out=hv[ktc, h, :, :], in_=vsb[:, j * 64:(j + 1) * 64]),
                           reads=["att_vsb"], writes=["hbm_v"], dma=True)
    P.fence()
    A.free_to(mkA)
    mkB = A.mark()
    kt = A.sb("att_kt", [64, SEQ], BF16, side="right")
    qt = A.sb("att_qt", [64, 512], BF16, side="right")
    dh = A.sb("att_dh", [128, 2016], I32, side="right")
    et = A.sb("att_et", [128, 2016], F32, side="right")
    ekb = A.sb("att_ekb", [128, 8064], BF16, side="right")
    at0 = A.sb("att_at0", [128, 512], BF16, side="right")
    at1 = A.sb("att_at1", [128, 512], BF16, side="right")
    at2 = A.sb("att_at2", [128, 512], BF16, side="right")
    at3 = A.sb("att_at3", [128, 512], BF16, side="right")
    v0 = A.sb("att_v0", [128, 65], BF16, side="right")
    v1 = A.sb("att_v1", [128, 65], BF16, side="right")
    rec = A.sb("att_rec", [128, 512], F32, side="right")
    avs = A.sb("att_avs", [64, 512], F32, side="right")
    avn0 = A.sb("att_avn0", [64, 512], BF16, side="right")
    avn1 = A.sb("att_avn1", [64, 512], BF16, side="right")
    P.emit("vector", lambda e: e.memset(v0[:, 64:65], 1.0), writes=["att_v0"])
    P.emit("vector", lambda e: e.memset(v1[:, 64:65], 1.0), writes=["att_v1"])
    ats = [at0, at1]
    ats2 = [at2, at3]
    vts = [v0, v1]
    avns = [avn0, avn1]
    for h in range(NH):
        P.emit("sync", lambda e, h=h: e.dma_start(out=kt[:], in_=hk[h * 64:(h + 1) * 64, :]),
               reads=["hbm_qk"], writes=["att_kt"], dma=True)
        P.emit("tensor", lambda e, h=h: e.matmul(ph3[:, 0:1], orf[0:1, :], pa[0:1, 16 + h:17 + h], start=True, stop=True),
               reads=["att_pa", "ones_row_f"], writes=["ph3"])
        P.emit("tensor", lambda e, h=h: e.matmul(ph3[:, 1:2], orf[0:1, :], pa[0:1, h:h + 1], start=True, stop=True),
               reads=["att_pa", "ones_row_f"], writes=["ph3"])
        for qq in range(4):
            base = -31 * 128 + qq * 2016
            P.emit("gpsimd", lambda e, base=base: e.iota(dh[:], pattern=[[1, 2016]], base=base, channel_multiplier=-1),
                   writes=["att_dh"])
            P.emit("scalar", lambda e: e.activation(out=et[:], in_=dh[:], func=AF.Abs),
                   reads=["att_dh"], writes=["att_et"])
            P.emit("vector", lambda e: e.tensor_scalar(et[:], et[:], ph3[:, 0:1], None, op0=ALU.mult),
                   reads=["att_et"], sreads=["ph3"], writes=["att_et"])
            P.emit("scalar", lambda e: e.activation(out=et[:], in_=et[:], func=AF.Ln, bias=1.0),
                   reads=["att_et"], writes=["att_et"])
            P.emit("vector", lambda e: e.tensor_scalar(et[:], et[:], ph3[:, 1:2], None, op0=ALU.mult),
                   reads=["att_et"], sreads=["ph3"], writes=["att_et"])
            P.emit("scalar", lambda e, qq=qq: e.activation(out=ekb[:, qq * 2016:(qq + 1) * 2016], in_=et[:], func=AF.Exp),
                   reads=["att_et"], writes=["att_ekb"])
        for qc in range(8):
            P.emit("sync", lambda e, h=h, qc=qc: e.dma_start(out=qt[:], in_=hq[h * 64:(h + 1) * 64, qc * 512:(qc + 1) * 512]),
                   reads=["hbm_qk"], writes=["att_qt"], dma=True)
            for ktc in range(32):
                pb = "ph0" if ktc % 2 == 0 else "ph1"
                pbt = ph0 if ktc % 2 == 0 else ph1
                ab = ktc % 2
                P.emit("tensor", lambda e, ktc=ktc, pbt=pbt: e.matmul(
                    pbt[:, 0:512], kt[:, ktc * 128:(ktc + 1) * 128], qt[:], start=True, stop=True),
                    reads=["att_kt", "att_qt"], writes=[pb])
                P.emit("scalar", lambda e, pbt=pbt, ab=ab: e.activation(out=ats[ab][:], in_=pbt[:, 0:512], func=AF.Exp),
                       reads=[pb], writes=[f"att_at{ab}"])
                off = (4 * qc - ktc + 31) * 128
                P.emit("vector", lambda e, ab=ab, off=off: e.tensor_tensor(
                    ats2[ab][:], ats[ab][:], ekb[:, off:off + 512], op=ALU.mult),
                    reads=[f"att_at{ab}", "att_ekb"], writes=[f"att_at2{ab}" if False else f"att_at{ab + 2}"])
                P.emit("sync" if ab == 0 else "gpsimd",
                       lambda e, ktc=ktc, h=h, ab=ab: e.dma_start(out=vts[ab][:, 0:64], in_=hv[ktc, h, :, :]),
                       reads=["hbm_v"], writes=[f"att_v{ab}"], dma=True)
                P.emit("tensor", lambda e, ab=ab, st=(ktc == 0), sp=(ktc == 31): e.matmul(
                    ph2[0:65, 0:512], vts[ab][:], ats2[ab][:], start=st, stop=sp),
                    reads=[f"att_v{ab}", f"att_at{ab + 2}"], writes=["ph2"])
            P.emit("vector", lambda e: e.reciprocal(rec[64:65, :], ph2[64:65, 0:512]),
                   reads=["ph2"], writes=["att_rec"])
            P.emit("tensor", lambda e: e.matmul(ph3[0:64, 0:512], g["ones_all_f"][64:65, 0:64], rec[64:65, :], start=True, stop=True),
                   reads=["att_rec", "ones_all_f"], writes=["ph3"])
            P.emit("vector", lambda e: e.tensor_copy(avs[:], ph2[0:64, 0:512]),
                   reads=["ph2"], writes=["att_avs"])
            ab2 = qc % 2
            avn = avns[ab2]
            P.emit("vector", lambda e, avn=avn: e.tensor_tensor(
                avn[:, :], avs[:], ph3[0:64, 0:512], op=ALU.mult),
                reads=["att_avs", "ph3"], writes=[f"att_avn{ab2}"])
            P.emit("gpsimd", lambda e, avn=avn, h=h, qc=qc: e.dma_start(
                out=havn[h * 64:(h + 1) * 64, qc * 512:(qc + 1) * 512], in_=avn[:, :]),
                reads=[f"att_avn{ab2}"], writes=["havn"], dma=True)
    P.fence()
    A.free_to(mkB)
    mkC = A.mark()
    for i in range(NDM):
        load_weight_bf16(P, A, f"att_wo{i}", W["att_outT"][i * 128:(i + 1) * 128, :], [128, 1024],
                         stage="wstage1k", side="right")
    for i in range(8):
        A.sb(f"att_po{i}", [128, 512], BF16, side="right")
    pos = [A.t[f"att_po{i}"] for i in range(8)]
    for qc in range(8):
        for hc in range(NDM):
            P.emit("sync", lambda e, hc=hc, qc=qc: e.dma_start(
                out=pos[hc][:, :], in_=havn[hc * 128:(hc + 1) * 128, qc * 512:(qc + 1) * 512]),
                reads=["havn"], writes=[f"att_po{hc}"], dma=True)
            if DEBUG_HBM:
                P.emit("gpsimd", lambda e, hc=hc, qc=qc: e.dma_start(
                    out=hpodbg[qc, hc, :, :], in_=pos[hc][:, :]),
                    reads=[f"att_po{hc}"], writes=["podbg"], dma=True)
        for dmc in range(NDM):
            pb = "ph4" if dmc % 2 == 0 else "ph5"
            pht_ = ph4 if dmc % 2 == 0 else ph5
            for hc in range(NDM):
                wo = A.t[f"att_wo{hc}"]
                P.emit("tensor", lambda e, wo=wo, hc=hc, dmc=dmc, pht_=pht_, st=(hc == 0), sp=(hc == 7):
                       e.matmul(pht_[:, 0:512], wo[:, dmc * 128:(dmc + 1) * 128], pos[hc][:, :], start=st, stop=sp),
                       reads=[f"att_wo{hc}", f"att_po{hc}"], writes=[pb])
            if DEBUG_HBM:
                dbge = A.t["att_rec2"] if "att_rec2" in A.t else A.sb("att_rec2", [128, 512], F32, side="right")
                P.emit("scalar", lambda e, pht_=pht_: e.activation(out=dbge[:], in_=pht_[:, 0:512], func=AF.Copy),
                       reads=[pb], writes=["att_rec2"])
                P.emit("gpsimd", lambda e, qc=qc, dmc=dmc, dbge=dbge: e.dma_start(
                    out=hpsdbg[qc, dmc, :, :], in_=dbge[:]),
                    reads=["att_rec2"], writes=["psdbg"], dma=True)
            xo = A.t[xk[dmc]]
            s = slice(qc * 512, (qc + 1) * 512)
            if DEBUG_HBM:
                P.emit("vector", lambda e, xo=xo, s=s, dbge=dbge: e.tensor_tensor(xo[:, s], xo[:, s], dbge[:], op=ALU.add),
                       reads=["att_rec2", xk[dmc]], writes=[xk[dmc]])
            else:
                P.emit("vector", lambda e, xo=xo, s=s, pht_=pht_: e.tensor_tensor(xo[:, s], xo[:, s], pht_[:, 0:512], op=ALU.add),
                       reads=[pb, xk[dmc]], writes=[xk[dmc]])
    P.fence()
    A.free_to(mkC)


def prep_l0attn_weights(inputs):
    W = {}
    qkv = np.asarray(inputs["L0_attn_qkv_w"])
    W["att_qkvT"] = np.ascontiguousarray(qkv.T)
    perm = np.arange(2048)
    for h in range(32):
        base = h * 64
        perm[base:base + 64] = np.concatenate([np.arange(base + 32, base + 64), np.arange(base, base + 32)])
    W["att_qkvTs"] = np.ascontiguousarray(qkv.T[:, :2048][:, perm])
    W["att_outT"] = np.ascontiguousarray(np.asarray(inputs["L0_attn_out_w"]).T)
    W["att_freqs"] = np.ascontiguousarray(np.tile(np.asarray(inputs["L0_attn_rope_freqs"]), 4).reshape(128, 1))
    sgn = np.tile(np.concatenate([-np.ones(32, np.float32), np.ones(32, np.float32)]), 2).reshape(128, 1)
    W["att_sgn"] = np.ascontiguousarray(sgn)
    W["att_pa"] = np.ascontiguousarray(np.concatenate(
        [np.asarray(inputs["L0_attn_bias_p"]).reshape(16), np.asarray(inputs["L0_attn_bias_a"]).reshape(16)]).reshape(1, 32))
    return W


ATTN_W_SHAPES = dict(
    att_qkvT=[1024, 3072], att_qkvTs=[1024, 2048], att_outT=[1024, 1024],
    att_freqs=[128, 1], att_sgn=[128, 1], att_pa=[1, 32])


def build_sl2_standalone():
    nc = bass.Bass("TRN2", target_bir_lowering=False)
    xin = nc.dram_tensor("xT", [D_MODEL, SEQ], F32, kind="ExternalInput").ap()
    xout = nc.dram_tensor("yT", [D_MODEL, SEQ], F32, kind="ExternalOutput").ap()
    Wd = {k: nc.dram_tensor(k, sh, F32, kind="ExternalInput").ap()
          for k, sh in ATTN_W_SHAPES.items()}
    with ExitStack() as stack:
        A = Alloc(nc, stack)
        P = Prog(nc)
        g = build_common(P, A)
        xkeys = [f"x{i}" for i in range(NDM)]
        for i in range(NDM):
            x = A.sb(xkeys[i], [128, SEQ], F32)
            P.emit("sync", lambda e, x=x, i=i: e.dma_start(out=x[:], in_=xin[i * 128:(i + 1) * 128, :]),
                   writes=[xkeys[i]], dma=True)
        for pn in ["ph0", "ph1", "ph2", "ph3", "ph4", "ph5"]:
            A.ps(pn, [128, 512], F32)
        sl_l0attn(P, A, g, Wd, xkeys)
        for i in range(NDM):
            x = A.t[xkeys[i]]
            P.emit("gpsimd", lambda e, x=x, i=i: e.dma_start(out=xout[i * 128:(i + 1) * 128, :], in_=x[:]),
                   reads=[xkeys[i]], writes=[f"outdma{i}"], dma=True)
        P.barrier_read("gpsimd", [f"outdma{i}" for i in range(NDM)])
        finalize(P, nc, A)
        print(f"SBUF bytes: {A.bytes/1e6:.2f}MB, ops: {len(P.ops)}", flush=True)
    return nc




def sl_l0ffn(P, A, g, W, xk, first_call=True):
    nc = P.nc
    E = 8
    CAP = 512
    NSC = CAP // 128
    ph = [A.t[f"ph{i}"] for i in range(6)]
    phbf = A.t["phbf"]
    ocf, orf = g["ones_col_f"], g["ones_row_f"]
    sfx = "a" if first_call else "b"
    if first_call:
        A.t["_moe_dram"] = dict(
            w1b=nc.dram_tensor("hbm_w1b", [8, 32, 8, 128, 128], BF16).ap(),
            w2b=nc.dram_tensor("hbm_w2b", [8, 8, 32, 128, 128], BF16).ap(),
            pos_d=nc.dram_tensor("moe_pos_d", [8, SEQ], F32).ap(),
            lnst=nc.dram_tensor("hbm_lnst_a", [SEQ, 1024], BF16).ap(),
            lnst_b=nc.dram_tensor("hbm_lnst_b", [SEQ, 1024], BF16).ap(),
            cin=nc.dram_tensor("moe_cin_a", [8, SEQ], F32).ap(),
            cout=nc.dram_tensor("moe_cout_a", [64, SEQ], F32).ap(),
            cin_b=nc.dram_tensor("moe_cin_b", [8, SEQ], F32).ap(),
            cout_b=nc.dram_tensor("moe_cout_b", [64, SEQ], F32).ap(),
        )
        md0 = A.t["_moe_dram"]
        md0["b2bf"] = nc.dram_tensor("hbm_b2bf", [8, 1024], BF16).ap()
        P.emit("sync", lambda e: e.dma_start(out=A.t["wstage1k"][0:8, 0:1024], in_=W["moe_b2"][:, :]),
               writes=["wstage1k"], dma=True)
        P.emit("scalar", lambda e: e.activation(out=A.t["wstage1k_bf"][0:8, 0:1024], in_=A.t["wstage1k"][0:8, 0:1024], func=AF.Copy),
               reads=["wstage1k"], writes=["wstage1k_bf"])
        P.emit("gpsimd", lambda e: e.dma_start(out=md0["b2bf"][:, :], in_=A.t["wstage1k_bf"][0:8, 0:1024]),
               reads=["wstage1k_bf"], writes=["hbm_b2bf"], dma=True)
        for e_ in range(E):
            cast_to_hbm_blocked(P, A, f"mw1_{e_}", W["moe_w1"][e_, :, :], 1024, 4096,
                                A.t["_moe_dram"]["w1b"][e_], "wstage1k")
            cast_to_hbm_blocked(P, A, f"mw2_{e_}", W["moe_w2"][e_, :, :], 4096, 1024,
                                A.t["_moe_dram"]["w2b"][e_], "wstage1k")
        rw = A.sb("moe_rw_s", [128, 64], F32)
        P.emit("sync", lambda e: e.dma_start(out=rw[:], in_=W["moe_rw"][:, :]), writes=["moe_rw_s"], dma=True)
        rb = A.sb("moe_rb_s", [1, 8], F32)
        P.emit("sync", lambda e: e.dma_start(out=rb[:], in_=W["moe_rb"][:, :]), writes=["moe_rb_s"], dma=True)
        cwrow = A.sb("moe_cwr", [1, 8], F32)
        rbcol = A.sb("moe_rbc", [8, 1], F32)
        ph2_ = A.t["ph2"]
        for i in range(NDM):
            P.emit("tensor", lambda e, i=i, st=(i == 0), sp=(i == NDM - 1): e.matmul(
                ph2_[0:8, 40:41], rw[:, i * 8:(i + 1) * 8], A.t["ones_col_f"][:], start=st, stop=sp),
                reads=["moe_rw_s", "ones_col_f"], writes=["ph2"])
        cwc = A.sb("moe_cwc", [8, 1], F32)
        P.emit("vector", lambda e: e.tensor_scalar(cwc[:], ph2_[0:8, 40:41], -1.0, None, op0=ALU.mult),
               reads=["ph2"], writes=["moe_cwc"])
        P.emit("tensor", lambda e: e.transpose(ph2_[0:1, 44:52], cwc[:], A.t["ident_f"][0:8, 0:8]),
               reads=["moe_cwc", "ident_f"], writes=["ph2"])
        P.emit("vector", lambda e: e.tensor_copy(cwrow[:], ph2_[0:1, 44:52]), reads=["ph2"], writes=["moe_cwr"])
        P.emit("tensor", lambda e: e.transpose(ph2_[0:8, 52:53], rb[:], A.t["ident_f"][0:1, 0:1]),
               reads=["moe_rb_s", "ident_f"], writes=["ph2"])
        P.emit("vector", lambda e: e.tensor_copy(rbcol[:], ph2_[0:8, 52:53]), reads=["ph2"], writes=["moe_rbc"])
        lmat = A.sb("moe_L_s", [128, 128], F32)
        P.emit("sync", lambda e: e.dma_start(out=lmat[:], in_=W["moe_L"][:, :]), writes=["moe_L_s"], dma=True)
        smat = A.sb("moe_S_s", [128, 8], F32)
        P.emit("sync", lambda e: e.dma_start(out=smat[:], in_=W["moe_S"][:, :]), writes=["moe_S_s"], dma=True)
        semat = A.sb("moe_SE_s", [8, 128], F32)
        P.emit("sync", lambda e: e.dma_start(out=semat[:], in_=W["moe_SE"][:, :]), writes=["moe_SE_s"], dma=True)
        ohot = A.sb("moe_oh", [128, 8], F32)
        P.emit("sync", lambda e: e.dma_start(out=ohot[:], in_=W["moe_ownhot"][:, :]), writes=["moe_oh"], dma=True)

        iotok = A.sb("moe_iotok", [128, 32], F32)
        P.emit("gpsimd", lambda e: e.iota(iotok[:], pattern=[[1, 32]], base=0, channel_multiplier=32,
                                          allow_small_or_imprecise_dtypes=True), writes=["moe_iotok"])
        iorow = A.sb("moe_iorow", [128, 512], F32)
        P.emit("gpsimd", lambda e: e.iota(iorow[:], pattern=[[1, 512]], base=0, channel_multiplier=0,
                                          allow_small_or_imprecise_dtypes=True), writes=["moe_iorow"])

    md = A.t["_moe_dram"]
    lnst = md["lnst"] if first_call else md["lnst_b"]
    cin = md["cin"] if first_call else md["cin_b"]
    cout = md["cout"] if first_call else md["cout_b"]
    rw, rb, lmat, smat, semat, ohot = [A.t[k] for k in
                                       ["moe_rw_s", "moe_rb_s", "moe_L_s", "moe_S_s", "moe_SE_s", "moe_oh"]]
    iotok, iorow = A.t["moe_iotok"], A.t["moe_iorow"]

    mkTOP = A.mark()
    idxu = A.sb("moe_idxu", [128, 4 * E], U32, side="right")
    gval = A.sb("moe_gval", [128, 4 * E], F32, side="right")
    idxf = A.sb("moe_idxf", [128, 4 * E], F32, side="right")
    mkS = A.mark()
    sct = A.sb("moe_sct", [8, SEQ], F32, side="right")
    mkA = A.mark()
    for i in range(NDM):
        A.sb(f"moe_xc{i}", [128, 512], BF16, side="right")
    xlnc = [f"moe_xc{i}" for i in range(NDM)]
    lgsb = A.sb("moe_lg", [8, 512], F32, side="right")
    tokp = A.sb("moe_tokp", [128, 8], F32, side="right")
    tnm = A.sb("moe_tnm", [128, 1], F32, side="right")
    tsum = A.sb("moe_tsum", [128, 1], F32, side="right")
    trec = A.sb("moe_trec", [128, 1], F32, side="right")
    tsc = A.sb("moe_tsc", [128, 8], F32, side="right")
    tmaj = A.sb("moe_tmaj", [128, 1024], BF16, side="right")
    cwrow, rbcol = A.t["moe_cwr"], A.t["moe_rbc"]
    for tc in range(8):
        s_ = slice(tc * 512, (tc + 1) * 512)
        ln_chunk(P, A, g, xk, tc, xlnc)
        for i in range(NDM):
            xi_ = A.t[xk[i]]
            P.emit("tensor", lambda e, xi_=xi_, i=i, s_=s_, st=(i == 0):
                   e.matmul(ph[2][0:8, 0:512], rw[:, i * 8:(i + 1) * 8], xi_[:, s_], start=st, stop=False),
                   reads=["moe_rw_s", xk[i]], writes=["ph2"])
        P.emit("tensor", lambda e: e.matmul(ph[2][0:8, 0:512], cwrow[0:1, :], A.t["lnrow_m"][0:1, 0:512], start=False, stop=True),
               reads=["moe_cwr", "lnrow_m"], writes=["ph2"])
        P.emit("tensor", lambda e: e.matmul(ph[5][0:8, 0:512], g["ones_row_f"][0:1, 0:8], A.t["lnrow_r"][0:1, 0:512], start=True, stop=True),
               reads=["ones_row_f", "lnrow_r"], writes=["ph5"])
        P.emit("vector", lambda e: e.tensor_copy(lgsb[:], ph[2][0:8, 0:512]), reads=["ph2"], writes=["moe_lg"])
        P.emit("vector", lambda e: e.tensor_tensor(lgsb[:], lgsb[:], ph[5][0:8, 0:512], op=ALU.mult),
               reads=["moe_lg", "ph5"], writes=["moe_lg"])
        P.emit("vector", lambda e: e.tensor_scalar(lgsb[:], lgsb[:], rbcol[:], None, op0=ALU.add),
               reads=["moe_lg"], sreads=["moe_rbc"], writes=["moe_lg"])
        for sb_ in range(4):
            cs = slice(sb_ * 128, (sb_ + 1) * 128)
            P.emit("tensor", lambda e, cs=cs: e.transpose(ph[2][:, 16:24], lgsb[:, cs], g["ident_f"][0:8, 0:8]),
                   reads=["moe_lg", "ident_f"], writes=["ph2"])
            P.emit("vector", lambda e: e.tensor_copy(tokp[:], ph[2][:, 16:24]),
                   reads=["ph2"], writes=["moe_tokp"])
            P.emit("vector", lambda e: e.tensor_reduce(tnm[:], tokp[:], axis=AX.X, op=ALU.max, negate=True),
                   reads=["moe_tokp"], writes=["moe_tnm"])
            P.emit("scalar", lambda e: e.activation(out=tokp[:], in_=tokp[:], func=AF.Exp, bias=tnm[:]),
                   reads=["moe_tokp"], sreads=["moe_tnm"], writes=["moe_tokp"])
            P.emit("vector", lambda e: e.tensor_reduce(tsum[:], tokp[:], axis=AX.X, op=ALU.add),
                   reads=["moe_tokp"], writes=["moe_tsum"])
            P.emit("vector", lambda e: e.reciprocal(trec[:], tsum[:]), reads=["moe_tsum"], writes=["moe_trec"])
            P.emit("vector", lambda e: e.tensor_scalar(tsc[:], tokp[:], trec[:], None, op0=ALU.mult),
                   reads=["moe_tokp"], sreads=["moe_trec"], writes=["moe_tsc"])
            P.emit("tensor", lambda e: e.transpose(ph[5][0:8, 0:128], tsc[:], g["ident_f"][:]),
                   reads=["moe_tsc", "ident_f"], writes=["ph5"])
            P.emit("vector", lambda e, tc=tc, cs=cs: e.tensor_copy(sct[0:8, tc * 512 + cs.start:tc * 512 + cs.stop], ph[5][0:8, 0:128]),
                   reads=["ph5"], writes=["moe_sct"])
        for sub in range(4):
            for i in range(NDM):
                xc = A.t[xlnc[i]]
                P.emit("tensor", lambda e, xc=xc, sub=sub: e.transpose(phbf[:, 0:128], xc[:, sub * 128:(sub + 1) * 128], g["ident_bf"][:]),
                       reads=[xlnc[i], "ident_bf"], writes=["phbf"])
                P.emit("vector", lambda e, i=i: e.tensor_copy(tmaj[:, i * 128:(i + 1) * 128], phbf[:, 0:128]),
                       reads=["phbf"], writes=["moe_tmaj"])
            P.emit("gpsimd", lambda e, tc=tc, sub=sub: e.dma_start(
                out=lnst[(tc * 4 + sub) * 128:(tc * 4 + sub + 1) * 128, :], in_=tmaj[:]),
                reads=["moe_tmaj"], writes=["moe_lnst"], dma=True)
    P.emit("gpsimd", lambda e: e.dma_start(out=cin[:, :], in_=sct[:]),
           reads=["moe_sct"], writes=["moe_cin"], dma=True)
    P.emit("gpsimd", lambda e: e.collective_compute(
        "AllGather", ALU.bypass, replica_groups=[list(range(8))],
        ins=[cin.opt()], outs=[cout.opt()]),
        reads=["moe_cin"], writes=["moe_cout"], tag="collective")
    P.fence()
    A.free_to(mkA)
    mkR = A.mark()
    lo = A.sb("moe_lo", [8, 1], F32, side="right")
    hi = A.sb("moe_hi", [8, 1], F32, side="right")
    mid = A.sb("moe_mid", [8, 1], F32, side="right")
    cge = A.sb("moe_cge", [8, 1], U32, side="right")
    cnt = A.sb("moe_cnt", [128, 1], F32, side="right")
    th128 = A.sb("moe_th128", [128, 1], F32, side="right")
    tneed = A.sb("moe_tneed", [8, 1], F32, side="right")
    rtot = A.sb("moe_rtot", [128, 1], F32, side="right")
    ooff = A.sb("moe_ooff", [8, 1], F32, side="right")
    th8 = A.sb("moe_th8", [8, 1], F32, side="right")
    tn8 = A.sb("moe_tn8", [8, 1], F32, side="right")
    zc = A.sb("moe_zc", [128, 1], F32, side="right")
    mkR1 = A.mark()
    scg = A.sb("moe_scg", [128, 2048], F32, side="right")
    msk = A.sb("moe_msk", [128, 2048], F32, side="right")
    P.emit("vector", lambda e: e.memset(zc[:], 0.0), writes=["moe_zc"])
    for p in range(128):
        e_, gg = p // 16, p % 16
        r_, c0 = 8 * (gg // 2) + e_, (gg % 2) * 2048
        P.emit("sync" if p % 2 == 0 else "gpsimd",
               lambda e, p=p, r_=r_, c0=c0: e.dma_start(out=scg[p:p + 1, :], in_=cout[r_:r_ + 1, c0:c0 + 2048]),
               reads=["moe_cout"], writes=["moe_scg"], dma=True)
    if DEBUG_ON:
        tapd(P, A, "scg2", "moe_scg", lambda: scg[:])
    P.emit("vector", lambda e: e.memset(lo[:], 0.0), writes=["moe_lo"])
    P.emit("vector", lambda e: e.memset(hi[:], 1.0), writes=["moe_hi"])
    P.emit("vector", lambda e: e.memset(mid[:], 0.5), writes=["moe_mid"])
    for it in range(56):
        P.emit("tensor", lambda e: e.matmul(ph[3][:, 0:1], semat[:], mid[:], start=True, stop=True),
               reads=["moe_SE_s", "moe_mid"], writes=["ph3"])
        P.emit("vector", lambda e: e.tensor_scalar(msk[:], scg[:], ph[3][:, 0:1], None, op0=ALU.is_gt,
                                                   op1=ALU.add, accum_out=cnt[:]),
               reads=["moe_scg"], sreads=["ph3"], writes=["moe_msk", "moe_cnt"])
        P.emit("tensor", lambda e: e.matmul(ph[3][0:8, 1:2], smat[:], cnt[:], start=True, stop=True),
               reads=["moe_S_s", "moe_cnt"], writes=["ph3"])
        P.emit("vector", lambda e: e.tensor_scalar(cge[:], ph[3][0:8, 1:2], 2048.0, None, op0=ALU.is_ge),
               reads=["ph3"], writes=["moe_cge"])
        P.emit("vector", lambda e: e.select(lo[:], cge[:], mid[:], lo[:]),
               reads=["moe_cge", "moe_mid", "moe_lo"], writes=["moe_lo"])
        P.emit("vector", lambda e: e.select(hi[:], cge[:], hi[:], mid[:]),
               reads=["moe_cge", "moe_hi", "moe_mid"], writes=["moe_hi"])
        P.emit("vector", lambda e: e.tensor_tensor(mid[:], lo[:], hi[:], op=ALU.add),
               reads=["moe_lo", "moe_hi"], writes=["moe_mid"])
        P.emit("vector", lambda e: e.tensor_scalar(mid[:], mid[:], 0.5, None, op0=ALU.mult),
               reads=["moe_mid"], writes=["moe_mid"])
    P.emit("tensor", lambda e: e.matmul(ph[3][:, 0:1], semat[:], hi[:], start=True, stop=True),
           reads=["moe_SE_s", "moe_hi"], writes=["ph3"])
    P.emit("vector", lambda e: e.tensor_copy(th128[:], ph[3][:, 0:1]), reads=["ph3"], writes=["moe_th128"])
    P.emit("vector", lambda e: e.tensor_scalar(msk[:], scg[:], th128[:], None, op0=ALU.is_gt,
                                               op1=ALU.add, accum_out=cnt[:]),
           reads=["moe_scg"], sreads=["moe_th128"], writes=["moe_msk", "moe_cnt"])
    P.emit("tensor", lambda e: e.matmul(ph[3][0:8, 1:2], smat[:], cnt[:], start=True, stop=True),
           reads=["moe_S_s", "moe_cnt"], writes=["ph3"])
    P.emit("vector", lambda e: e.tensor_scalar(tneed[:], ph[3][0:8, 1:2], -1.0, 2048.0, op0=ALU.mult, op1=ALU.add),
           reads=["ph3"], writes=["moe_tneed"])
    P.emit("vector", lambda e: e.tensor_scalar(msk[:], scg[:], th128[:], None, op0=ALU.is_equal),
           reads=["moe_scg"], sreads=["moe_th128"], writes=["moe_msk"])
    P.emit("vector", lambda e: e.tensor_tensor_scan(scg[:], zc[:].broadcast_to([128, 2048]), msk[:], 0.0,
                                                    op0=ALU.add, op1=ALU.add),
           reads=["moe_zc", "moe_msk"], writes=["moe_scg"])
    P.emit("vector", lambda e: e.tensor_copy(rtot[:], scg[:, 2047:2048]), reads=["moe_scg"], writes=["moe_rtot"])
    P.emit("tensor", lambda e: e.matmul(ph[3][:, 2:3], lmat[:], rtot[:], start=True, stop=True),
           reads=["moe_L_s", "moe_rtot"], writes=["ph3"])
    P.emit("vector", lambda e: e.tensor_copy(rtot[:], ph[3][:, 2:3]), reads=["ph3"], writes=["moe_rtot"])
    P.emit("tensor", lambda e: e.matmul(ph[3][0:8, 3:4], ohot[:], rtot[:], start=True, stop=True),
           reads=["moe_oh", "moe_rtot"], writes=["ph3"])
    P.emit("vector", lambda e: e.tensor_copy(ooff[:], ph[3][0:8, 3:4]), reads=["ph3"], writes=["moe_ooff"])
    P.fence()
    A.free_to(mkR1)
    selA = A.sb("moe_selA", [8, SEQ], F32, side="right")
    selB = A.sb("moe_selB", [8, SEQ], F32, side="right")
    P.emit("vector", lambda e: e.tensor_copy(th8[:], hi[:]), reads=["moe_hi"], writes=["moe_th8"])
    P.emit("vector", lambda e: e.scalar_tensor_tensor(tn8[:], tneed[:], 1.0, ooff[:], op0=ALU.mult, op1=ALU.subtract),
           reads=["moe_tneed", "moe_ooff"], writes=["moe_tn8"])
    P.emit("vector", lambda e: e.tensor_scalar(selB[:], sct[:], th8[:], None, op0=ALU.is_equal),
           reads=["moe_sct"], sreads=["moe_th8"], writes=["moe_selB"])
    P.emit("vector", lambda e: e.tensor_tensor_scan(selA[:], zc[0:8, :].broadcast_to([8, SEQ]), selB[:], 0.0,
                                                    op0=ALU.add, op1=ALU.add),
           reads=["moe_zc", "moe_selB"], writes=["moe_selA"])
    P.emit("vector", lambda e: e.tensor_scalar(selA[:], selA[:], tn8[:], None, op0=ALU.is_le),
           reads=["moe_selA"], sreads=["moe_tn8"], writes=["moe_selA"])
    P.emit("vector", lambda e: e.tensor_tensor(selB[:], selB[:], selA[:], op=ALU.mult),
           reads=["moe_selB", "moe_selA"], writes=["moe_selB"])
    P.emit("vector", lambda e: e.tensor_scalar(selA[:], sct[:], th8[:], None, op0=ALU.is_gt),
           reads=["moe_sct"], sreads=["moe_th8"], writes=["moe_selA"])
    P.emit("vector", lambda e: e.tensor_tensor(selA[:], selA[:], selB[:], op=ALU.add),
           reads=["moe_selA", "moe_selB"], writes=["moe_selA"])
    P.emit("vector", lambda e: e.tensor_tensor_scan(selB[:], zc[0:8, :].broadcast_to([8, SEQ]), selA[:], 0.0,
                                                    op0=ALU.add, op1=ALU.add),
           reads=["moe_zc", "moe_selA"], writes=["moe_selB"])
    P.emit("vector", lambda e: e.tensor_tensor(selB[:], selB[:], selA[:], op=ALU.mult),
           reads=["moe_selB", "moe_selA"], writes=["moe_selB"])
    P.emit("vector", lambda e: e.tensor_scalar(selB[:], selB[:], -1.0, None, op0=ALU.add),
           reads=["moe_selB"], writes=["moe_selB"])
    P.emit("gpsimd", lambda e: e.dma_start(out=md["pos_d"][:, :], in_=selB[:]),
           reads=["moe_selB"], writes=["moe_pos_d"], dma=True)
    if DEBUG_ON:
        tapd(P, A, "selA", "moe_selA", lambda: selA[:])
        tapd(P, A, "sct", "moe_sct", lambda: sct[:])
    if DEBUG_ON:
        dbg = A.sb("moe_dbg", [8, 8], F32)
        for ci, (tt, kk) in enumerate([(hi, "moe_hi"), (tneed, "moe_tneed"), (ooff, "moe_ooff"),
                                       (lo, "moe_lo"), (th8, "moe_th8"), (tn8, "moe_tn8")]):
            P.emit("vector", lambda e, tt=tt, ci=ci: e.tensor_copy(dbg[0:8, ci:ci + 1], tt[:]),
                   reads=[kk], writes=["moe_dbg"])
        tap(P, A, "dbg8", "moe_dbg", lambda: A.t["moe_dbg"][:])
        selcnt = A.sb("moe_selcnt", [8, 1], F32)
        P.emit("vector", lambda e: e.tensor_scalar(selA[:], selA[:], 1.0, None, op0=ALU.mult,
                                                   op1=ALU.add, accum_out=selcnt[:]),
               reads=["moe_selA"], writes=["moe_selA", "moe_selcnt"])
        P.emit("vector", lambda e: e.tensor_copy(dbg[0:8, 6:7], selcnt[:]), reads=["moe_selcnt"], writes=["moe_dbg"])
    P.fence()
    A.free_to(mkR)
    posp = A.sb("moe_posp", [128, 32], F32, side="right")
    gsp = A.sb("moe_gsp", [128, 32], F32, side="right")
    payl = A.sb("moe_payl", [128, 64], F32, side="right")
    ohw = A.sb("moe_ohw", [128, 512], F32, side="right")
    for e_ in range(E):
        P.emit("sync", lambda e, e_=e_: e.dma_start(
            out=posp[:], in_=md["pos_d"][e_, :].rearrange("(p w) -> p w", p=128)),
            reads=["moe_pos_d"], writes=["moe_posp"], dma=True)
        P.emit("gpsimd", lambda e, e_=e_, cin=cin: e.dma_start(
            out=gsp[:], in_=cin[e_, :].rearrange("(p w) -> p w", p=128)),
            reads=["moe_cin"], writes=["moe_gsp"], dma=True)
        P.emit("vector", lambda e: e.tensor_copy(payl[:, 0:64:2], iotok[:]),
               reads=["moe_iotok"], writes=["moe_payl"])
        P.emit("vector", lambda e: e.tensor_copy(payl[:, 1:64:2], gsp[:]),
               reads=["moe_gsp"], writes=["moe_payl"])
        for w in range(32):
            P.emit("vector", lambda e, w=w: e.tensor_scalar(ohw[:], iorow[:], posp[:, w:w + 1], None, op0=ALU.is_equal),
                   reads=["moe_iorow"], sreads=["moe_posp"], writes=["moe_ohw"])
            for sc in range(NSC):
                P.emit("tensor", lambda e, w=w, sc=sc, st=(w == 0 and sc == 0), sp=(w == 31 and sc == NSC - 1): e.matmul(
                    ph[4][:, 2 * sc:2 * sc + 2], ohw[:, sc * 128:(sc + 1) * 128], payl[:, 2 * w:2 * w + 2],
                    start=st, stop=sp),
                    reads=["moe_ohw", "moe_payl"], writes=["ph4"])
        for sc in range(NSC):
            col = e_ * 4 + sc
            P.emit("vector", lambda e, sc=sc, col=col: e.tensor_copy(idxf[:, col:col + 1], ph[4][:, 2 * sc:2 * sc + 1]),
                   reads=["ph4"], writes=["moe_idxf"])
            P.emit("vector", lambda e, sc=sc, col=col: e.tensor_copy(gval[:, col:col + 1], ph[4][:, 2 * sc + 1:2 * sc + 2]),
                   reads=["ph4"], writes=["moe_gval"])
            P.emit("vector", lambda e, col=col: e.tensor_copy(idxu[:, col:col + 1], idxf[:, col:col + 1]),
                   reads=["moe_idxf"], writes=["moe_idxu"])
    if DEBUG_ON:
        tap(P, A, "idxf", "moe_idxf", lambda: A.t["moe_idxf"][:])
        tap(P, A, "gval", "moe_gval", lambda: A.t["moe_gval"][:])
    P.fence()
    A.free_to(mkS)
    mkE = A.mark()
    idxsh = A.sb("moe_idxsh", [128, 1], F32, side="right")
    h1t = [A.sb(f"moe_h1_{f_}", [128, CAP], BF16, side="right") for f_ in range(32)]
    for pp in range(2):
        for i in range(NDM):
            A.sb(f"moe_w1c{pp}_{i}", [128, 128], BF16, side="right")
    for pp in range(2):
        A.sb(f"moe_w2c{pp}", [128, 512], BF16, side="right")
    b1sb = A.sb("moe_b1_s", [128, 32], F32, side="right")
    b2bf = A.sb("moe_b2bf", [1, 1024], BF16, side="right")
    w1b, w2b = md["w1b"], md["w2b"]
    orbf = g["ones_row_bf"]
    for e_ in range(E):
        P.emit("sync", lambda e, e_=e_: e.dma_start(out=b1sb[:], in_=W["moe_b1"][e_, :, :]),
               writes=["moe_b1_s"], dma=True)
        P.emit("sync", lambda e, e_=e_: e.dma_start(out=b2bf[:], in_=md["b2bf"][e_:e_ + 1, :]),
               reads=["hbm_b2bf"], writes=["moe_b2bf"], dma=True)
        mkT = A.mark()
        tokm = A.sb("moe_tokm", [128, 1024], BF16, side="right")
        tksT = [A.sb(f"moe_tk{i}", [128, CAP], BF16, side="right") for i in range(NDM)]
        for sc in range(NSC):
            P.emit("gpsimd", lambda e, e_=e_, sc=sc: e.indirect_dma_start(
                out=tokm[:], out_offset=None, in_=lnst[:, :],
                in_offset=IndirectOffsetOnAxis(ap=idxu[:, e_ * 4 + sc:e_ * 4 + sc + 1], axis=0)),
                reads=["moe_lnst", "moe_idxu"], writes=["moe_tokm"], dma=True)
            for i in range(NDM):
                P.emit("tensor", lambda e, i=i: e.transpose(phbf[:, 0:128], tokm[:, i * 128:(i + 1) * 128], g["ident_bf"][:]),
                       reads=["moe_tokm", "ident_bf"], writes=["phbf"])
                P.emit("vector", lambda e, i=i, sc=sc: e.tensor_copy(tksT[i][:, sc * 128:(sc + 1) * 128], phbf[:, 0:128]),
                       reads=["phbf"], writes=[f"moe_tk{i}"])
        for f_ in range(32):
            pp = f_ % 2
            for i in range(NDM):
                wt = A.t[f"moe_w1c{pp}_{i}"]
                P.emit("sync", lambda e, wt=wt, e_=e_, f_=f_, i=i: e.dma_start(
                    out=wt[:], in_=w1b[e_, f_, i, :, :]),
                    reads=[f"mw1_{e_}_blk"], writes=[f"moe_w1c{pp}_{i}"], dma=True)
            for i in range(NDM):
                wt = A.t[f"moe_w1c{pp}_{i}"]
                P.emit("tensor", lambda e, wt=wt, i=i, st=(i == 0), sp=(i == NDM - 1): e.matmul(
                    ph[0][:, 0:CAP], wt[:], tksT[i][:], start=st, stop=sp),
                    reads=[f"moe_w1c{pp}_{i}", f"moe_tk{i}"], writes=["ph0"])
            P.emit("scalar", lambda e, f_=f_: e.activation(out=h1t[f_][:], in_=ph[0][:, 0:CAP], func=AF.Relu,
                                                           bias=b1sb[:, f_:f_ + 1]),
                   reads=["ph0"], sreads=["moe_b1_s"], writes=[f"moe_h1_{f_}"])
            if e_ == 0 and f_ == 0 and DEBUG_ON:
                tap(P, A, "w1c_07", "moe_w1c0_7", lambda: A.t["moe_w1c0_7"][:])
        if e_ == 0 and DEBUG_ON:
            tap(P, A, "tk7", "moe_tk7", lambda: tksT[7][:])
            tap(P, A, "h1_0", "moe_h1_0", lambda: h1t[0][:])
        P.fence()
        A.free_to(mkT)
        eot = {}
        for sc in range(NSC):
            for dv in range(2):
                eot[(sc, dv)] = A.sb(f"moe_eo{sc}_{dv}", [128, 512], BF16, side="right")
        pg = A.sb("moe_pg", [128, 512 * NSC], BF16, side="right")
        for dv in range(2):
            for f_ in range(32):
                pp = f_ % 2
                w2t = A.t[f"moe_w2c{pp}"]
                P.emit("sync", lambda e, w2t=w2t, e_=e_, f_=f_, dv=dv: e.dma_start(
                    out=w2t[:].rearrange("p (j c) -> p j c", j=4),
                    in_=w2b[e_, dv * 4:dv * 4 + 4, f_, :, :].rearrange("j p c -> p j c")),
                    reads=[f"mw2_{e_}_blk"], writes=[f"moe_w2c{pp}"], dma=True)
                for sc in range(NSC):
                    P.emit("tensor", lambda e, f_=f_, sc=sc, w2t=w2t, st=(f_ == 0): e.matmul(
                        ph[sc][:, 0:512], h1t[f_][:, sc * 128:(sc + 1) * 128], w2t[:], start=st, stop=False),
                        reads=[f"moe_h1_{f_}", f"moe_w2c{pp}"], writes=[f"ph{sc}"])
            for sc in range(NSC):
                P.emit("tensor", lambda e, sc=sc, dv=dv: e.matmul(
                    ph[sc][:, 0:512], orbf[0:1, 0:128], b2bf[0:1, dv * 512:(dv + 1) * 512], start=False, stop=True),
                    reads=["moe_b2bf", "ones_row_bf"], writes=[f"ph{sc}"])
                P.emit("scalar", lambda e, sc=sc, dv=dv: e.activation(out=eot[(sc, dv)][:], in_=ph[sc][:, 0:512], func=AF.Copy),
                       reads=[f"ph{sc}"], writes=[f"moe_eo{sc}_{dv}"])
        for tokc in range(8):
            for sc in range(NSC):
                P.emit("vector", lambda e, e_=e_, sc=sc, tokc=tokc: e.tensor_scalar(
                    idxsh[:], idxf[:, e_ * 4 + sc:e_ * 4 + sc + 1], float(-512 * tokc), None, op0=ALU.add),
                    reads=["moe_idxf"], writes=["moe_idxsh"])
                P.emit("vector", lambda e, e_=e_, sc=sc: e.tensor_scalar(
                    pg[:, sc * 512:(sc + 1) * 512], iorow[:],
                    idxsh[:], gval[:, e_ * 4 + sc:e_ * 4 + sc + 1],
                    op0=ALU.is_equal, op1=ALU.mult),
                    reads=["moe_iorow"], sreads=["moe_idxsh", "moe_gval"], writes=["moe_pg"])
            for dmc in range(NDM):
                dv = dmc // 4
                pk = "ph4" if dmc % 2 == 0 else "ph5"
                pht_ = ph[4] if dmc % 2 == 0 else ph[5]
                for sc in range(NSC):
                    eo = eot[(sc, dv)]
                    P.emit("tensor", lambda e, eo=eo, sc=sc, dmc=dmc, pht_=pht_, st=(sc == 0), sp=(sc == NSC - 1):
                           e.matmul(pht_[:, 0:512], eo[:, (dmc % 4) * 128:(dmc % 4 + 1) * 128],
                                    pg[:, sc * 512:(sc + 1) * 512], start=st, stop=sp),
                           reads=[f"moe_eo{sc}_{dv}", "moe_pg"], writes=[pk])
                xo = A.t[xk[dmc]]
                s = slice(tokc * 512, (tokc + 1) * 512)
                P.emit("vector", lambda e, xo=xo, s=s, pht_=pht_: e.scalar_tensor_tensor(
                    xo[:, s], pht_[:, 0:512], 0.5, xo[:, s], op0=ALU.mult, op1=ALU.add),
                    reads=[pk, xk[dmc]], writes=[xk[dmc]])
        P.fence()
        A.free_to(mkT)
    P.fence()
    A.free_to(mkE)
    A.free_to(mkTOP)


def build_sl3_standalone():
    return build_sl1_standalone()


def build_sl6_standalone():
    return build_sl4_standalone()


def kernel(**inputs):
    x = np.asarray(inputs["L0_attn_x"], dtype=np.float32)
    xT = [np.ascontiguousarray(x[c].T) for c in range(N_CORES)]
    Wmoe = [prep_l0ffn_weights(inputs, c) for c in range(N_CORES)]
    Watt = prep_l0attn_weights(inputs)
    W4 = prep_l1ffn_weights(inputs)
    W5 = prep_l1attn_weights(inputs)

    def run(nc, wlist):
        ims = [dict(wlist[c]) for c in range(N_CORES)]
        for c in range(N_CORES):
            ims[c]["xT"] = xT[c]
        res = run_bass_kernel_spmd(nc, ims, core_ids=list(range(N_CORES)))
        for c in range(N_CORES):
            xT[c] = np.ascontiguousarray(np.asarray(res.results[c]["yT"], np.float32))

    global DEBUG_TAPS
    DEBUG_TAPS = []
    run(build_sl1_standalone(), Wmoe)
    run(build_sl2_standalone(), [Watt] * N_CORES)
    run(build_sl3_standalone(), Wmoe)
    run(build_sl4_standalone(), [W4] * N_CORES)
    run(build_sl5_standalone(), [W5] * N_CORES)
    run(build_sl6_standalone(), [W4] * N_CORES)
    return np.stack([xT[c].T for c in range(N_CORES)], axis=0)




def sl_l1attn(P, A, g, W, xk, xres, xout_ap, alloc_x=None):
    nc = P.nc
    TC = 512
    NTC = SEQ // TC
    ph = [A.t[f"ph{i}"] for i in range(6)]
    phbf = A.t["phbf"]
    dirs = ("f", "b")
    hxi = {d: nc.dram_tensor(f"hbm_xi_{d}", [2048, SEQ], BF16).ap() for d in dirs}
    hz = {d: nc.dram_tensor(f"hbm_z_{d}", [2048, SEQ], BF16).ap() for d in dirs}
    hmo = {d: nc.dram_tensor(f"hbm_mo_{d}", [1024, SEQ], BF16).ap() for d in dirs}
    hinw = {d: nc.dram_tensor(f"hbm_inw_{d}", [32, 8, 128, 128], BF16).ap() for d in dirs}
    houtw = {d: nc.dram_tensor(f"hbm_outw_{d}", [8, 16, 128, 128], BF16).ap() for d in dirs}
    hprj = nc.dram_tensor("hbm_prj", [8, 16, 128, 128], BF16).ap()
    for d in dirs:
        cast_to_hbm_blocked(P, A, f"inw{d}", W[f"mb_inWT_{d}"], 1024, 4096, hinw[d], "wstage1k")
        cast_to_hbm_blocked(P, A, f"outw{d}", W[f"mb_outWT_{d}"], 2048, 1024, houtw[d], "wstage1k")
    cast_to_hbm_blocked(P, A, "prj", W["mb_projT"], 2048, 1024, hprj, "wstage1k")
    for d in dirs:
        cw = A.sb(f"mb_cw_{d}", [128, 64], F32)
        P.emit("sync", lambda e, cw=cw, d=d: e.dma_start(out=cw[:], in_=W[f"mb_convw_{d}"][:, :]),
               writes=[f"mb_cw_{d}"], dma=True)
        cb = A.sb(f"mb_cb_{d}", [128, 16], F32)
        P.emit("sync", lambda e, cb=cb, d=d: e.dma_start(out=cb[:], in_=W[f"mb_convb_{d}"][:, :]),
               writes=[f"mb_cb_{d}"], dma=True)
        cbh = A.sb(f"mb_cbh_{d}", [128, 16], F32)
        P.emit("vector", lambda e, cbh=cbh, cb=cb: e.tensor_scalar(cbh[:], cb[:], 0.5, None, op0=ALU.mult),
               reads=[f"mb_cb_{d}"], writes=[f"mb_cbh_{d}"])
        load_weight_bf16(P, A, f"mb_xpT_{d}", W[f"mb_xpT_{d}"], [128, 96 * 16], stage="wstage1k")
        load_weight_bf16(P, A, f"mb_dtWT_{d}", W[f"mb_dtWT_{d}"], [64, 2048], stage="wstage1k")
        alog = A.sb(f"mb_A_{d}", [128, 256], F32)
        P.emit("sync", lambda e, alog=alog, d=d: e.dma_start(out=alog[:], in_=W[f"mb_Alog_{d}"][:, :]),
               writes=[f"mb_A_{d}"], dma=True)
        P.emit("scalar", lambda e, alog=alog: e.activation(out=alog[:], in_=alog[:], func=AF.Exp),
               reads=[f"mb_A_{d}"], writes=[f"mb_A_{d}"])
        P.emit("vector", lambda e, alog=alog: e.tensor_scalar(alog[:], alog[:], -1.0, None, op0=ALU.mult),
               reads=[f"mb_A_{d}"], writes=[f"mb_A_{d}"])
        dsb = A.sb(f"mb_D_{d}", [128, 16], F32)
        P.emit("sync", lambda e, dsb=dsb, d=d: e.dma_start(out=dsb[:], in_=W[f"mb_D_{d}"][:, :]),
               writes=[f"mb_D_{d}"], dma=True)
        dtb = A.sb(f"mb_dtb_{d}", [128, 16], F32)
        P.emit("sync", lambda e, dtb=dtb, d=d: e.dma_start(out=dtb[:], in_=W[f"mb_dtb_{d}"][:, :]),
               writes=[f"mb_dtb_{d}"], dma=True)
    load_weight_bf16(P, A, "mb_seln", W["mb_seln"], [32, 4096], stage="wstage1k")
    pjb = A.sb("mb_pjb", [128, 8], F32)
    P.emit("sync", lambda e: e.dma_start(out=pjb[:], in_=W["mb_pjb"][:, :]), writes=["mb_pjb"], dma=True)
    if alloc_x is not None:
        alloc_x()

    mk1 = A.mark()
    for i in range(NDM):
        A.sb(f"mb_xc{i}", [128, 512], BF16, side="right")
    xlnc = [f"mb_xc{i}" for i in range(NDM)]
    for pp in range(2):
        for i in range(NDM):
            A.sb(f"mb_wc{pp}_{i}", [128, 128], BF16, side="right")
    ev = A.sb("mb_ev", [128, 512], BF16, side="right")
    for tc in range(8):
        s = slice(tc * 512, (tc + 1) * 512)
        ln_chunk(P, A, g, xk, tc, xlnc)
        for d in dirs:
            for mc in range(32):
                pp = mc % 2
                for i in range(NDM):
                    wc = A.t[f"mb_wc{pp}_{i}"]
                    P.emit("sync" if i % 2 == 0 else "gpsimd",
                           lambda e, wc=wc, d=d, mc=mc, i=i: e.dma_start(out=wc[:], in_=hinw[d][mc, i, :, :]),
                           reads=[f"inw{d}_blk"], writes=[f"mb_wc{pp}_{i}"], dma=True)
                for i in range(NDM):
                    wc = A.t[f"mb_wc{pp}_{i}"]
                    xc = A.t[xlnc[i]]
                    P.emit("tensor", lambda e, wc=wc, xc=xc, st=(i == 0), sp=(i == NDM - 1):
                           e.matmul(ph[0][:, 0:512], wc[:], xc[:], start=st, stop=sp),
                           reads=[f"mb_wc{pp}_{i}", xlnc[i]], writes=["ph0"])
                if d == "f":
                    P.emit("scalar", lambda e: e.activation(out=ev[:], in_=ph[0][:, 0:512], func=AF.Copy),
                           reads=["ph0"], writes=["mb_ev"])
                else:
                    P.emit("scalar", lambda e: e.activation(out=ev[:, ::-1], in_=ph[0][:, 0:512], func=AF.Copy),
                           reads=["ph0"], writes=["mb_ev"])
                xi_or_z = hxi[d] if mc < 16 else hz[d]
                row = (mc % 16) * 128
                col = s if d == "f" else slice((7 - tc) * 512, (8 - tc) * 512)
                P.emit("gpsimd", lambda e, xi_or_z=xi_or_z, row=row, col=col: e.dma_start(
                    out=xi_or_z[row:row + 128, col], in_=ev[:]),
                    reads=["mb_ev"], writes=["mb_spill"], dma=True)
    P.fence()
    A.free_to(mk1)
    for i in range(NDM):
        x = A.t[xk[i]]
        P.emit("sync", lambda e, x=x, i=i: e.dma_start(out=xres[i * 128:(i + 1) * 128, :], in_=x[:]),
               reads=[xk[i]], writes=["mb_xres"], dma=True)
    P.fence()
    A.free_left(*xk)

    def p2_dir(d):
        mk2 = A.mark()
        xct = [A.sb(f"mbx{i}", [128, 512], BF16, side="right") for i in range(16)]
        mkxi = A.mark()
        xit = [A.sb(f"mbxi{i}", [128, 515], BF16, side="right") for i in range(16)]
        cw, cbh = A.t[f"mb_cw_{d}"], A.t[f"mb_cbh_{d}"]
        cb = A.t[f"mb_cb_{d}"]
        alog, dsb, dtb = A.t[f"mb_A_{d}"], A.t[f"mb_D_{d}"], A.t[f"mb_dtb_{d}"]
        xpT = A.t[f"mb_xpT_{d}"]
        dtWT = A.t[f"mb_dtWT_{d}"]
        seln = A.t["mb_seln"]
        convt = A.sb("mb_cvt", [128, 512], F32, side="right")
        convt2 = A.sb("mb_cvt2", [128, 512], F32, side="right")
        xdbl = A.sb("mb_xdbl", [64, 512], BF16, side="right")
        bcsb = A.sb("mb_bcsb", [32, 512], BF16, side="right")
        BB = A.sb("mb_BB", [128, 16 * 512], BF16, side="right")
        CB = A.sb("mb_CB", [128, 16 * 512], BF16, side="right")
        hint = A.sb("mb_hint", [128, 16 * 512], BF16, side="right")
        carry = A.sb("mb_carry", [128, 256], BF16, side="right")
        dts = A.sb("mb_dts", [128, 512], BF16, side="right")
        dtx = A.sb("mb_dtx", [128, 512], BF16, side="right")
        dA = A.sb("mb_dA", [128, 512], F32, side="right")
        ut = A.sb("mb_ut", [128, 512], F32, side="right")
        yt = A.sb("mb_yt", [128, 512], F32, side="right")
        zt = A.sb("mb_zt", [128, 512], BF16, side="right")
        zs = A.sb("mb_zs", [128, 512], F32, side="right")
        for pp in range(2):
            A.sb(f"mb_owc{pp}", [128, 128], BF16, side="right")
        for tc in range(NTC):
            s = slice(tc * 512, (tc + 1) * 512)
            for i in range(16):
                xi_ = xit[i]
                if tc == 0:
                    P.emit("vector", lambda e, xi_=xi_: e.memset(xi_[:, 0:3], 0.0), writes=[f"mbxi{i}"])
                    P.emit("sync", lambda e, xi_=xi_, i=i: e.dma_start(
                        out=xi_[:, 3:515], in_=hxi[d][i * 128:(i + 1) * 128, 0:512]),
                        reads=["mb_spill"], writes=[f"mbxi{i}"], dma=True)
                else:
                    P.emit("sync", lambda e, xi_=xi_, i=i, tc=tc: e.dma_start(
                        out=xi_[:, 0:515], in_=hxi[d][i * 128:(i + 1) * 128, tc * 512 - 3:(tc + 1) * 512]),
                        reads=["mb_spill"], writes=[f"mbxi{i}"], dma=True)
            for i in range(16):
                xi_ = xit[i]
                xc_ = xct[i]
                P.emit("vector", lambda e, xi_=xi_, i=i: e.tensor_scalar(
                    convt[:], xi_[:, 0:512], cw[:, 4 * i:4 * i + 1], None, op0=ALU.mult),
                    reads=[f"mbxi{i}"], sreads=[f"mb_cw_{d}"], writes=["mb_cvt"])
                for j in (1, 2, 3):
                    src_t = convt if j % 2 == 1 else convt2
                    dst_t = convt2 if j % 2 == 1 else convt
                    sk = "mb_cvt" if j % 2 == 1 else "mb_cvt2"
                    dk = "mb_cvt2" if j % 2 == 1 else "mb_cvt"
                    P.emit("vector", lambda e, xi_=xi_, i=i, j=j, src_t=src_t, dst_t=dst_t: e.scalar_tensor_tensor(
                        dst_t[:], xi_[:, j:512 + j], cw[:, 4 * i + j:4 * i + j + 1], src_t[:],
                        op0=ALU.mult, op1=ALU.add),
                        reads=[f"mbxi{i}", sk], sreads=[f"mb_cw_{d}"], writes=[dk])
                P.emit("vector", lambda e, i=i: e.tensor_scalar(
                    convt[:], convt2[:], cb[:, i:i + 1], None, op0=ALU.add),
                    reads=["mb_cvt2"], sreads=[f"mb_cb_{d}"], writes=["mb_cvt"])
                P.emit("scalar", lambda e, i=i: e.activation(out=convt2[:], in_=convt2[:], func=AF.Tanh,
                                                             scale=0.5, bias=cbh[:, i:i + 1]),
                       reads=["mb_cvt2"], sreads=[f"mb_cbh_{d}"], writes=["mb_cvt2"])
                P.emit("vector", lambda e: e.tensor_scalar(convt2[:], convt2[:], 0.5, 0.5, op0=ALU.mult, op1=ALU.add),
                       reads=["mb_cvt2"], writes=["mb_cvt2"])
                P.emit("vector", lambda e, xc_=xc_, i=i: e.tensor_tensor(xc_[:], convt[:], convt2[:], op=ALU.mult),
                       reads=["mb_cvt", "mb_cvt2"], writes=[f"mbx{i}"])
            if DEBUG_ON and d == "f" and tc == 0:
                tap(P, A, "xc0", "mbx0", lambda: xct[0][:])
                tap(P, A, "xc15", "mbx15", lambda: xct[15][:])
            for i in range(16):
                xc_ = xct[i]
                P.emit("tensor", lambda e, xc_=xc_, i=i, st=(i == 0), sp=(i == 15): e.matmul(
                    ph[1][0:64, 0:512], xpT[:, i * 96:i * 96 + 64], xc_[:], start=st, stop=sp),
                    reads=[f"mb_xpT_{d}", f"mbx{i}"], writes=["ph1"])
                P.emit("tensor", lambda e, xc_=xc_, i=i, st=(i == 0), sp=(i == 15): e.matmul(
                    ph[2][0:32, 0:512], xpT[:, i * 96 + 64:i * 96 + 96], xc_[:], start=st, stop=sp),
                    reads=[f"mb_xpT_{d}", f"mbx{i}"], writes=["ph2"])
            P.emit("scalar", lambda e: e.activation(out=xdbl[:], in_=ph[1][0:64, 0:512], func=AF.Copy),
                   reads=["ph1"], writes=["mb_xdbl"])
            P.emit("scalar", lambda e: e.activation(out=bcsb[:], in_=ph[2][0:32, 0:512], func=AF.Copy),
                   reads=["ph2"], writes=["mb_bcsb"])
            for n in range(16):
                P.emit("tensor", lambda e, n=n: e.matmul(ph[3][:, 0:512], seln[:, n * 128:(n + 1) * 128], bcsb[:],
                                                         start=True, stop=True),
                       reads=["mb_seln", "mb_bcsb"], writes=["ph3"])
                P.emit("scalar", lambda e, n=n: e.activation(out=BB[:, n * 512:(n + 1) * 512], in_=ph[3][:, 0:512], func=AF.Copy),
                       reads=["ph3"], writes=["mb_BB"])
                P.emit("tensor", lambda e, n=n: e.matmul(ph[4][:, 0:512], seln[:, (16 + n) * 128:(17 + n) * 128], bcsb[:],
                                                         start=True, stop=True),
                       reads=["mb_seln", "mb_bcsb"], writes=["ph4"])
                P.emit("scalar", lambda e, n=n: e.activation(out=CB[:, n * 512:(n + 1) * 512], in_=ph[4][:, 0:512], func=AF.Copy),
                       reads=["ph4"], writes=["mb_CB"])
            for i in range(16):
                xc_ = xct[i]
                P.emit("tensor", lambda e, i=i: e.matmul(ph[5][:, 0:512], dtWT[:, i * 128:(i + 1) * 128], xdbl[:],
                                                         start=True, stop=True),
                       reads=[f"mb_dtWT_{d}", "mb_xdbl"], writes=["ph5"])
                P.emit("scalar", lambda e, i=i: e.activation(out=dA[:], in_=ph[5][:, 0:512], func=AF.Exp,
                                                             bias=dtb[:, i:i + 1]),
                       reads=["ph5"], sreads=[f"mb_dtb_{d}"], writes=["mb_dA"])
                P.emit("scalar", lambda e: e.activation(out=dts[:], in_=dA[:], func=AF.Ln, bias=1.0),
                       reads=["mb_dA"], writes=["mb_dts"])
                P.emit("vector", lambda e, xc_=xc_, i=i: e.tensor_tensor(dtx[:], dts[:], xc_[:], op=ALU.mult),
                       reads=["mb_dts", f"mbx{i}"], writes=["mb_dtx"])
                if DEBUG_ON and d == "f" and tc == 0 and i == 0:
                    tap(P, A, "dts0", "mb_dts", lambda: dts[:])
                for n in range(16):
                    P.emit("scalar", lambda e, i=i, n=n: e.activation(out=dA[:], in_=dts[:], func=AF.Exp,
                                                                      scale=alog[:, i * 16 + n:i * 16 + n + 1]),
                           reads=["mb_dts"], sreads=[f"mb_A_{d}"], writes=["mb_dA"])
                    P.emit("vector", lambda e, n=n: e.tensor_tensor(ut[:], dtx[:], BB[:, n * 512:(n + 1) * 512], op=ALU.mult),
                           reads=["mb_dtx", "mb_BB"], writes=["mb_ut"])
                    if tc == 0:
                        P.emit("vector", lambda e, i=i, n=n: e.tensor_tensor_scan(
                            hint[:, n * 512:(n + 1) * 512], dA[:], ut[:], 0.0, op0=ALU.mult, op1=ALU.add),
                            reads=["mb_dA", "mb_ut"], writes=["mb_hint"])
                    else:
                        P.emit("vector", lambda e, i=i, n=n: e.tensor_tensor_scan(
                            hint[:, n * 512:(n + 1) * 512], dA[:], ut[:], carry[:, i * 16 + n:i * 16 + n + 1],
                            op0=ALU.mult, op1=ALU.add),
                            reads=["mb_dA", "mb_ut"], sreads=["mb_carry"], writes=["mb_hint"])
                if tc < NTC - 1:
                    P.emit("vector", lambda e, i=i: e.tensor_copy(
                        carry[:, i * 16:(i + 1) * 16], hint[:, 511::512]),
                        reads=["mb_hint"], writes=["mb_carry"])
                P.emit("vector", lambda e, i=i: e.tensor_tensor(hint[:], hint[:], CB[:], op=ALU.mult),
                       reads=["mb_hint", "mb_CB"], writes=["mb_hint"])
                P.emit("vector", lambda e: e.tensor_reduce(
                    yt[:], hint[:].rearrange("p (n t) -> p t n", n=16), axis=AX.X, op=ALU.add),
                    reads=["mb_hint"], writes=["mb_yt"])
                P.emit("vector", lambda e, xc_=xc_, i=i: e.scalar_tensor_tensor(
                    yt[:], xc_[:], dsb[:, i:i + 1], yt[:], op0=ALU.mult, op1=ALU.add),
                    reads=[f"mbx{i}", "mb_yt"], sreads=[f"mb_D_{d}"], writes=["mb_yt"])
                if DEBUG_ON and d == "f" and tc == 0 and i == 0:
                    tap(P, A, "yt0", "mb_yt", lambda: yt[:])
                P.emit("sync", lambda e, i=i, s=s: e.dma_start(out=zt[:], in_=hz[d][i * 128:(i + 1) * 128, s]),
                       reads=["mb_spill"], writes=["mb_zt"], dma=True)
                P.emit("scalar", lambda e: e.activation(out=zs[:], in_=zt[:], func=AF.Tanh, scale=0.5),
                       reads=["mb_zt"], writes=["mb_zs"])
                P.emit("vector", lambda e: e.tensor_scalar(zs[:], zs[:], 0.5, 0.5, op0=ALU.mult, op1=ALU.add),
                       reads=["mb_zs"], writes=["mb_zs"])
                P.emit("vector", lambda e: e.tensor_tensor(zs[:], zs[:], zt[:], op=ALU.mult),
                       reads=["mb_zs", "mb_zt"], writes=["mb_zs"])
                P.emit("vector", lambda e, xc_=xc_, i=i: e.tensor_tensor(xc_[:], yt[:], zs[:], op=ALU.mult),
                       reads=["mb_yt", "mb_zs"], writes=[f"mbx{i}"])
                if DEBUG_ON and d == "f" and tc == 0 and i == 0:
                    tap(P, A, "zs0", "mb_zs", lambda: zs[:])
                    tap(P, A, "yg0", "mbx0", lambda: xc_[:])
            for mc in range(8):
                for i in range(16):
                    pp = i % 2
                    owc = A.t[f"mb_owc{pp}"]
                    P.emit("sync", lambda e, owc=owc, d=d, mc=mc, i=i: e.dma_start(
                        out=owc[:], in_=houtw[d][mc, i, :, :]),
                        reads=[f"outw{d}_blk"], writes=[f"mb_owc{pp}"], dma=True)
                    xc_ = xct[i]
                    P.emit("tensor", lambda e, owc=owc, xc_=xc_, st=(i == 0), sp=(i == 15): e.matmul(
                        ph[0][:, 0:512], owc[:], xc_[:], start=st, stop=sp),
                        reads=[f"mb_owc{pp}", f"mbx{i}"], writes=["ph0"])
                if d == "f":
                    P.emit("scalar", lambda e: e.activation(out=dA[:], in_=ph[0][:, 0:512], func=AF.Copy),
                           reads=["ph0"], writes=["mb_dA"])
                else:
                    P.emit("scalar", lambda e: e.activation(out=dA[:, ::-1], in_=ph[0][:, 0:512], func=AF.Copy),
                           reads=["ph0"], writes=["mb_dA"])
                col = s if d == "f" else slice((NTC - 1 - tc) * 512, (NTC - tc) * 512)
                P.emit("vector", lambda e: e.tensor_copy(zt[:], dA[:]), reads=["mb_dA"], writes=["mb_zt"])
                P.emit("gpsimd", lambda e, mc=mc, col=col, d=d: e.dma_start(
                    out=hmo[d][mc * 128:(mc + 1) * 128, col], in_=zt[:]),
                    reads=["mb_zt"], writes=["mb_mo"], dma=True)
        P.fence()
        A.free_to(mk2)

    for d in dirs:
        p2_dir(d)
    mk3 = A.mark()
    mot = [A.sb(f"mb_mot{i}", [128, 512], BF16, side="right") for i in range(16)]
    xtl = [A.sb(f"mb_xt{i}", [128, 512], F32, side="right") for i in range(NDM)]
    for pp in range(2):
        A.sb(f"mb_pjc{pp}", [128, 128], BF16, side="right")
    for tc in range(8):
        s = slice(tc * 512, (tc + 1) * 512)
        for i in range(16):
            src_d = "f" if i < 8 else "b"
            row = (i % 8) * 128
            P.emit("sync", lambda e, i=i, src_d=src_d, row=row, s=s: e.dma_start(
                out=mot[i][:], in_=hmo[src_d][row:row + 128, s]),
                reads=["mb_mo"], writes=[f"mb_mot{i}"], dma=True)
        for dmc in range(NDM):
            P.emit("sync", lambda e, dmc=dmc, s=s: e.dma_start(
                out=xtl[dmc][:], in_=xres[dmc * 128:(dmc + 1) * 128, s]),
                reads=["mb_xres"], writes=[f"mb_xt{dmc}"], dma=True)
        for dmc in range(NDM):
            pb = "ph4" if dmc % 2 == 0 else "ph5"
            pht_ = ph[4] if dmc % 2 == 0 else ph[5]
            for i in range(16):
                pp = i % 2
                pjc = A.t[f"mb_pjc{pp}"]
                P.emit("gpsimd", lambda e, pjc=pjc, dmc=dmc, i=i: e.dma_start(
                    out=pjc[:], in_=hprj[dmc, i, :, :]),
                    reads=["prj_blk"], writes=[f"mb_pjc{pp}"], dma=True)
                P.emit("tensor", lambda e, pjc=pjc, i=i, pht_=pht_, st=(i == 0), sp=(i == 15): e.matmul(
                    pht_[:, 0:512], pjc[:], mot[i][:], start=st, stop=sp),
                    reads=[f"mb_pjc{pp}", f"mb_mot{i}"], writes=[pb])
            P.emit("vector", lambda e, dmc=dmc, pht_=pht_: e.scalar_tensor_tensor(
                xtl[dmc][:], pht_[:, 0:512], pjb[:, dmc:dmc + 1], xtl[dmc][:], op0=ALU.add, op1=ALU.add),
                reads=[pb, f"mb_xt{dmc}"], sreads=["mb_pjb"], writes=[f"mb_xt{dmc}"])
            P.emit("gpsimd", lambda e, dmc=dmc, s=s: e.dma_start(
                out=xout_ap[dmc * 128:(dmc + 1) * 128, s], in_=xtl[dmc][:]),
                reads=[f"mb_xt{dmc}"], writes=["mb_xout"], dma=True)
    P.fence()
    A.free_to(mk3)


def prep_l0ffn_weights(inputs, core_id):
    W = {}
    rw = np.asarray(inputs["L0_ffn_router_w"])
    W["moe_rw"] = np.ascontiguousarray(rw.reshape(8, 128, 8).transpose(1, 0, 2).reshape(128, 64))
    W["moe_rb"] = np.ascontiguousarray(np.asarray(inputs["L0_ffn_router_b"]).reshape(1, 8))
    W["moe_w1"] = np.ascontiguousarray(np.asarray(inputs["L0_ffn_w1"]))
    W["moe_w2"] = np.ascontiguousarray(np.asarray(inputs["L0_ffn_w2"]))
    W["moe_b1"] = np.ascontiguousarray(np.asarray(inputs["L0_ffn_b1"]).reshape(8, 32, 128).transpose(0, 2, 1))
    W["moe_b2"] = np.ascontiguousarray(np.asarray(inputs["L0_ffn_b2"]))
    L = np.zeros((128, 128), np.float32)
    for k in range(128):
        for p in range(128):
            if k // 16 == p // 16 and (k % 16) < (p % 16):
                L[k, p] = 1.0
    W["moe_L"] = L
    S = np.zeros((128, 8), np.float32)
    for p in range(128):
        S[p, p // 16] = 1.0
    W["moe_S"] = S
    W["moe_SE"] = np.ascontiguousarray(S.T)
    oh = np.zeros((128, 8), np.float32)
    for e_ in range(8):
        oh[e_ * 16 + 2 * core_id, e_] = 1.0
    W["moe_ownhot"] = oh
    return W


MOE_W_SHAPES = dict(
    moe_rw=[128, 64], moe_rb=[1, 8], moe_w1=[8, 1024, 4096], moe_w2=[8, 4096, 1024],
    moe_b1=[8, 128, 32], moe_b2=[8, 1024], moe_L=[128, 128], moe_S=[128, 8],
    moe_SE=[8, 128], moe_ownhot=[128, 8])


def build_sl1_standalone(n_cores=8):
    nc = bass.Bass("TRN2", target_bir_lowering=False)
    xin = nc.dram_tensor("xT", [D_MODEL, SEQ], F32, kind="ExternalInput").ap()
    xout = nc.dram_tensor("yT", [D_MODEL, SEQ], F32, kind="ExternalOutput").ap()
    Wd = {k: nc.dram_tensor(k, sh, F32, kind="ExternalInput").ap()
          for k, sh in MOE_W_SHAPES.items()}
    with ExitStack() as stack:
        A = Alloc(nc, stack)
        P = Prog(nc)
        g = build_common(P, A)
        A.sb("wstage1k", [128, 1024], F32)
        A.sb("wstage1k_bf", [128, 1024], BF16)
        xkeys = [f"x{i}" for i in range(NDM)]
        for i in range(NDM):
            x = A.sb(xkeys[i], [128, SEQ], F32)
            P.emit("sync", lambda e, x=x, i=i: e.dma_start(out=x[:], in_=xin[i * 128:(i + 1) * 128, :]),
                   writes=[xkeys[i]], dma=True)
        for pn in ["ph0", "ph1", "ph2", "ph3", "ph4", "ph5"]:
            A.ps(pn, [128, 512], F32)
        A.ps("phbf", [128, 1024], BF16)
        sl_l0ffn(P, A, g, Wd, xkeys, first_call=True)
        for nm, tk in DEBUG_TAPS:
            t = A.t[tk]
            dt_ = nc.dram_tensor("tap_" + nm, list(t[:].shape), t[:].dtype, kind="ExternalOutput").ap()
            P.emit("gpsimd", lambda e, dt_=dt_, t=t: e.dma_start(out=dt_, in_=t[:]),
                   reads=[tk], writes=["tapdma_" + nm], dma=True)
        for i in range(NDM):
            x = A.t[xkeys[i]]
            P.emit("gpsimd", lambda e, x=x, i=i: e.dma_start(out=xout[i * 128:(i + 1) * 128, :], in_=x[:]),
                   reads=[xkeys[i]], writes=[f"outdma{i}"], dma=True)
        P.barrier_read("gpsimd", [f"outdma{i}" for i in range(NDM)])
        finalize(P, nc, A)
        print(f"SBUF bytes: {A.bytes/1e6:.2f}MB, ops: {len(P.ops)}", flush=True)
    return nc


def prep_l1attn_weights(inputs):
    W = {}
    for d, pre in (("f", "L1_attn_fwd"), ("b", "L1_attn_bwd")):
        W[f"mb_inWT_{d}"] = np.ascontiguousarray(np.asarray(inputs[f"{pre}_in_W"]).T)
        cwr = np.asarray(inputs[f"{pre}_conv_w"])
        W[f"mb_convw_{d}"] = np.ascontiguousarray(
            cwr.reshape(16, 128, 4).transpose(1, 0, 2).reshape(128, 64))
        W[f"mb_convb_{d}"] = np.ascontiguousarray(
            np.asarray(inputs[f"{pre}_conv_b"]).reshape(16, 128).T)
        xp = np.asarray(inputs[f"{pre}_xproj_W"]).T
        W[f"mb_xpT_{d}"] = np.ascontiguousarray(xp.reshape(16, 128, 96).transpose(1, 0, 2).reshape(128, 1536))
        W[f"mb_dtWT_{d}"] = np.ascontiguousarray(np.asarray(inputs[f"{pre}_dt_W"]).T)
        W[f"mb_dtb_{d}"] = np.ascontiguousarray(np.asarray(inputs[f"{pre}_dt_b"]).reshape(16, 128).T)
        W[f"mb_Alog_{d}"] = np.ascontiguousarray(
            np.asarray(inputs[f"{pre}_A_log"]).reshape(16, 128, 16).transpose(1, 0, 2).reshape(128, 256))
        W[f"mb_D_{d}"] = np.ascontiguousarray(np.asarray(inputs[f"{pre}_D"]).reshape(16, 128).T)
        W[f"mb_outWT_{d}"] = np.ascontiguousarray(np.asarray(inputs[f"{pre}_out_W"]).T)
    W["mb_projT"] = np.ascontiguousarray(np.asarray(inputs["L1_attn_proj_W"]).T)
    W["mb_pjb"] = np.ascontiguousarray(np.asarray(inputs["L1_attn_proj_b"]).reshape(8, 128).T)
    seln = np.zeros((32, 32 * 128), np.float32)
    for r in range(32):
        seln[r, r * 128:(r + 1) * 128] = 1.0
    W["mb_seln"] = seln
    return W


MB_W_SHAPES = dict(
    mb_projT=[2048, 1024], mb_pjb=[128, 8], mb_seln=[32, 4096])
for _d in ("f", "b"):
    MB_W_SHAPES.update({
        f"mb_inWT_{_d}": [1024, 4096], f"mb_convw_{_d}": [128, 64], f"mb_convb_{_d}": [128, 16],
        f"mb_xpT_{_d}": [128, 1536], f"mb_dtWT_{_d}": [64, 2048], f"mb_dtb_{_d}": [128, 16],
        f"mb_Alog_{_d}": [128, 256], f"mb_D_{_d}": [128, 16], f"mb_outWT_{_d}": [2048, 1024]})


def build_sl5_standalone():
    nc = bass.Bass("TRN2", target_bir_lowering=False)
    xin = nc.dram_tensor("xT", [D_MODEL, SEQ], F32, kind="ExternalInput").ap()
    xout = nc.dram_tensor("yT", [D_MODEL, SEQ], F32, kind="ExternalOutput").ap()
    xres = nc.dram_tensor("hbm_xres", [D_MODEL, SEQ], F32).ap()
    Wd = {k: nc.dram_tensor(k, sh, F32, kind="ExternalInput").ap()
          for k, sh in MB_W_SHAPES.items()}
    with ExitStack() as stack:
        A = Alloc(nc, stack)
        P = Prog(nc)
        g = build_common(P, A)
        A.sb("wstage1k", [128, 1024], F32)
        A.sb("wstage1k_bf", [128, 1024], BF16)
        for pn in ["ph0", "ph1", "ph2", "ph3", "ph4", "ph5"]:
            A.ps(pn, [128, 512], F32)
        A.ps("phbf", [128, 1024], BF16)
        P2 = P
        xkeys = [f"x{i}" for i in range(NDM)]
        def alloc_x():
            for i in range(NDM):
                x = A.sb(xkeys[i], [128, SEQ], F32)
                P.emit("sync", lambda e, x=x, i=i: e.dma_start(out=x[:], in_=xin[i * 128:(i + 1) * 128, :]),
                       writes=[xkeys[i]], dma=True)
        sl_l1attn(P, A, g, Wd, xkeys, xres, xout, alloc_x=alloc_x)
        P.barrier_read("gpsimd", ["mb_xout"])
        finalize(P, nc, A)
        print(f"SBUF bytes: {A.bytes/1e6:.2f}MB, ops: {len(P.ops)}", flush=True)
    return nc


# version 03
# speedup vs baseline: 1.1464x; 1.1464x over previous
import math
from contextlib import ExitStack

import numpy as np
import concourse.bass as bass
import concourse.mybir as mybir
from concourse.bass_utils import run_bass_kernel_spmd
from concourse.bass import IndirectOffsetOnAxis

F32 = mybir.dt.float32
BF16 = mybir.dt.bfloat16
I32 = mybir.dt.int32
U32 = mybir.dt.uint32
AF = mybir.ActivationFunctionType
ALU = mybir.AluOpType
AX = mybir.AxisListType

ENGINES = ("sync", "tensor", "vector", "scalar", "gpsimd")
DEBUG_TAPS = []

DEBUG_ON = False
DEBUG_HBM = False
SIZES = {}
PARANOID = False


def tapd(P, A, name, key, ap_fn):
    if DEBUG_ON:
        ap = ap_fn()
        d = P.nc.dram_tensor("tap_" + name, list(ap.shape), ap.dtype, kind="ExternalOutput").ap()
        P.emit("gpsimd", lambda e: e.dma_start(out=d, in_=ap), reads=[key],
               writes=["tap_" + name], dma=True)


def tap(P, A, name, key, ap_fn):
    if DEBUG_ON:
        ap = ap_fn()
        tk = "tapt_" + name
        t = A.sb(tk, [ap.shape[0], int(np.prod(ap.shape[1:]))], ap.dtype)
        P.emit("vector", lambda e: e.tensor_copy(t[:], ap), reads=[key], writes=[tk])
        d = P.nc.dram_tensor("tap_" + name, list(t[:].shape), ap.dtype, kind="ExternalOutput").ap()
        P.emit("gpsimd", lambda e: e.dma_start(out=d, in_=t[:]), reads=[tk],
               writes=["tap_" + name], dma=True)

D_MODEL = 1024
SEQ = 4096
N_CORES = 8
NDM = D_MODEL // 128


class Prog:
    def __init__(self, nc):
        self.nc = nc
        self.ops = []
        self.lastw = {}
        self.readers = {}
        self.eng_last = {}

    def emit(self, eng, fn, reads=(), writes=(), sreads=(), dma=False, tag=""):
        i = len(self.ops)
        deps = set()
        sdep = set()
        for k in sreads:
            j = self.lastw.get(k)
            if j is not None:
                deps.add(j)
                sdep.add(j)
        for k in reads:
            j = self.lastw.get(k)
            if j is not None:
                deps.add(j)
        for k in writes:
            j = self.lastw.get(k)
            if j is not None:
                deps.add(j)
            deps |= self.readers.get(k, set())
        pruned = set()
        for j in deps:
            oj = self.ops[j]
            if (not PARANOID and oj["eng"] == eng and not oj["dma"] and not dma
                    and j not in sdep and not oj["small"]):
                continue
            pruned.add(j)
        small = any(SIZES.get(k, 1 << 30) <= 16 for k in writes)
        self.ops.append(dict(eng=eng, fn=fn, dma=dma, deps=pruned, tag=tag, small=small))
        self.eng_last[eng] = i
        for k in reads:
            self.readers.setdefault(k, set()).add(i)
        for k in sreads:
            self.readers.setdefault(k, set()).add(i)
        for k in writes:
            self.lastw[k] = i
            self.readers[k] = set()
        return i

    def barrier_read(self, eng, keys, tag="final"):
        self.emit(eng, lambda e: e.nop(), reads=tuple(keys), tag=tag)

    def fence(self):
        lasts = dict(self.eng_last)
        for eng in ENGINES:
            deps = {j for e2, j in lasts.items() if e2 != eng}
            i = len(self.ops)
            self.ops.append(dict(eng=eng, fn=lambda e: e.nop(), dma=False,
                                 deps=deps, tag="fence", small=False))
            self.eng_last[eng] = i

    def materialize(self, block, sems):
        needed = set()
        for op in self.ops:
            needed |= op["deps"]
        sig_at = {}
        counters = {k: 0 for k in sems}
        for i, op in enumerate(self.ops):
            if op["dma"] or i in needed:
                k = (op["eng"], "d" if op["dma"] else "c")
                counters[k] += 16 if op["dma"] else 1
                sig_at[i] = (k, counters[k])
        prewaits = [None] * len(self.ops)
        bound = {}
        for i, op in enumerate(self.ops):
            if not op["dma"]:
                continue
            k = (op["eng"], "d")
            b = bound.get(k)
            if b is not None:
                prewaits[i] = b
            if i in needed:
                bound[k] = sig_at[i][1]
        waits = []
        marks = {eng: {} for eng in ENGINES}
        idxs = {eng: [] for eng in ENGINES}
        for i, op in enumerate(self.ops):
            w = {}
            for j in op["deps"]:
                k, v = sig_at[j]
                if w.get(k, -1) < v:
                    w[k] = v
            eng = op["eng"]
            if prewaits[i] is not None:
                kq = (eng, "d")
                if w.get(kq, -1) < prewaits[i]:
                    w[kq] = prewaits[i]
            ww = []
            for k, v in w.items():
                if marks[eng].get(k, -1) >= v:
                    continue
                marks[eng][k] = v
                ww.append((k, v))
            waits.append(ww)
            idxs[eng].append(i)

        def mk(eng):
            def body(e):
                for i in idxs[eng]:
                    for (k, v) in waits[i]:
                        e.wait_ge(sems[k], v)
                    inst = self.ops[i]["fn"](e)
                    if i in sig_at:
                        k, _ = sig_at[i]
                        inst.then_inc(sems[k], 16 if k[1] == "d" else 1)
            return body

        for eng in ENGINES:
            getattr(block, eng)(mk(eng))
        return counters


class Alloc:
    def __init__(self, nc, stack):
        self.nc = nc
        self.stack = stack
        self.bytes = 0
        self.t = {}
        self.cms = {}
        self.rorder = []

    def sb(self, name, shape, dtype, side=None):
        assert name not in self.t, name
        SIZES[name] = int(np.prod(shape[1:]))
        self.colbytes = getattr(self, "colbytes", {})
        self.colbytes[name] = (int(np.prod(shape[1:])) * mybir.dt.size(dtype), side)
        self.bytes += 128 * int(np.prod(shape[1:])) * mybir.dt.size(dtype)
        self.uid = getattr(self, "uid", 0) + 1
        bname = f"{name}_u{self.uid}"
        cm = self.nc.sbuf_tensor(bname, shape, dtype, side=side) if side else self.nc.sbuf_tensor(bname, shape, dtype)
        h = cm.__enter__()
        self.t[name] = h
        self.cms[name] = cm
        if side == "right":
            self.rorder.append(name)
        return h

    def mark(self):
        return len(self.rorder)

    def free_left(self, *names):
        for name in reversed(names):
            self.cms.pop(name).__exit__(None, None, None)
            del self.t[name]

    def free_to(self, mark):
        while len(self.rorder) > mark:
            name = self.rorder.pop()
            self.cms.pop(name).__exit__(None, None, None)
            del self.t[name]

    def ps(self, name, shape, dtype):
        h = self.stack.enter_context(self.nc.psum_tensor(name, shape, dtype))
        self.t[name] = h
        return h


def build_common(P, A):
    g = {}
    g["ident_bf"] = A.sb("ident_bf", [128, 128], BF16)
    g["ident_f"] = A.sb("ident_f", [128, 128], F32)
    g["ones_col_f"] = A.sb("ones_col_f", [128, 1], F32)
    g["ones_col_bf"] = A.sb("ones_col_bf", [128, 1], BF16)
    g["ones_row_f"] = A.sb("ones_row_f", [1, 128], F32)
    g["ones_row_bf"] = A.sb("ones_row_bf", [1, 128], BF16)
    g["iota_i"] = A.sb("iota_i", [128, 128], I32)
    g["iota_f"] = A.sb("iota_f", [128, 128], F32)
    g["pid_f"] = A.sb("pid_f", [128, 1], F32)
    ii, pf, io = g["iota_i"], g["pid_f"], g["iota_f"]
    P.emit("gpsimd", lambda e: e.iota(ii[:], pattern=[[1, 128]], base=0, channel_multiplier=0),
           writes=["iota_i"])
    P.emit("gpsimd", lambda e: e.iota(pf[:], pattern=[[0, 1]], base=0, channel_multiplier=1,
                                      allow_small_or_imprecise_dtypes=True), writes=["pid_f"])
    P.emit("vector", lambda e: e.tensor_copy(io[:], ii[:]), reads=["iota_i"], writes=["iota_f"])
    ibf, iff = g["ident_bf"], g["ident_f"]
    P.emit("vector", lambda e: e.tensor_scalar(ibf[:], io[:], pf[:], None, op0=ALU.is_equal),
           reads=["iota_f"], sreads=["pid_f"], writes=["ident_bf"])
    P.emit("vector", lambda e: e.tensor_scalar(iff[:], io[:], pf[:], None, op0=ALU.is_equal),
           reads=["iota_f"], sreads=["pid_f"], writes=["ident_f"])
    g["ones_all_f"] = A.sb("ones_all_f", [128, 128], F32)
    A.sb("ln_sqs0", [128, 512], F32)
    A.sb("ln_sqs1", [128, 512], F32)
    A.sb("lnrow_m", [1, 512], F32)
    A.sb("lnrow_v", [1, 512], F32)
    A.sb("lnrow_r", [1, 512], F32)
    A.sb("ln_t", [128, 512], F32)
    for n in ("ones_col_f", "ones_col_bf", "ones_row_f", "ones_row_bf", "ones_all_f"):
        h = g[n]
        P.emit("vector", lambda e, h=h: e.memset(h[:], 1.0), writes=[n])
    return g


def load_weight_bf16(P, A, name, ap, shape, eng="sync", stage=None, side=None):
    if stage is None:
        f = A.sb(name + "_f", shape, F32)
        fkey = name + "_f"
    else:
        f = A.t[stage]
        fkey = stage
    b = A.sb(name, shape, BF16, side=side)
    stw = int(f[:].shape[1])
    for c0 in range(0, shape[1], stw):
        c1 = min(c0 + stw, shape[1])
        P.emit(eng, lambda e, c0=c0, c1=c1: e.dma_start(out=f[0:shape[0], 0:c1 - c0], in_=ap[:, c0:c1]),
               writes=[fkey], dma=True)
        P.emit("scalar", lambda e, c0=c0, c1=c1: e.activation(out=b[:, c0:c1], in_=f[0:shape[0], 0:c1 - c0], func=AF.Copy),
               reads=[fkey], writes=[name])
    return b


def ln_chunk(P, A, g, xk, tc, dst_names, phS="ph3", phR="ph4", rev=False,
             dst_key_suffix="", width=512):
    if "ln_sqs0" not in A.t:
        A.sb("ln_sqs0", [128, 512], F32)
        A.sb("ln_sqs1", [128, 512], F32)
        A.sb("lnrow_m", [1, 512], F32)
        A.sb("lnrow_v", [1, 512], F32)
        A.sb("lnrow_r", [1, 512], F32)
        A.sb("ln_t", [128, 512], F32)
    s = slice(tc * width, (tc + 1) * width)
    pS, pR = A.t[phS], A.t[phR]
    m, v, r = A.t["lnrow_m"], A.t["lnrow_v"], A.t["lnrow_r"]
    oc = g["ones_col_f"]
    orf = g["ones_row_f"]
    for i in range(NDM):
        sqn = f"ln_sqs{i % 2}"
        sq = A.t[sqn]
        xi = A.t[xk[i]]
        P.emit("scalar", lambda e, sq=sq, xi=xi, s=s: e.activation(out=sq[0:128, 0:width], in_=xi[:, s], func=AF.Square),
               reads=[xk[i]], writes=[sqn])
        P.emit("tensor", lambda e, xi=xi, s=s, st=(i == 0): e.matmul(
            pS[0:1, 0:width], oc[:], xi[:, s], start=st, stop=(i == NDM - 1)),
            reads=[xk[i], "ones_col_f"], writes=[phS])
        P.emit("tensor", lambda e, sq=sq, st=(i == 0), sp=(i == NDM - 1): e.matmul(
            pR[0:1, 0:width], oc[:], sq[0:128, 0:width], start=st, stop=sp),
            reads=[sqn, "ones_col_f"], writes=[phR])
    P.emit("vector", lambda e: e.tensor_scalar(m[0:1, 0:width], pS[0:1, 0:width], 1.0 / D_MODEL, None, op0=ALU.mult),
           reads=[phS], writes=["lnrow_m"])
    P.emit("vector", lambda e: e.tensor_tensor(v[0:1, 0:width], m[0:1, 0:width], m[0:1, 0:width], op=ALU.mult),
           reads=["lnrow_m"], writes=["lnrow_v"])
    P.emit("vector", lambda e: e.scalar_tensor_tensor(
        v[0:1, 0:width], pR[0:1, 0:width], 1.0 / D_MODEL, v[0:1, 0:width], op0=ALU.mult, op1=ALU.subtract),
        reads=[phR, "lnrow_v"], writes=["lnrow_v"])
    P.emit("vector", lambda e: e.tensor_scalar(v[0:1, 0:width], v[0:1, 0:width], 1e-5, None, op0=ALU.add),
           reads=["lnrow_v"], writes=["lnrow_v"])
    P.emit("scalar", lambda e: e.activation(out=v[0:1, 0:width], in_=v[0:1, 0:width], func=AF.Ln),
           reads=["lnrow_v"], writes=["lnrow_v"])
    P.emit("vector", lambda e: e.tensor_scalar(v[0:1, 0:width], v[0:1, 0:width], -0.5, None, op0=ALU.mult),
           reads=["lnrow_v"], writes=["lnrow_v"])
    P.emit("scalar", lambda e: e.activation(out=r[0:1, 0:width], in_=v[0:1, 0:width], func=AF.Exp),
           reads=["lnrow_v"], writes=["lnrow_r"])
    P.emit("tensor", lambda e: e.matmul(pS[:, 0:width], orf[0:1, :], m[0:1, 0:width], start=True, stop=True),
           reads=["lnrow_m", "ones_row_f"], writes=[phS])
    P.emit("tensor", lambda e: e.matmul(pR[:, 0:width], orf[0:1, :], r[0:1, 0:width], start=True, stop=True),
           reads=["lnrow_r", "ones_row_f"], writes=[phR])
    t = A.t["ln_t"]
    for i in range(NDM):
        xi = A.t[xk[i]]
        d = A.t[dst_names[i]]
        dk = dst_names[i] + dst_key_suffix
        P.emit("vector", lambda e, xi=xi, s=s: e.tensor_tensor(t[:, 0:width], xi[:, s], pS[:, 0:width], op=ALU.subtract),
               reads=[xk[i], phS], writes=["ln_t"])
        if rev:
            P.emit("vector", lambda e, d=d: e.tensor_tensor(d[:, width - 1::-1], t[:, 0:width], pR[:, 0:width], op=ALU.mult),
                   reads=["ln_t", phR], writes=[dk])
        else:
            P.emit("vector", lambda e, d=d: e.tensor_tensor(d[:, 0:width], t[:, 0:width], pR[:, 0:width], op=ALU.mult),
                   reads=["ln_t", phR], writes=[dk])


def sl_l1ffn(P, A, g, W, xk, first_call=True):
    CLAMP = float(np.log(1.0 / 0.01))
    sfx = "a" if first_call else "b"
    mkF = A.mark()
    if True:
        if "wstage" not in A.t:
            A.sb("wstage", [128, 1024], F32, side="right")
        for i in range(NDM):
            load_weight_bf16(P, A, f"l1WT{i}", W["l1_WT"][i * 128:(i + 1) * 128, :], [128, 1024], stage="wstage", side="right")
            load_weight_bf16(P, A, f"l1WpT{i}", W["l1_WpT"][i * 128:(i + 1) * 128, :], [128, 256], stage="wstage", side="right")
            load_weight_bf16(P, A, f"l1WdT{i}", W["l1_WdT"][i * 128:(i + 1) * 128, :], [128, 15], stage="wstage", side="right")
        load_weight_bf16(P, A, "l1WuT", W["l1_WuT"][:, :], [15, 1024], stage="wstage", side="right")
        bp = A.sb("l1bp", [128, 2], F32, side="right")
        P.emit("sync", lambda e: e.dma_start(out=bp[:], in_=W["l1_bp"][:, :]), writes=["l1bp"], dma=True)
        simf = A.sb("l1simf", [128, 8], F32, side="right")
        P.emit("sync", lambda e: e.dma_start(out=simf[:], in_=W["l1_sim"][:, :]), writes=["l1simf"], dma=True)
        tempf = A.sb("l1temp", [1, 1], F32, side="right")
        P.emit("sync", lambda e: e.dma_start(out=tempf[:], in_=W["l1_temp"][:, :]), writes=["l1temp"], dma=True)
        load_weight_bf16(P, A, "l1ea", W["l1_ea"][:, :], [4, 15], stage="wstage", side="right")
        A.sb("l1_xd", [15, 512], F32, side="right")
        A.sb("l1_xds", [15, 512], BF16, side="right")
        A.sb("l1_proj", [128, 1024], F32, side="right")
        A.sb("l1_projn", [128, 1024], BF16, side="right")
        A.sb("l1_psq", [128, 512], F32, side="right")
        A.sb("l1_rstd", [1, 512], F32, side="right")
        A.sb("l1_logit", [4, 512], F32, side="right")
        A.sb("l1_tokg", [128, 4], F32, side="right")
        A.sb("l1_retT", [4, 512], BF16, side="right")
        A.sb("l1_s4", [1, 1], F32, side="right")
        A.sb("l1_zero4", [128, 4], F32, side="right")
        A.sb("l1_nm", [128, 1], F32, side="right")
        A.sb("l1_sum", [128, 1], F32, side="right")
        A.sb("l1_rec", [128, 1], F32, side="right")
        A.sb("l1_mx2", [128, 1], F32, side="right")
        A.sb("l1_cum", [128, 4], F32, side="right")
        A.sb("l1_msk", [128, 4], F32, side="right")
        A.sb("l1_lw", [128, 4], F32, side="right")
        A.sb("l1_rettm", [128, 4], BF16, side="right")
        A.sb("l1_nrm", [1, 8], F32, side="right")
        z4 = A.t["l1_zero4"]
        P.emit("vector", lambda e: e.memset(z4[:], 0.0), writes=["l1_zero4"])
        for i in range(NDM):
            A.sb(f"l1xc{i}", [128, 512], BF16, side="right")

    simn = A.sb(f"l1simn{sfx}", [128, 8], BF16, side="right")
    ssq = A.t["l1_psq"]
    simf = A.t["l1simf"]
    tempf = A.t["l1temp"]
    s4 = A.t["l1_s4"]
    ph0, ph1, ph2, ph3 = A.t["ph0"], A.t["ph1"], A.t["ph2"], A.t["ph3"]
    phbf = A.t["phbf"]
    ocf = g["ones_col_f"]
    orf = g["ones_row_f"]
    nrm = A.t["l1_nrm"]

    P.emit("vector", lambda e: e.tensor_scalar(s4[:], tempf[:], CLAMP, None, op0=ALU.min),
           reads=["l1temp"], writes=["l1_s4"])
    P.emit("scalar", lambda e: e.activation(out=s4[:], in_=s4[:], func=AF.Exp),
           reads=["l1_s4"], writes=["l1_s4"])
    P.emit("scalar", lambda e: e.activation(out=ssq[:, 0:8], in_=simf[:], func=AF.Square),
           reads=["l1simf"], writes=["l1_psq"])
    P.emit("tensor", lambda e: e.matmul(ph2[0:1, 0:8], ocf[:], ssq[:, 0:8], start=True, stop=True),
           reads=["l1_psq", "ones_col_f"], writes=["ph2"])
    P.emit("vector", lambda e: e.tensor_copy(nrm[0:1, 0:8], ph2[0:1, 0:8]),
           reads=["ph2"], writes=["l1_nrm"])
    P.emit("vector", lambda e: e.tensor_tensor(nrm[0:1, 0:4], nrm[0:1, 0:4], nrm[0:1, 4:8], op=ALU.add),
           reads=["l1_nrm"], writes=["l1_nrm"])
    P.emit("vector", lambda e: e.tensor_scalar(nrm[0:1, 0:4], nrm[0:1, 0:4], 1e-24, None, op0=ALU.max),
           reads=["l1_nrm"], writes=["l1_nrm"])
    P.emit("scalar", lambda e: e.activation(out=nrm[0:1, 0:4], in_=nrm[0:1, 0:4], func=AF.Ln),
           reads=["l1_nrm"], writes=["l1_nrm"])
    P.emit("vector", lambda e: e.tensor_scalar(nrm[0:1, 0:4], nrm[0:1, 0:4], -0.5, None, op0=ALU.mult),
           reads=["l1_nrm"], writes=["l1_nrm"])
    P.emit("scalar", lambda e: e.activation(out=nrm[0:1, 0:4], in_=nrm[0:1, 0:4], func=AF.Exp),
           reads=["l1_nrm"], writes=["l1_nrm"])
    P.emit("vector", lambda e: e.tensor_copy(nrm[0:1, 4:8], nrm[0:1, 0:4]),
           reads=["l1_nrm"], writes=["l1_nrm"])
    P.emit("vector", lambda e: e.tensor_scalar(nrm[0:1, :], nrm[0:1, :], s4[:], None, op0=ALU.mult),
           reads=["l1_nrm"], sreads=["l1_s4"], writes=["l1_nrm"])
    P.emit("tensor", lambda e: e.matmul(ph2[:, 8:16], orf[0:1, :], nrm[0:1, :], start=True, stop=True),
           reads=["l1_nrm", "ones_row_f"], writes=["ph2"])
    P.emit("vector", lambda e: e.tensor_tensor(simn[:], simf[:], ph2[:, 8:16], op=ALU.mult),
           reads=["l1simf", "ph2"], writes=[f"l1simn{sfx}"])

    xlnc = [f"l1xc{i}" for i in range(NDM)]
    xd = A.t["l1_xd"]
    xds = A.t["l1_xds"]
    proj = A.t["l1_proj"]
    projn = A.t["l1_projn"]
    psq = A.t["l1_psq"]
    rstd = A.t["l1_rstd"]
    logit = A.t["l1_logit"]
    retT = A.t["l1_retT"]
    ea = A.t["l1ea"]
    bp = A.t["l1bp"]
    z4 = A.t["l1_zero4"]

    for tc in range(8):
        ln_chunk(P, A, g, xk, tc, xlnc)
        for i in range(NDM):
            wd = A.t[f"l1WdT{i}"]
            xc = A.t[xlnc[i]]
            P.emit("tensor", lambda e, wd=wd, xc=xc, st=(i == 0), sp=(i == NDM - 1):
                   e.matmul(ph1[0:15, 0:512], wd[:], xc[:], start=st, stop=sp),
                   reads=[f"l1WdT{i}", xlnc[i]], writes=["ph1"])
        P.emit("vector", lambda e: e.tensor_copy(xd[:], ph1[0:15, 0:512]),
               reads=["ph1"], writes=["l1_xd"])
        for pc in range(2):
            for i in range(NDM):
                wp = A.t[f"l1WpT{i}"]
                xc = A.t[xlnc[i]]
                P.emit("tensor", lambda e, wp=wp, xc=xc, pc=pc, st=(i == 0), sp=(i == NDM - 1):
                       e.matmul(ph2[:, 0:512], wp[:, pc * 128:(pc + 1) * 128], xc[:], start=st, stop=sp),
                       reads=[f"l1WpT{i}", xlnc[i]], writes=["ph2"])
            P.emit("vector", lambda e, pc=pc: e.tensor_scalar(
                proj[:, pc * 512:(pc + 1) * 512], ph2[:, 0:512], bp[:, pc:pc + 1], None, op0=ALU.add),
                reads=["ph2"], sreads=["l1bp"], writes=["l1_proj"])
        for pc in range(2):
            P.emit("scalar", lambda e, pc=pc: e.activation(out=psq[:], in_=proj[:, pc * 512:(pc + 1) * 512], func=AF.Square),
                   reads=["l1_proj"], writes=["l1_psq"])
            P.emit("tensor", lambda e, st=(pc == 0), sp=(pc == 1):
                   e.matmul(ph2[0:1, 0:512], ocf[:], psq[:], start=st, stop=sp),
                   reads=["l1_psq", "ones_col_f"], writes=["ph2"])
        P.emit("vector", lambda e: e.tensor_scalar(rstd[:], ph2[0:1, 0:512], 1e-24, None, op0=ALU.max),
               reads=["ph2"], writes=["l1_rstd"])
        P.emit("scalar", lambda e: e.activation(out=rstd[:], in_=rstd[:], func=AF.Ln),
               reads=["l1_rstd"], writes=["l1_rstd"])
        P.emit("vector", lambda e: e.tensor_scalar(rstd[:], rstd[:], -0.5, None, op0=ALU.mult),
               reads=["l1_rstd"], writes=["l1_rstd"])
        P.emit("scalar", lambda e: e.activation(out=rstd[:], in_=rstd[:], func=AF.Exp),
               reads=["l1_rstd"], writes=["l1_rstd"])
        P.emit("tensor", lambda e: e.matmul(ph2[:, 0:512], orf[0:1, :], rstd[0:1, :], start=True, stop=True),
               reads=["l1_rstd", "ones_row_f"], writes=["ph2"])
        for pc in range(2):
            P.emit("vector", lambda e, pc=pc: e.tensor_tensor(
                projn[:, pc * 512:(pc + 1) * 512], proj[:, pc * 512:(pc + 1) * 512], ph2[:, 0:512], op=ALU.mult),
                reads=["l1_proj", "ph2"], writes=["l1_projn"])
        for pc in range(2):
            P.emit("tensor", lambda e, pc=pc, st=(pc == 0), sp=(pc == 1):
                   e.matmul(ph2[0:4, 0:512], simn[:, pc * 4:(pc + 1) * 4], projn[:, pc * 512:(pc + 1) * 512], start=st, stop=sp),
                   reads=[f"l1simn{sfx}", "l1_projn"], writes=["ph2"])
        P.emit("vector", lambda e: e.tensor_copy(logit[:], ph2[0:4, 0:512]),
               reads=["ph2"], writes=["l1_logit"])
        tokg = A.t["l1_tokg"]
        nm, sm, rec, mx2 = A.t["l1_nm"], A.t["l1_sum"], A.t["l1_rec"], A.t["l1_mx2"]
        cum, msk, lw, rtm = A.t["l1_cum"], A.t["l1_msk"], A.t["l1_lw"], A.t["l1_rettm"]
        for sb in range(4):
            cs = slice(sb * 128, (sb + 1) * 128)
            P.emit("tensor", lambda e, cs=cs: e.transpose(ph2[:, 16:20], logit[:, cs], g["ident_f"][0:4, 0:4]),
                   reads=["l1_logit", "ident_f"], writes=["ph2"])
            P.emit("vector", lambda e: e.tensor_reduce(nm[:], ph2[:, 16:20], axis=AX.X, op=ALU.max, negate=True),
                   reads=["ph2"], writes=["l1_nm"])
            P.emit("scalar", lambda e: e.activation(out=tokg[:], in_=ph2[:, 16:20], func=AF.Exp, bias=nm[:]),
                   reads=["ph2"], sreads=["l1_nm"], writes=["l1_tokg"])
            P.emit("vector", lambda e: e.tensor_reduce(sm[:], tokg[:], axis=AX.X, op=ALU.add),
                   reads=["l1_tokg"], writes=["l1_sum"])
            P.emit("vector", lambda e: e.reciprocal(rec[:], sm[:]), reads=["l1_sum"], writes=["l1_rec"])
            P.emit("vector", lambda e: e.tensor_scalar(lw[:], tokg[:], rec[:], None, op0=ALU.mult),
                   reads=["l1_tokg"], sreads=["l1_rec"], writes=["l1_lw"])
            P.emit("vector", lambda e: e.tensor_reduce(mx2[:], lw[:], axis=AX.X, op=ALU.max),
                   reads=["l1_lw"], writes=["l1_mx2"])
            P.emit("vector", lambda e: e.tensor_scalar(msk[:], lw[:], mx2[:], None, op0=ALU.is_equal),
                   reads=["l1_lw"], sreads=["l1_mx2"], writes=["l1_msk"])
            P.emit("vector", lambda e: e.tensor_tensor_scan(cum[:], z4[:], msk[:], 0.0, op0=ALU.add, op1=ALU.add),
                   reads=["l1_zero4", "l1_msk"], writes=["l1_cum"])
            P.emit("vector", lambda e: e.tensor_scalar(cum[:], cum[:], 1.0, None, op0=ALU.is_equal),
                   reads=["l1_cum"], writes=["l1_cum"])
            P.emit("vector", lambda e: e.tensor_tensor(cum[:], cum[:], msk[:], op=ALU.mult),
                   reads=["l1_cum", "l1_msk"], writes=["l1_cum"])
            P.emit("vector", lambda e: e.tensor_tensor(rtm[:], cum[:], lw[:], op=ALU.mult),
                   reads=["l1_cum", "l1_lw"], writes=["l1_rettm"])
            if tc == 0 and sb == 0 and DEBUG_ON:
                for nm2, src2 in [("lw", lw), ("msk", msk), ("cum", cum), ("tokg", tokg), ("rtm", rtm)]:
                    cp = A.sb("dbg_" + nm2, [128, 4], F32)
                    P.emit("vector", lambda e, cp=cp, src2=src2: e.tensor_copy(cp[:], src2[:]),
                           reads=["l1_" + nm2 if nm2 != "rtm" else "l1_rettm"], writes=["dbg_" + nm2])
                    tap(P, A, nm2, "dbg_" + nm2, lambda cp=cp: cp[:])
            P.emit("tensor", lambda e: e.transpose(phbf[0:4, 0:128], rtm[:], g["ident_bf"][:]),
                   reads=["l1_rettm", "ident_bf"], writes=["phbf"])
            P.emit("vector", lambda e, cs=cs: e.tensor_copy(retT[0:4, cs], phbf[0:4, 0:128]),
                   reads=["phbf"], writes=["l1_retT"])
        P.emit("tensor", lambda e: e.matmul(ph1[0:15, 0:512], ea[:], retT[:], start=True, stop=True),
               reads=["l1ea", "l1_retT"], writes=["ph1"])
        P.emit("vector", lambda e: e.tensor_tensor(xds[:], ph1[0:15, 0:512], xd[:], op=ALU.mult),
               reads=["ph1", "l1_xd"], writes=["l1_xds"])
        if tc == 0:
            tap(P, A, "xd", "l1_xd", lambda: A.t["l1_xd"][:])
            tap(P, A, "xds", "l1_xds", lambda: A.t["l1_xds"][:])
            tap(P, A, "retT", "l1_retT", lambda: A.t["l1_retT"][:])
            tap(P, A, "logit", "l1_logit", lambda: A.t["l1_logit"][:])
            tap(P, A, "proj", "l1_proj", lambda: A.t["l1_proj"][:])
            tap(P, A, "projn", "l1_projn", lambda: A.t["l1_projn"][:])
            tap(P, A, "simn", f"l1simn{sfx}", lambda: A.t[f"l1simn{sfx}"][:])
            tap(P, A, "xln0", "l1xc0", lambda: A.t["l1xc0"][:])
        for oc in range(NDM):
            for i in range(NDM):
                wt = A.t[f"l1WT{i}"]
                xc = A.t[xlnc[i]]
                P.emit("tensor", lambda e, wt=wt, xc=xc, oc=oc, st=(i == 0):
                       e.matmul(ph0[:, 0:512], wt[:, oc * 128:(oc + 1) * 128], xc[:], start=st, stop=False),
                       reads=[f"l1WT{i}", xlnc[i]], writes=["ph0"])
            wu = A.t["l1WuT"]
            P.emit("tensor", lambda e, oc=oc: e.matmul(ph0[:, 0:512], wu[:, oc * 128:(oc + 1) * 128], xds[:], start=False, stop=True),
                   reads=["l1WuT", "l1_xds"], writes=["ph0"])
            xo = A.t[xk[oc]]
            s = slice(tc * 512, (tc + 1) * 512)
            P.emit("vector", lambda e, xo=xo, s=s: e.scalar_tensor_tensor(
                xo[:, s], ph0[:, 0:512], 0.5, xo[:, s], op0=ALU.mult, op1=ALU.add),
                reads=["ph0", xk[oc]], writes=[xk[oc]])
    P.fence()
    A.free_to(mkF)


def prep_l1ffn_weights(inputs):
    W = {}
    W["l1_WT"] = np.ascontiguousarray(inputs["L1_ffn_W"].T)
    W["l1_WpT"] = np.ascontiguousarray(inputs["L1_ffn_Wp"].T)
    W["l1_bp"] = np.ascontiguousarray(np.asarray(inputs["L1_ffn_bp"]).reshape(2, 128).T)
    W["l1_sim"] = np.ascontiguousarray(
        np.asarray(inputs["L1_ffn_sim"]).reshape(2, 128, 4).transpose(1, 0, 2).reshape(128, 8))
    W["l1_temp"] = np.ascontiguousarray(np.asarray(inputs["L1_ffn_temp"]).reshape(1, 1))
    wd = np.concatenate([np.asarray(inputs[f"L1_ffn_Wd{i}"]) for i in range(4)], axis=0)
    W["l1_WdT"] = np.ascontiguousarray(wd.T)
    wu = np.concatenate([np.asarray(inputs[f"L1_ffn_Wu{i}"]) for i in range(4)], axis=1)
    W["l1_WuT"] = np.ascontiguousarray(wu.T)
    ea = np.zeros((4, 15), np.float32)
    for e_i, (a, b) in enumerate([(0, 1), (1, 3), (3, 7), (7, 15)]):
        ea[e_i, a:b] = 1.0
    W["l1_ea"] = ea
    return W


L1FFN_W_SHAPES = dict(
    l1_WT=[1024, 1024], l1_WpT=[1024, 256], l1_bp=[128, 2], l1_sim=[128, 8],
    l1_temp=[1, 1], l1_WdT=[1024, 15], l1_WuT=[15, 1024], l1_ea=[4, 15])


def finalize(P, nc, A):
    with nc.Block() as block, ExitStack() as semstack:
        sems = {}
        for eng in ENGINES:
            for kk in ("c", "d"):
                sems[(eng, kk)] = semstack.enter_context(nc.semaphore(f"s_{eng}_{kk}"))
        P.materialize(block, sems)


def build_sl4_standalone():
    nc = bass.Bass("TRN2", target_bir_lowering=False)
    xin = nc.dram_tensor("xT", [D_MODEL, SEQ], F32, kind="ExternalInput").ap()
    xout = nc.dram_tensor("yT", [D_MODEL, SEQ], F32, kind="ExternalOutput").ap()
    Wd = {k: nc.dram_tensor(k, sh, F32, kind="ExternalInput").ap()
          for k, sh in L1FFN_W_SHAPES.items()}
    with ExitStack() as stack:
        A = Alloc(nc, stack)
        P = Prog(nc)
        g = build_common(P, A)
        xk = [f"x{i}" for i in range(NDM)]
        for i in range(NDM):
            x = A.sb(xk[i], [128, SEQ], F32)
            P.emit("sync", lambda e, x=x, i=i: e.dma_start(out=x[:], in_=xin[i * 128:(i + 1) * 128, :]),
                   writes=[xk[i]], dma=True)
        for pn in ["ph0", "ph1", "ph2", "ph3", "ph4"]:
            A.ps(pn, [128, 512], F32)
        A.ps("phbf", [128, 1024], BF16)
        sl_l1ffn(P, A, g, Wd, xk, first_call=True)
        for nm, tk in DEBUG_TAPS:
            t = A.t[tk]
            dt = nc.dram_tensor("tap_" + nm, list(t[:].shape), t[:].dtype, kind="ExternalOutput").ap()
            P.emit("gpsimd", lambda e, dt=dt, t=t: e.dma_start(out=dt, in_=t[:]),
                   reads=[tk], writes=["tapdma_" + nm], dma=True)
        for i in range(NDM):
            x = A.t[xk[i]]
            P.emit("gpsimd", lambda e, x=x, i=i: e.dma_start(out=xout[i * 128:(i + 1) * 128, :], in_=x[:]),
                   reads=[xk[i]], writes=[f"outdma{i}"], dma=True)
        P.barrier_read("gpsimd", [f"outdma{i}" for i in range(NDM)])
        finalize(P, nc, A)
        print(f"SBUF bytes: {A.bytes/1e6:.2f}MB, ops: {len(P.ops)}", flush=True)
    return nc




def cast_to_hbm_blocked(P, A, name, src_ap, rows, cols, dst_ap, stage):
    st_ = A.t[stage]
    bsn = stage + "_bf"
    bs = A.t[bsn]
    for rc in range(rows // 128):
        for cc in range(cols // 1024):
            P.emit("sync", lambda e, rc=rc, cc=cc: e.dma_start(
                out=st_[:, 0:1024], in_=src_ap[rc * 128:(rc + 1) * 128, cc * 1024:(cc + 1) * 1024]),
                writes=[stage], dma=True)
            P.emit("scalar", lambda e: e.activation(out=bs[:, 0:1024], in_=st_[:, 0:1024], func=AF.Copy),
                   reads=[stage], writes=[bsn])
            for j in range(8):
                mc = cc * 8 + j
                P.emit("gpsimd", lambda e, mc=mc, rc=rc, j=j: e.dma_start(
                    out=dst_ap[mc, rc, :, :], in_=bs[:, j * 128:(j + 1) * 128]),
                    reads=[bsn], writes=[name + "_blk"], dma=True)


def sl_l0attn(P, A, g, W, xk):
    nc = P.nc
    NH = 16
    ph0, ph1, ph2, ph3, ph4, ph5 = [A.t[f"ph{i}"] for i in range(6)]
    ocf, orf = g["ones_col_f"], g["ones_row_f"]
    kind = "ExternalOutput" if DEBUG_HBM else "Internal"
    hq = nc.dram_tensor("hbm_q", [1024, SEQ], BF16, kind=kind).ap()
    hk = nc.dram_tensor("hbm_k", [1024, SEQ], BF16, kind=kind).ap()
    hv = nc.dram_tensor("hbm_v", [32, 16, 128, 64], BF16, kind=kind).ap()
    havn = nc.dram_tensor("hbm_avn", [1024, SEQ], BF16, kind=kind).ap()
    if DEBUG_HBM:
        hpodbg = nc.dram_tensor("hbm_podbg", [8, 8, 128, 512], BF16, kind="ExternalOutput").ap()
        hpsdbg = nc.dram_tensor("hbm_psdbg", [8, 8, 128, 512], F32, kind="ExternalOutput").ap()
    else:
        hpodbg = hpsdbg = None
    hwq = nc.dram_tensor("hbm_wq", [24, 8, 128, 128], BF16).ap()
    hwqs = nc.dram_tensor("hbm_wqs", [16, 8, 128, 128], BF16).ap()
    if "wstage1k" not in A.t:
        A.sb("wstage1k", [128, 1024], F32)
        A.sb("wstage1k_bf", [128, 1024], BF16)
    pa = A.sb("att_pa_sb", [1, 32], F32, side="right")
    P.emit("sync", lambda e: e.dma_start(out=pa[:], in_=W["att_pa"][:, :]), writes=["att_pa"], dma=True)
    P.emit("vector", lambda e: e.tensor_scalar(pa[0:1, 0:32], pa[0:1, 0:32], 0.01, None, op0=ALU.max),
           reads=["att_pa"], writes=["att_pa"])
    P.emit("vector", lambda e: e.tensor_scalar(pa[0:1, 0:16], pa[0:1, 0:16], -1.0, None, op0=ALU.mult),
           reads=["att_pa"], writes=["att_pa"])
    fr = A.sb("att_fr_sb", [128, 1], F32, side="right")
    P.emit("sync", lambda e: e.dma_start(out=fr[:], in_=W["att_freqs"][:, :]), writes=["att_fr"], dma=True)
    sg = A.sb("att_sg_sb", [128, 1], F32, side="right")
    P.emit("sync", lambda e: e.dma_start(out=sg[:], in_=W["att_sgn"][:, :]), writes=["att_sg"], dma=True)
    cast_to_hbm_blocked(P, A, "awq", W["att_qkvT"], 1024, 3072, hwq, "wstage1k")
    cast_to_hbm_blocked(P, A, "awqs", W["att_qkvTs"], 1024, 2048, hwqs, "wstage1k")
    mkA = A.mark()
    crep = A.sb("att_crep", [128, SEQ], BF16, side="right")
    srep = A.sb("att_srep", [128, SEQ], BF16, side="right")
    angi = A.sb("att_angi", [128, 512], I32, side="right")
    tmpa = A.sb("att_tmpa", [128, 512], F32, side="right")
    tmpb = A.sb("att_tmpb", [128, 512], F32, side="right")
    for i in range(NDM):
        A.sb(f"att_xc{i}", [128, 512], BF16, side="right")
    for pp in range(2):
        for i in range(NDM):
            A.sb(f"att_wc{pp}_{i}", [128, 128], BF16, side="right")
            A.sb(f"att_ws{pp}_{i}", [128, 128], BF16, side="right")
    for pp in range(2):
        A.sb(f"att_vwc{pp}", [128, 512], BF16, side="right")
    ro1 = A.sb("att_ro1", [128, 512], F32, side="right")
    ro2 = A.sb("att_ro2", [128, 512], F32, side="right")
    qkb = A.sb("att_qk", [128, 512], BF16, side="right")
    vsb = A.sb("att_vsb", [128, 512], BF16, side="right")
    for tc in range(8):
        for (dst, dk, phase) in ((srep, "att_srep", 0.0), (crep, "att_crep", math.pi / 2)):
            P.emit("gpsimd", lambda e, tc=tc: e.iota(angi[:], pattern=[[1, 512]], base=tc * 512, channel_multiplier=0),
                   writes=["att_angi"])
            P.emit("vector", lambda e: e.tensor_copy(tmpa[:], angi[:]), reads=["att_angi"], writes=["att_tmpa"])
            P.emit("vector", lambda e: e.tensor_scalar(tmpa[:], tmpa[:], fr[:], None, op0=ALU.mult),
                   reads=["att_tmpa"], sreads=["att_fr"], writes=["att_tmpa"])
            P.emit("vector", lambda e, phase=phase: e.tensor_scalar(
                tmpa[:], tmpa[:], 1.0 / (2 * math.pi), phase / (2 * math.pi), op0=ALU.mult, op1=ALU.add),
                reads=["att_tmpa"], writes=["att_tmpa"])
            P.emit("vector", lambda e: e.tensor_copy(angi[:], tmpa[:]), reads=["att_tmpa"], writes=["att_angi"])
            P.emit("vector", lambda e: e.tensor_copy(tmpb[:], angi[:]), reads=["att_angi"], writes=["att_tmpb"])
            P.emit("vector", lambda e: e.tensor_tensor(tmpa[:], tmpa[:], tmpb[:], op=ALU.subtract),
                   reads=["att_tmpa", "att_tmpb"], writes=["att_tmpa"])
            P.emit("vector", lambda e: e.tensor_scalar(tmpb[:], tmpa[:], 0.0, None, op0=ALU.is_lt),
                   reads=["att_tmpa"], writes=["att_tmpb"])
            P.emit("vector", lambda e: e.tensor_tensor(tmpa[:], tmpa[:], tmpb[:], op=ALU.add),
                   reads=["att_tmpa", "att_tmpb"], writes=["att_tmpa"])
            P.emit("vector", lambda e: e.tensor_scalar(tmpa[:], tmpa[:], 2 * math.pi, None, op0=ALU.mult),
                   reads=["att_tmpa"], writes=["att_tmpa"])
            P.emit("vector", lambda e: e.tensor_scalar(tmpb[:], tmpa[:], math.pi, None, op0=ALU.is_gt),
                   reads=["att_tmpa"], writes=["att_tmpb"])
            P.emit("vector", lambda e: e.scalar_tensor_tensor(tmpa[:], tmpb[:], -2 * math.pi, tmpa[:],
                                                              op0=ALU.mult, op1=ALU.add),
                   reads=["att_tmpb", "att_tmpa"], writes=["att_tmpa"])
            P.emit("scalar", lambda e, dst=dst, tc=tc: e.activation(out=dst[:, tc * 512:(tc + 1) * 512], in_=tmpa[:], func=AF.Sin),
                   reads=["att_tmpa"], writes=[dk])
    P.emit("vector", lambda e: e.tensor_scalar(srep[:], srep[:], sg[:], None, op0=ALU.mult),
           reads=["att_srep"], sreads=["att_sg"], writes=["att_srep"])
    xlnc = [f"att_xc{i}" for i in range(NDM)]
    for tc in range(8):
        s = slice(tc * 512, (tc + 1) * 512)
        ln_chunk(P, A, g, xk, tc, xlnc)
        for mc in range(16):
            pp = mc % 2
            for i in range(NDM):
                wc = A.t[f"att_wc{pp}_{i}"]
                ws = A.t[f"att_ws{pp}_{i}"]
                P.emit("sync", lambda e, wc=wc, mc=mc, i=i: e.dma_start(out=wc[:], in_=hwq[mc, i, :, :]),
                       reads=["awq_blk"], writes=[f"att_wc{pp}_{i}"], dma=True)
                P.emit("gpsimd", lambda e, ws=ws, mc=mc, i=i: e.dma_start(out=ws[:], in_=hwqs[mc, i, :, :]),
                       reads=["awqs_blk"], writes=[f"att_ws{pp}_{i}"], dma=True)
            for i in range(NDM):
                wc = A.t[f"att_wc{pp}_{i}"]
                ws = A.t[f"att_ws{pp}_{i}"]
                xc = A.t[xlnc[i]]
                P.emit("tensor", lambda e, wc=wc, xc=xc, st=(i == 0), sp=(i == NDM - 1):
                       e.matmul(ph0[:, 0:512], wc[:], xc[:], start=st, stop=sp),
                       reads=[f"att_wc{pp}_{i}", xlnc[i]], writes=["ph0"])
                P.emit("tensor", lambda e, ws=ws, xc=xc, st=(i == 0), sp=(i == NDM - 1):
                       e.matmul(ph1[:, 0:512], ws[:], xc[:], start=st, stop=sp),
                       reads=[f"att_ws{pp}_{i}", xlnc[i]], writes=["ph1"])
            scl = 0.125 if mc < 8 else 1.0
            P.emit("vector", lambda e, s=s, scl=scl: e.scalar_tensor_tensor(
                ro1[:], ph0[:, 0:512], scl, crep[:, s], op0=ALU.mult, op1=ALU.mult),
                reads=["ph0", "att_crep"], writes=["att_ro1"])
            P.emit("vector", lambda e, s=s, scl=scl: e.scalar_tensor_tensor(
                ro2[:], ph1[:, 0:512], scl, srep[:, s], op0=ALU.mult, op1=ALU.mult),
                reads=["ph1", "att_srep"], writes=["att_ro2"])
            P.emit("vector", lambda e: e.tensor_tensor(qkb[:], ro1[:], ro2[:], op=ALU.add),
                   reads=["att_ro1", "att_ro2"], writes=["att_qk"])
            dst = hq if mc < 8 else hk
            row = (mc % 8) * 128
            P.emit("gpsimd", lambda e, dst=dst, row=row, s=s: e.dma_start(
                out=dst[row:row + 128, s], in_=qkb[:]),
                reads=["att_qk"], writes=["hbm_qk"], dma=True)
        for vc in range(2):
            for i in range(NDM):
                pp = i % 2
                vw = A.t[f"att_vwc{pp}"]
                P.emit("sync", lambda e, vw=vw, vc=vc, i=i: e.dma_start(
                    out=vw[:].rearrange("p (j c) -> p j c", j=4),
                    in_=hwq[16 + vc * 4:16 + vc * 4 + 4, i, :, :].rearrange("j p c -> p j c")),
                    reads=["awq_blk"], writes=[f"att_vwc{pp}"], dma=True)
                for sub in range(4):
                    xc = A.t[xlnc[i]]
                    P.emit("tensor", lambda e, vw=vw, xc=xc, sub=sub, st=(i == 0), sp=(i == NDM - 1):
                           e.matmul(ph2[:, 0:512] if sub == 0 else (ph4[:, 0:512] if sub == 1 else (ph5[:, 0:512] if sub == 2 else ph3[:, 0:512])),
                                    xc[:, sub * 128:(sub + 1) * 128], vw[:], start=st, stop=sp),
                           reads=[f"att_vwc{pp}", xlnc[i]],
                           writes=[["ph2", "ph4", "ph5", "ph3"][sub]])
            for sub in range(4):
                ktc = tc * 4 + sub
                pht_v = [ph2, ph4, ph5, ph3][sub]
                pk = ["ph2", "ph4", "ph5", "ph3"][sub]
                P.emit("scalar", lambda e, pht_v=pht_v: e.activation(out=vsb[:], in_=pht_v[:, 0:512], func=AF.Copy),
                       reads=[pk], writes=["att_vsb"])
                for j in range(8):
                    h = vc * 8 + j
                    P.emit("sync" if j % 2 == 0 else "gpsimd",
                           lambda e, ktc=ktc, h=h, j=j: e.dma_start(
                               out=hv[ktc, h, :, :], in_=vsb[:, j * 64:(j + 1) * 64]),
                           reads=["att_vsb"], writes=["hbm_v"], dma=True)
    P.fence()
    A.free_to(mkA)
    mkB = A.mark()
    kt = A.sb("att_kt", [64, SEQ], BF16, side="right")
    qt = A.sb("att_qt", [64, 512], BF16, side="right")
    dh = A.sb("att_dh", [128, 2016], I32, side="right")
    et = A.sb("att_et", [128, 2016], F32, side="right")
    ekb = A.sb("att_ekb", [128, 8064], BF16, side="right")
    at0 = A.sb("att_at0", [128, 512], BF16, side="right")
    at1 = A.sb("att_at1", [128, 512], BF16, side="right")
    at2 = A.sb("att_at2", [128, 512], BF16, side="right")
    at3 = A.sb("att_at3", [128, 512], BF16, side="right")
    v0 = A.sb("att_v0", [128, 65], BF16, side="right")
    v1 = A.sb("att_v1", [128, 65], BF16, side="right")
    rec = A.sb("att_rec", [128, 512], F32, side="right")
    avs = A.sb("att_avs", [64, 512], F32, side="right")
    avn0 = A.sb("att_avn0", [64, 512], BF16, side="right")
    avn1 = A.sb("att_avn1", [64, 512], BF16, side="right")
    P.emit("vector", lambda e: e.memset(v0[:, 64:65], 1.0), writes=["att_v0"])
    P.emit("vector", lambda e: e.memset(v1[:, 64:65], 1.0), writes=["att_v1"])
    ats = [at0, at1]
    ats2 = [at2, at3]
    vts = [v0, v1]
    avns = [avn0, avn1]
    for h in range(NH):
        P.emit("sync", lambda e, h=h: e.dma_start(out=kt[:], in_=hk[h * 64:(h + 1) * 64, :]),
               reads=["hbm_qk"], writes=["att_kt"], dma=True)
        P.emit("tensor", lambda e, h=h: e.matmul(ph3[:, 0:1], orf[0:1, :], pa[0:1, 16 + h:17 + h], start=True, stop=True),
               reads=["att_pa", "ones_row_f"], writes=["ph3"])
        P.emit("tensor", lambda e, h=h: e.matmul(ph3[:, 1:2], orf[0:1, :], pa[0:1, h:h + 1], start=True, stop=True),
               reads=["att_pa", "ones_row_f"], writes=["ph3"])
        for qq in range(4):
            base = -31 * 128 + qq * 2016
            P.emit("gpsimd", lambda e, base=base: e.iota(dh[:], pattern=[[1, 2016]], base=base, channel_multiplier=-1),
                   writes=["att_dh"])
            P.emit("scalar", lambda e: e.activation(out=et[:], in_=dh[:], func=AF.Abs),
                   reads=["att_dh"], writes=["att_et"])
            P.emit("vector", lambda e: e.tensor_scalar(et[:], et[:], ph3[:, 0:1], None, op0=ALU.mult),
                   reads=["att_et"], sreads=["ph3"], writes=["att_et"])
            P.emit("scalar", lambda e: e.activation(out=et[:], in_=et[:], func=AF.Ln, bias=1.0),
                   reads=["att_et"], writes=["att_et"])
            P.emit("vector", lambda e: e.tensor_scalar(et[:], et[:], ph3[:, 1:2], None, op0=ALU.mult),
                   reads=["att_et"], sreads=["ph3"], writes=["att_et"])
            P.emit("scalar", lambda e, qq=qq: e.activation(out=ekb[:, qq * 2016:(qq + 1) * 2016], in_=et[:], func=AF.Exp),
                   reads=["att_et"], writes=["att_ekb"])
        for qc in range(8):
            P.emit("sync", lambda e, h=h, qc=qc: e.dma_start(out=qt[:], in_=hq[h * 64:(h + 1) * 64, qc * 512:(qc + 1) * 512]),
                   reads=["hbm_qk"], writes=["att_qt"], dma=True)
            for ktc in range(32):
                pb = "ph0" if ktc % 2 == 0 else "ph1"
                pbt = ph0 if ktc % 2 == 0 else ph1
                ab = ktc % 2
                P.emit("tensor", lambda e, ktc=ktc, pbt=pbt: e.matmul(
                    pbt[:, 0:512], kt[:, ktc * 128:(ktc + 1) * 128], qt[:], start=True, stop=True),
                    reads=["att_kt", "att_qt"], writes=[pb])
                P.emit("scalar", lambda e, pbt=pbt, ab=ab: e.activation(out=ats[ab][:], in_=pbt[:, 0:512], func=AF.Exp),
                       reads=[pb], writes=[f"att_at{ab}"])
                off = (4 * qc - ktc + 31) * 128
                P.emit("vector", lambda e, ab=ab, off=off: e.tensor_tensor(
                    ats2[ab][:], ats[ab][:], ekb[:, off:off + 512], op=ALU.mult),
                    reads=[f"att_at{ab}", "att_ekb"], writes=[f"att_at2{ab}" if False else f"att_at{ab + 2}"])
                P.emit("sync" if ab == 0 else "gpsimd",
                       lambda e, ktc=ktc, h=h, ab=ab: e.dma_start(out=vts[ab][:, 0:64], in_=hv[ktc, h, :, :]),
                       reads=["hbm_v"], writes=[f"att_v{ab}"], dma=True)
                P.emit("tensor", lambda e, ab=ab, st=(ktc == 0), sp=(ktc == 31): e.matmul(
                    ph2[0:65, 0:512], vts[ab][:], ats2[ab][:], start=st, stop=sp),
                    reads=[f"att_v{ab}", f"att_at{ab + 2}"], writes=["ph2"])
            P.emit("vector", lambda e: e.reciprocal(rec[64:65, :], ph2[64:65, 0:512]),
                   reads=["ph2"], writes=["att_rec"])
            P.emit("tensor", lambda e: e.matmul(ph3[0:64, 0:512], g["ones_all_f"][64:65, 0:64], rec[64:65, :], start=True, stop=True),
                   reads=["att_rec", "ones_all_f"], writes=["ph3"])
            P.emit("vector", lambda e: e.tensor_copy(avs[:], ph2[0:64, 0:512]),
                   reads=["ph2"], writes=["att_avs"])
            ab2 = qc % 2
            avn = avns[ab2]
            P.emit("vector", lambda e, avn=avn: e.tensor_tensor(
                avn[:, :], avs[:], ph3[0:64, 0:512], op=ALU.mult),
                reads=["att_avs", "ph3"], writes=[f"att_avn{ab2}"])
            P.emit("gpsimd", lambda e, avn=avn, h=h, qc=qc: e.dma_start(
                out=havn[h * 64:(h + 1) * 64, qc * 512:(qc + 1) * 512], in_=avn[:, :]),
                reads=[f"att_avn{ab2}"], writes=["havn"], dma=True)
    P.fence()
    A.free_to(mkB)
    mkC = A.mark()
    for i in range(NDM):
        load_weight_bf16(P, A, f"att_wo{i}", W["att_outT"][i * 128:(i + 1) * 128, :], [128, 1024],
                         stage="wstage1k", side="right")
    for i in range(8):
        A.sb(f"att_po{i}", [128, 512], BF16, side="right")
    pos = [A.t[f"att_po{i}"] for i in range(8)]
    for qc in range(8):
        for hc in range(NDM):
            P.emit("sync", lambda e, hc=hc, qc=qc: e.dma_start(
                out=pos[hc][:, :], in_=havn[hc * 128:(hc + 1) * 128, qc * 512:(qc + 1) * 512]),
                reads=["havn"], writes=[f"att_po{hc}"], dma=True)
            if DEBUG_HBM:
                P.emit("gpsimd", lambda e, hc=hc, qc=qc: e.dma_start(
                    out=hpodbg[qc, hc, :, :], in_=pos[hc][:, :]),
                    reads=[f"att_po{hc}"], writes=["podbg"], dma=True)
        for dmc in range(NDM):
            pb = "ph4" if dmc % 2 == 0 else "ph5"
            pht_ = ph4 if dmc % 2 == 0 else ph5
            for hc in range(NDM):
                wo = A.t[f"att_wo{hc}"]
                P.emit("tensor", lambda e, wo=wo, hc=hc, dmc=dmc, pht_=pht_, st=(hc == 0), sp=(hc == 7):
                       e.matmul(pht_[:, 0:512], wo[:, dmc * 128:(dmc + 1) * 128], pos[hc][:, :], start=st, stop=sp),
                       reads=[f"att_wo{hc}", f"att_po{hc}"], writes=[pb])
            if DEBUG_HBM:
                dbge = A.t["att_rec2"] if "att_rec2" in A.t else A.sb("att_rec2", [128, 512], F32, side="right")
                P.emit("scalar", lambda e, pht_=pht_: e.activation(out=dbge[:], in_=pht_[:, 0:512], func=AF.Copy),
                       reads=[pb], writes=["att_rec2"])
                P.emit("gpsimd", lambda e, qc=qc, dmc=dmc, dbge=dbge: e.dma_start(
                    out=hpsdbg[qc, dmc, :, :], in_=dbge[:]),
                    reads=["att_rec2"], writes=["psdbg"], dma=True)
            xo = A.t[xk[dmc]]
            s = slice(qc * 512, (qc + 1) * 512)
            if DEBUG_HBM:
                P.emit("vector", lambda e, xo=xo, s=s, dbge=dbge: e.tensor_tensor(xo[:, s], xo[:, s], dbge[:], op=ALU.add),
                       reads=["att_rec2", xk[dmc]], writes=[xk[dmc]])
            else:
                P.emit("vector", lambda e, xo=xo, s=s, pht_=pht_: e.tensor_tensor(xo[:, s], xo[:, s], pht_[:, 0:512], op=ALU.add),
                       reads=[pb, xk[dmc]], writes=[xk[dmc]])
    P.fence()
    A.free_to(mkC)


def prep_l0attn_weights(inputs):
    W = {}
    qkv = np.asarray(inputs["L0_attn_qkv_w"])
    W["att_qkvT"] = np.ascontiguousarray(qkv.T)
    perm = np.arange(2048)
    for h in range(32):
        base = h * 64
        perm[base:base + 64] = np.concatenate([np.arange(base + 32, base + 64), np.arange(base, base + 32)])
    W["att_qkvTs"] = np.ascontiguousarray(qkv.T[:, :2048][:, perm])
    W["att_outT"] = np.ascontiguousarray(np.asarray(inputs["L0_attn_out_w"]).T)
    W["att_freqs"] = np.ascontiguousarray(np.tile(np.asarray(inputs["L0_attn_rope_freqs"]), 4).reshape(128, 1))
    sgn = np.tile(np.concatenate([-np.ones(32, np.float32), np.ones(32, np.float32)]), 2).reshape(128, 1)
    W["att_sgn"] = np.ascontiguousarray(sgn)
    W["att_pa"] = np.ascontiguousarray(np.concatenate(
        [np.asarray(inputs["L0_attn_bias_p"]).reshape(16), np.asarray(inputs["L0_attn_bias_a"]).reshape(16)]).reshape(1, 32))
    return W


ATTN_W_SHAPES = dict(
    att_qkvT=[1024, 3072], att_qkvTs=[1024, 2048], att_outT=[1024, 1024],
    att_freqs=[128, 1], att_sgn=[128, 1], att_pa=[1, 32])


def build_sl2_standalone():
    nc = bass.Bass("TRN2", target_bir_lowering=False)
    xin = nc.dram_tensor("xT", [D_MODEL, SEQ], F32, kind="ExternalInput").ap()
    xout = nc.dram_tensor("yT", [D_MODEL, SEQ], F32, kind="ExternalOutput").ap()
    Wd = {k: nc.dram_tensor(k, sh, F32, kind="ExternalInput").ap()
          for k, sh in ATTN_W_SHAPES.items()}
    with ExitStack() as stack:
        A = Alloc(nc, stack)
        P = Prog(nc)
        g = build_common(P, A)
        xkeys = [f"x{i}" for i in range(NDM)]
        for i in range(NDM):
            x = A.sb(xkeys[i], [128, SEQ], F32)
            P.emit("sync", lambda e, x=x, i=i: e.dma_start(out=x[:], in_=xin[i * 128:(i + 1) * 128, :]),
                   writes=[xkeys[i]], dma=True)
        for pn in ["ph0", "ph1", "ph2", "ph3", "ph4", "ph5"]:
            A.ps(pn, [128, 512], F32)
        sl_l0attn(P, A, g, Wd, xkeys)
        for i in range(NDM):
            x = A.t[xkeys[i]]
            P.emit("gpsimd", lambda e, x=x, i=i: e.dma_start(out=xout[i * 128:(i + 1) * 128, :], in_=x[:]),
                   reads=[xkeys[i]], writes=[f"outdma{i}"], dma=True)
        P.barrier_read("gpsimd", [f"outdma{i}" for i in range(NDM)])
        finalize(P, nc, A)
        print(f"SBUF bytes: {A.bytes/1e6:.2f}MB, ops: {len(P.ops)}", flush=True)
    return nc




def sl_l0ffn(P, A, g, W, xk, first_call=True):
    nc = P.nc
    E = 8
    CAP = 512
    NSC = CAP // 128
    ph = [A.t[f"ph{i}"] for i in range(6)]
    phbf = A.t["phbf"]
    ocf, orf = g["ones_col_f"], g["ones_row_f"]
    sfx = "a" if first_call else "b"
    if first_call:
        A.t["_moe_dram"] = dict(
            w1b=nc.dram_tensor("hbm_w1b", [8, 32, 8, 128, 128], BF16).ap(),
            w2b=nc.dram_tensor("hbm_w2b", [8, 8, 32, 128, 128], BF16).ap(),
            pos_d=nc.dram_tensor("moe_pos_d", [8, SEQ], F32).ap(),
            lnst=nc.dram_tensor("hbm_lnst_a", [SEQ, 1024], BF16).ap(),
            lnst_b=nc.dram_tensor("hbm_lnst_b", [SEQ, 1024], BF16).ap(),
            cin=nc.dram_tensor("moe_cin_a", [8, SEQ], F32).ap(),
            cout=nc.dram_tensor("moe_cout_a", [64, SEQ], F32).ap(),
            cin_b=nc.dram_tensor("moe_cin_b", [8, SEQ], F32).ap(),
            cout_b=nc.dram_tensor("moe_cout_b", [64, SEQ], F32).ap(),
        )
        md0 = A.t["_moe_dram"]
        md0["b2bf"] = nc.dram_tensor("hbm_b2bf", [8, 1024], BF16).ap()
        P.emit("sync", lambda e: e.dma_start(out=A.t["wstage1k"][0:8, 0:1024], in_=W["moe_b2"][:, :]),
               writes=["wstage1k"], dma=True)
        P.emit("scalar", lambda e: e.activation(out=A.t["wstage1k_bf"][0:8, 0:1024], in_=A.t["wstage1k"][0:8, 0:1024], func=AF.Copy),
               reads=["wstage1k"], writes=["wstage1k_bf"])
        P.emit("gpsimd", lambda e: e.dma_start(out=md0["b2bf"][:, :], in_=A.t["wstage1k_bf"][0:8, 0:1024]),
               reads=["wstage1k_bf"], writes=["hbm_b2bf"], dma=True)
        for e_ in range(E):
            cast_to_hbm_blocked(P, A, f"mw1_{e_}", W["moe_w1"][e_, :, :], 1024, 4096,
                                A.t["_moe_dram"]["w1b"][e_], "wstage1k")
            cast_to_hbm_blocked(P, A, f"mw2_{e_}", W["moe_w2"][e_, :, :], 4096, 1024,
                                A.t["_moe_dram"]["w2b"][e_], "wstage1k")
        rw = A.sb("moe_rw_s", [128, 64], F32, side="right")
        P.emit("sync", lambda e: e.dma_start(out=rw[:], in_=W["moe_rw"][:, :]), writes=["moe_rw_s"], dma=True)
        rb = A.sb("moe_rb_s", [1, 8], F32, side="right")
        P.emit("sync", lambda e: e.dma_start(out=rb[:], in_=W["moe_rb"][:, :]), writes=["moe_rb_s"], dma=True)
        cwrow = A.sb("moe_cwr", [1, 8], F32, side="right")
        rbcol = A.sb("moe_rbc", [8, 1], F32, side="right")
        ph2_ = A.t["ph2"]
        for i in range(NDM):
            P.emit("tensor", lambda e, i=i, st=(i == 0), sp=(i == NDM - 1): e.matmul(
                ph2_[0:8, 40:41], rw[:, i * 8:(i + 1) * 8], A.t["ones_col_f"][:], start=st, stop=sp),
                reads=["moe_rw_s", "ones_col_f"], writes=["ph2"])
        cwc = A.sb("moe_cwc", [8, 1], F32, side="right")
        P.emit("vector", lambda e: e.tensor_scalar(cwc[:], ph2_[0:8, 40:41], -1.0, None, op0=ALU.mult),
               reads=["ph2"], writes=["moe_cwc"])
        P.emit("tensor", lambda e: e.transpose(ph2_[0:1, 44:52], cwc[:], A.t["ident_f"][0:8, 0:8]),
               reads=["moe_cwc", "ident_f"], writes=["ph2"])
        P.emit("vector", lambda e: e.tensor_copy(cwrow[:], ph2_[0:1, 44:52]), reads=["ph2"], writes=["moe_cwr"])
        P.emit("tensor", lambda e: e.transpose(ph2_[0:8, 52:53], rb[:], A.t["ident_f"][0:1, 0:1]),
               reads=["moe_rb_s", "ident_f"], writes=["ph2"])
        P.emit("vector", lambda e: e.tensor_copy(rbcol[:], ph2_[0:8, 52:53]), reads=["ph2"], writes=["moe_rbc"])
        lmat = A.sb("moe_L_s", [128, 128], F32, side="right")
        P.emit("sync", lambda e: e.dma_start(out=lmat[:], in_=W["moe_L"][:, :]), writes=["moe_L_s"], dma=True)
        smat = A.sb("moe_S_s", [128, 8], F32, side="right")
        P.emit("sync", lambda e: e.dma_start(out=smat[:], in_=W["moe_S"][:, :]), writes=["moe_S_s"], dma=True)
        semat = A.sb("moe_SE_s", [8, 128], F32, side="right")
        P.emit("sync", lambda e: e.dma_start(out=semat[:], in_=W["moe_SE"][:, :]), writes=["moe_SE_s"], dma=True)
        ohot = A.sb("moe_oh", [128, 8], F32, side="right")
        P.emit("sync", lambda e: e.dma_start(out=ohot[:], in_=W["moe_ownhot"][:, :]), writes=["moe_oh"], dma=True)

        iotok = A.sb("moe_iotok", [128, 32], F32, side="right")
        P.emit("gpsimd", lambda e: e.iota(iotok[:], pattern=[[1, 32]], base=0, channel_multiplier=32,
                                          allow_small_or_imprecise_dtypes=True), writes=["moe_iotok"])
        iorow = A.sb("moe_iorow", [128, 512], F32, side="right")
        P.emit("gpsimd", lambda e: e.iota(iorow[:], pattern=[[1, 512]], base=0, channel_multiplier=0,
                                          allow_small_or_imprecise_dtypes=True), writes=["moe_iorow"])

    md = A.t["_moe_dram"]
    lnst = md["lnst"] if first_call else md["lnst_b"]
    cin = md["cin"] if first_call else md["cin_b"]
    cout = md["cout"] if first_call else md["cout_b"]
    rw, rb, lmat, smat, semat, ohot = [A.t[k] for k in
                                       ["moe_rw_s", "moe_rb_s", "moe_L_s", "moe_S_s", "moe_SE_s", "moe_oh"]]
    iotok, iorow = A.t["moe_iotok"], A.t["moe_iorow"]

    mkTOP = A.mark()
    idxu = A.sb("moe_idxu", [128, 4 * E], U32, side="right")
    gval = A.sb("moe_gval", [128, 4 * E], F32, side="right")
    idxf = A.sb("moe_idxf", [128, 4 * E], F32, side="right")
    mkS = A.mark()
    sct = A.sb("moe_sct", [8, SEQ], F32, side="right")
    mkA = A.mark()
    for i in range(NDM):
        A.sb(f"moe_xc{i}", [128, 512], BF16, side="right")
    xlnc = [f"moe_xc{i}" for i in range(NDM)]
    lgsb = A.sb("moe_lg", [8, 512], F32, side="right")
    tokp = A.sb("moe_tokp", [128, 8], F32, side="right")
    tnm = A.sb("moe_tnm", [128, 1], F32, side="right")
    tsum = A.sb("moe_tsum", [128, 1], F32, side="right")
    trec = A.sb("moe_trec", [128, 1], F32, side="right")
    tsc = A.sb("moe_tsc", [128, 8], F32, side="right")
    tmaj = A.sb("moe_tmaj", [128, 1024], BF16, side="right")
    cwrow, rbcol = A.t["moe_cwr"], A.t["moe_rbc"]
    for tc in range(8):
        s_ = slice(tc * 512, (tc + 1) * 512)
        ln_chunk(P, A, g, xk, tc, xlnc)
        for i in range(NDM):
            xi_ = A.t[xk[i]]
            P.emit("tensor", lambda e, xi_=xi_, i=i, s_=s_, st=(i == 0):
                   e.matmul(ph[2][0:8, 0:512], rw[:, i * 8:(i + 1) * 8], xi_[:, s_], start=st, stop=False),
                   reads=["moe_rw_s", xk[i]], writes=["ph2"])
        P.emit("tensor", lambda e: e.matmul(ph[2][0:8, 0:512], cwrow[0:1, :], A.t["lnrow_m"][0:1, 0:512], start=False, stop=True),
               reads=["moe_cwr", "lnrow_m"], writes=["ph2"])
        P.emit("tensor", lambda e: e.matmul(ph[5][0:8, 0:512], g["ones_row_f"][0:1, 0:8], A.t["lnrow_r"][0:1, 0:512], start=True, stop=True),
               reads=["ones_row_f", "lnrow_r"], writes=["ph5"])
        P.emit("vector", lambda e: e.tensor_copy(lgsb[:], ph[2][0:8, 0:512]), reads=["ph2"], writes=["moe_lg"])
        P.emit("vector", lambda e: e.tensor_tensor(lgsb[:], lgsb[:], ph[5][0:8, 0:512], op=ALU.mult),
               reads=["moe_lg", "ph5"], writes=["moe_lg"])
        P.emit("vector", lambda e: e.tensor_scalar(lgsb[:], lgsb[:], rbcol[:], None, op0=ALU.add),
               reads=["moe_lg"], sreads=["moe_rbc"], writes=["moe_lg"])
        for sb_ in range(4):
            cs = slice(sb_ * 128, (sb_ + 1) * 128)
            P.emit("tensor", lambda e, cs=cs: e.transpose(ph[2][:, 16:24], lgsb[:, cs], g["ident_f"][0:8, 0:8]),
                   reads=["moe_lg", "ident_f"], writes=["ph2"])
            P.emit("vector", lambda e: e.tensor_copy(tokp[:], ph[2][:, 16:24]),
                   reads=["ph2"], writes=["moe_tokp"])
            P.emit("vector", lambda e: e.tensor_reduce(tnm[:], tokp[:], axis=AX.X, op=ALU.max, negate=True),
                   reads=["moe_tokp"], writes=["moe_tnm"])
            P.emit("scalar", lambda e: e.activation(out=tokp[:], in_=tokp[:], func=AF.Exp, bias=tnm[:]),
                   reads=["moe_tokp"], sreads=["moe_tnm"], writes=["moe_tokp"])
            P.emit("vector", lambda e: e.tensor_reduce(tsum[:], tokp[:], axis=AX.X, op=ALU.add),
                   reads=["moe_tokp"], writes=["moe_tsum"])
            P.emit("vector", lambda e: e.reciprocal(trec[:], tsum[:]), reads=["moe_tsum"], writes=["moe_trec"])
            P.emit("vector", lambda e: e.tensor_scalar(tsc[:], tokp[:], trec[:], None, op0=ALU.mult),
                   reads=["moe_tokp"], sreads=["moe_trec"], writes=["moe_tsc"])
            P.emit("tensor", lambda e: e.transpose(ph[5][0:8, 0:128], tsc[:], g["ident_f"][:]),
                   reads=["moe_tsc", "ident_f"], writes=["ph5"])
            P.emit("vector", lambda e, tc=tc, cs=cs: e.tensor_copy(sct[0:8, tc * 512 + cs.start:tc * 512 + cs.stop], ph[5][0:8, 0:128]),
                   reads=["ph5"], writes=["moe_sct"])
        for sub in range(4):
            for i in range(NDM):
                xc = A.t[xlnc[i]]
                P.emit("tensor", lambda e, xc=xc, sub=sub: e.transpose(phbf[:, 0:128], xc[:, sub * 128:(sub + 1) * 128], g["ident_bf"][:]),
                       reads=[xlnc[i], "ident_bf"], writes=["phbf"])
                P.emit("vector", lambda e, i=i: e.tensor_copy(tmaj[:, i * 128:(i + 1) * 128], phbf[:, 0:128]),
                       reads=["phbf"], writes=["moe_tmaj"])
            P.emit("gpsimd", lambda e, tc=tc, sub=sub: e.dma_start(
                out=lnst[(tc * 4 + sub) * 128:(tc * 4 + sub + 1) * 128, :], in_=tmaj[:]),
                reads=["moe_tmaj"], writes=["moe_lnst"], dma=True)
    P.emit("gpsimd", lambda e: e.dma_start(out=cin[:, :], in_=sct[:]),
           reads=["moe_sct"], writes=["moe_cin"], dma=True)
    P.emit("gpsimd", lambda e: e.collective_compute(
        "AllGather", ALU.bypass, replica_groups=[list(range(8))],
        ins=[cin.opt()], outs=[cout.opt()]),
        reads=["moe_cin"], writes=["moe_cout"], tag="collective")
    P.fence()
    A.free_to(mkA)
    mkR = A.mark()
    lo = A.sb("moe_lo", [8, 1], F32, side="right")
    hi = A.sb("moe_hi", [8, 1], F32, side="right")
    mid = A.sb("moe_mid", [8, 1], F32, side="right")
    cge = A.sb("moe_cge", [8, 1], U32, side="right")
    cnt = A.sb("moe_cnt", [128, 1], F32, side="right")
    th128 = A.sb("moe_th128", [128, 1], F32, side="right")
    tneed = A.sb("moe_tneed", [8, 1], F32, side="right")
    rtot = A.sb("moe_rtot", [128, 1], F32, side="right")
    ooff = A.sb("moe_ooff", [8, 1], F32, side="right")
    th8 = A.sb("moe_th8", [8, 1], F32, side="right")
    tn8 = A.sb("moe_tn8", [8, 1], F32, side="right")
    zc = A.sb("moe_zc", [128, 1], F32, side="right")
    mkR1 = A.mark()
    scg = A.sb("moe_scg", [128, 2048], F32, side="right")
    msk = A.sb("moe_msk", [128, 2048], F32, side="right")
    P.emit("vector", lambda e: e.memset(zc[:], 0.0), writes=["moe_zc"])
    for p in range(128):
        e_, gg = p // 16, p % 16
        r_, c0 = 8 * (gg // 2) + e_, (gg % 2) * 2048
        P.emit("sync" if p % 2 == 0 else "gpsimd",
               lambda e, p=p, r_=r_, c0=c0: e.dma_start(out=scg[p:p + 1, :], in_=cout[r_:r_ + 1, c0:c0 + 2048]),
               reads=["moe_cout"], writes=["moe_scg"], dma=True)
    if DEBUG_ON:
        tapd(P, A, "scg2", "moe_scg", lambda: scg[:])
    P.emit("vector", lambda e: e.memset(lo[:], 0.0), writes=["moe_lo"])
    P.emit("vector", lambda e: e.memset(hi[:], 1.0), writes=["moe_hi"])
    P.emit("vector", lambda e: e.memset(mid[:], 0.5), writes=["moe_mid"])
    for it in range(56):
        P.emit("tensor", lambda e: e.matmul(ph[3][:, 0:1], semat[:], mid[:], start=True, stop=True),
               reads=["moe_SE_s", "moe_mid"], writes=["ph3"])
        P.emit("vector", lambda e: e.tensor_scalar(msk[:], scg[:], ph[3][:, 0:1], None, op0=ALU.is_gt,
                                                   op1=ALU.add, accum_out=cnt[:]),
               reads=["moe_scg"], sreads=["ph3"], writes=["moe_msk", "moe_cnt"])
        P.emit("tensor", lambda e: e.matmul(ph[3][0:8, 1:2], smat[:], cnt[:], start=True, stop=True),
               reads=["moe_S_s", "moe_cnt"], writes=["ph3"])
        P.emit("vector", lambda e: e.tensor_scalar(cge[:], ph[3][0:8, 1:2], 2048.0, None, op0=ALU.is_ge),
               reads=["ph3"], writes=["moe_cge"])
        P.emit("vector", lambda e: e.select(lo[:], cge[:], mid[:], lo[:]),
               reads=["moe_cge", "moe_mid", "moe_lo"], writes=["moe_lo"])
        P.emit("vector", lambda e: e.select(hi[:], cge[:], hi[:], mid[:]),
               reads=["moe_cge", "moe_hi", "moe_mid"], writes=["moe_hi"])
        P.emit("vector", lambda e: e.tensor_tensor(mid[:], lo[:], hi[:], op=ALU.add),
               reads=["moe_lo", "moe_hi"], writes=["moe_mid"])
        P.emit("vector", lambda e: e.tensor_scalar(mid[:], mid[:], 0.5, None, op0=ALU.mult),
               reads=["moe_mid"], writes=["moe_mid"])
    P.emit("tensor", lambda e: e.matmul(ph[3][:, 0:1], semat[:], hi[:], start=True, stop=True),
           reads=["moe_SE_s", "moe_hi"], writes=["ph3"])
    P.emit("vector", lambda e: e.tensor_copy(th128[:], ph[3][:, 0:1]), reads=["ph3"], writes=["moe_th128"])
    P.emit("vector", lambda e: e.tensor_scalar(msk[:], scg[:], th128[:], None, op0=ALU.is_gt,
                                               op1=ALU.add, accum_out=cnt[:]),
           reads=["moe_scg"], sreads=["moe_th128"], writes=["moe_msk", "moe_cnt"])
    P.emit("tensor", lambda e: e.matmul(ph[3][0:8, 1:2], smat[:], cnt[:], start=True, stop=True),
           reads=["moe_S_s", "moe_cnt"], writes=["ph3"])
    P.emit("vector", lambda e: e.tensor_scalar(tneed[:], ph[3][0:8, 1:2], -1.0, 2048.0, op0=ALU.mult, op1=ALU.add),
           reads=["ph3"], writes=["moe_tneed"])
    P.emit("vector", lambda e: e.tensor_scalar(msk[:], scg[:], th128[:], None, op0=ALU.is_equal),
           reads=["moe_scg"], sreads=["moe_th128"], writes=["moe_msk"])
    P.emit("vector", lambda e: e.tensor_tensor_scan(scg[:], zc[:].broadcast_to([128, 2048]), msk[:], 0.0,
                                                    op0=ALU.add, op1=ALU.add),
           reads=["moe_zc", "moe_msk"], writes=["moe_scg"])
    P.emit("vector", lambda e: e.tensor_copy(rtot[:], scg[:, 2047:2048]), reads=["moe_scg"], writes=["moe_rtot"])
    P.emit("tensor", lambda e: e.matmul(ph[3][:, 2:3], lmat[:], rtot[:], start=True, stop=True),
           reads=["moe_L_s", "moe_rtot"], writes=["ph3"])
    P.emit("vector", lambda e: e.tensor_copy(rtot[:], ph[3][:, 2:3]), reads=["ph3"], writes=["moe_rtot"])
    P.emit("tensor", lambda e: e.matmul(ph[3][0:8, 3:4], ohot[:], rtot[:], start=True, stop=True),
           reads=["moe_oh", "moe_rtot"], writes=["ph3"])
    P.emit("vector", lambda e: e.tensor_copy(ooff[:], ph[3][0:8, 3:4]), reads=["ph3"], writes=["moe_ooff"])
    P.fence()
    A.free_to(mkR1)
    selA = A.sb("moe_selA", [8, SEQ], F32, side="right")
    selB = A.sb("moe_selB", [8, SEQ], F32, side="right")
    P.emit("vector", lambda e: e.tensor_copy(th8[:], hi[:]), reads=["moe_hi"], writes=["moe_th8"])
    P.emit("vector", lambda e: e.scalar_tensor_tensor(tn8[:], tneed[:], 1.0, ooff[:], op0=ALU.mult, op1=ALU.subtract),
           reads=["moe_tneed", "moe_ooff"], writes=["moe_tn8"])
    P.emit("vector", lambda e: e.tensor_scalar(selB[:], sct[:], th8[:], None, op0=ALU.is_equal),
           reads=["moe_sct"], sreads=["moe_th8"], writes=["moe_selB"])
    P.emit("vector", lambda e: e.tensor_tensor_scan(selA[:], zc[0:8, :].broadcast_to([8, SEQ]), selB[:], 0.0,
                                                    op0=ALU.add, op1=ALU.add),
           reads=["moe_zc", "moe_selB"], writes=["moe_selA"])
    P.emit("vector", lambda e: e.tensor_scalar(selA[:], selA[:], tn8[:], None, op0=ALU.is_le),
           reads=["moe_selA"], sreads=["moe_tn8"], writes=["moe_selA"])
    P.emit("vector", lambda e: e.tensor_tensor(selB[:], selB[:], selA[:], op=ALU.mult),
           reads=["moe_selB", "moe_selA"], writes=["moe_selB"])
    P.emit("vector", lambda e: e.tensor_scalar(selA[:], sct[:], th8[:], None, op0=ALU.is_gt),
           reads=["moe_sct"], sreads=["moe_th8"], writes=["moe_selA"])
    P.emit("vector", lambda e: e.tensor_tensor(selA[:], selA[:], selB[:], op=ALU.add),
           reads=["moe_selA", "moe_selB"], writes=["moe_selA"])
    P.emit("vector", lambda e: e.tensor_tensor_scan(selB[:], zc[0:8, :].broadcast_to([8, SEQ]), selA[:], 0.0,
                                                    op0=ALU.add, op1=ALU.add),
           reads=["moe_zc", "moe_selA"], writes=["moe_selB"])
    P.emit("vector", lambda e: e.tensor_tensor(selB[:], selB[:], selA[:], op=ALU.mult),
           reads=["moe_selB", "moe_selA"], writes=["moe_selB"])
    P.emit("vector", lambda e: e.tensor_scalar(selB[:], selB[:], -1.0, None, op0=ALU.add),
           reads=["moe_selB"], writes=["moe_selB"])
    P.emit("gpsimd", lambda e: e.dma_start(out=md["pos_d"][:, :], in_=selB[:]),
           reads=["moe_selB"], writes=["moe_pos_d"], dma=True)
    if DEBUG_ON:
        tapd(P, A, "selA", "moe_selA", lambda: selA[:])
        tapd(P, A, "sct", "moe_sct", lambda: sct[:])
    if DEBUG_ON:
        dbg = A.sb("moe_dbg", [8, 8], F32)
        for ci, (tt, kk) in enumerate([(hi, "moe_hi"), (tneed, "moe_tneed"), (ooff, "moe_ooff"),
                                       (lo, "moe_lo"), (th8, "moe_th8"), (tn8, "moe_tn8")]):
            P.emit("vector", lambda e, tt=tt, ci=ci: e.tensor_copy(dbg[0:8, ci:ci + 1], tt[:]),
                   reads=[kk], writes=["moe_dbg"])
        tap(P, A, "dbg8", "moe_dbg", lambda: A.t["moe_dbg"][:])
        selcnt = A.sb("moe_selcnt", [8, 1], F32)
        P.emit("vector", lambda e: e.tensor_scalar(selA[:], selA[:], 1.0, None, op0=ALU.mult,
                                                   op1=ALU.add, accum_out=selcnt[:]),
               reads=["moe_selA"], writes=["moe_selA", "moe_selcnt"])
        P.emit("vector", lambda e: e.tensor_copy(dbg[0:8, 6:7], selcnt[:]), reads=["moe_selcnt"], writes=["moe_dbg"])
    P.fence()
    A.free_to(mkR)
    posp = A.sb("moe_posp", [128, 32], F32, side="right")
    gsp = A.sb("moe_gsp", [128, 32], F32, side="right")
    payl = A.sb("moe_payl", [128, 64], F32, side="right")
    ohw = A.sb("moe_ohw", [128, 512], F32, side="right")
    for e_ in range(E):
        P.emit("sync", lambda e, e_=e_: e.dma_start(
            out=posp[:], in_=md["pos_d"][e_, :].rearrange("(p w) -> p w", p=128)),
            reads=["moe_pos_d"], writes=["moe_posp"], dma=True)
        P.emit("gpsimd", lambda e, e_=e_, cin=cin: e.dma_start(
            out=gsp[:], in_=cin[e_, :].rearrange("(p w) -> p w", p=128)),
            reads=["moe_cin"], writes=["moe_gsp"], dma=True)
        P.emit("vector", lambda e: e.tensor_copy(payl[:, 0:64:2], iotok[:]),
               reads=["moe_iotok"], writes=["moe_payl"])
        P.emit("vector", lambda e: e.tensor_copy(payl[:, 1:64:2], gsp[:]),
               reads=["moe_gsp"], writes=["moe_payl"])
        for w in range(32):
            P.emit("vector", lambda e, w=w: e.tensor_scalar(ohw[:], iorow[:], posp[:, w:w + 1], None, op0=ALU.is_equal),
                   reads=["moe_iorow"], sreads=["moe_posp"], writes=["moe_ohw"])
            for sc in range(NSC):
                P.emit("tensor", lambda e, w=w, sc=sc, st=(w == 0 and sc == 0), sp=(w == 31 and sc == NSC - 1): e.matmul(
                    ph[4][:, 2 * sc:2 * sc + 2], ohw[:, sc * 128:(sc + 1) * 128], payl[:, 2 * w:2 * w + 2],
                    start=st, stop=sp),
                    reads=["moe_ohw", "moe_payl"], writes=["ph4"])
        for sc in range(NSC):
            col = e_ * 4 + sc
            P.emit("vector", lambda e, sc=sc, col=col: e.tensor_copy(idxf[:, col:col + 1], ph[4][:, 2 * sc:2 * sc + 1]),
                   reads=["ph4"], writes=["moe_idxf"])
            P.emit("vector", lambda e, sc=sc, col=col: e.tensor_copy(gval[:, col:col + 1], ph[4][:, 2 * sc + 1:2 * sc + 2]),
                   reads=["ph4"], writes=["moe_gval"])
            P.emit("vector", lambda e, col=col: e.tensor_copy(idxu[:, col:col + 1], idxf[:, col:col + 1]),
                   reads=["moe_idxf"], writes=["moe_idxu"])
    if DEBUG_ON:
        tap(P, A, "idxf", "moe_idxf", lambda: A.t["moe_idxf"][:])
        tap(P, A, "gval", "moe_gval", lambda: A.t["moe_gval"][:])
    P.fence()
    A.free_to(mkS)
    mkE = A.mark()
    idxsh = A.sb("moe_idxsh", [128, 1], F32, side="right")
    h1t = [A.sb(f"moe_h1_{f_}", [128, CAP], BF16, side="right") for f_ in range(32)]
    for pp in range(2):
        for i in range(NDM):
            A.sb(f"moe_w1c{pp}_{i}", [128, 128], BF16, side="right")
    for pp in range(2):
        A.sb(f"moe_w2c{pp}", [128, 512], BF16, side="right")
    b1sb = A.sb("moe_b1_s", [128, 32], F32, side="right")
    b2bf = A.sb("moe_b2bf", [1, 1024], BF16, side="right")
    w1b, w2b = md["w1b"], md["w2b"]
    orbf = g["ones_row_bf"]
    for e_ in range(E):
        P.emit("sync", lambda e, e_=e_: e.dma_start(out=b1sb[:], in_=W["moe_b1"][e_, :, :]),
               writes=["moe_b1_s"], dma=True)
        P.emit("sync", lambda e, e_=e_: e.dma_start(out=b2bf[:], in_=md["b2bf"][e_:e_ + 1, :]),
               reads=["hbm_b2bf"], writes=["moe_b2bf"], dma=True)
        mkT = A.mark()
        tokm = A.sb("moe_tokm", [128, 1024], BF16, side="right")
        tksT = [A.sb(f"moe_tk{i}", [128, CAP], BF16, side="right") for i in range(NDM)]
        for sc in range(NSC):
            P.emit("gpsimd", lambda e, e_=e_, sc=sc: e.indirect_dma_start(
                out=tokm[:], out_offset=None, in_=lnst[:, :],
                in_offset=IndirectOffsetOnAxis(ap=idxu[:, e_ * 4 + sc:e_ * 4 + sc + 1], axis=0)),
                reads=["moe_lnst", "moe_idxu"], writes=["moe_tokm"], dma=True)
            for i in range(NDM):
                P.emit("tensor", lambda e, i=i: e.transpose(phbf[:, 0:128], tokm[:, i * 128:(i + 1) * 128], g["ident_bf"][:]),
                       reads=["moe_tokm", "ident_bf"], writes=["phbf"])
                P.emit("vector", lambda e, i=i, sc=sc: e.tensor_copy(tksT[i][:, sc * 128:(sc + 1) * 128], phbf[:, 0:128]),
                       reads=["phbf"], writes=[f"moe_tk{i}"])
        for f_ in range(32):
            pp = f_ % 2
            for i in range(NDM):
                wt = A.t[f"moe_w1c{pp}_{i}"]
                P.emit("sync", lambda e, wt=wt, e_=e_, f_=f_, i=i: e.dma_start(
                    out=wt[:], in_=w1b[e_, f_, i, :, :]),
                    reads=[f"mw1_{e_}_blk"], writes=[f"moe_w1c{pp}_{i}"], dma=True)
            for i in range(NDM):
                wt = A.t[f"moe_w1c{pp}_{i}"]
                P.emit("tensor", lambda e, wt=wt, i=i, st=(i == 0), sp=(i == NDM - 1): e.matmul(
                    ph[0][:, 0:CAP], wt[:], tksT[i][:], start=st, stop=sp),
                    reads=[f"moe_w1c{pp}_{i}", f"moe_tk{i}"], writes=["ph0"])
            P.emit("scalar", lambda e, f_=f_: e.activation(out=h1t[f_][:], in_=ph[0][:, 0:CAP], func=AF.Relu,
                                                           bias=b1sb[:, f_:f_ + 1]),
                   reads=["ph0"], sreads=["moe_b1_s"], writes=[f"moe_h1_{f_}"])
            if e_ == 0 and f_ == 0 and DEBUG_ON:
                tap(P, A, "w1c_07", "moe_w1c0_7", lambda: A.t["moe_w1c0_7"][:])
        if e_ == 0 and DEBUG_ON:
            tap(P, A, "tk7", "moe_tk7", lambda: tksT[7][:])
            tap(P, A, "h1_0", "moe_h1_0", lambda: h1t[0][:])
        P.fence()
        A.free_to(mkT)
        eot = {}
        for sc in range(NSC):
            for dv in range(2):
                eot[(sc, dv)] = A.sb(f"moe_eo{sc}_{dv}", [128, 512], BF16, side="right")
        pg = A.sb("moe_pg", [128, 512 * NSC], BF16, side="right")
        for dv in range(2):
            for f_ in range(32):
                pp = f_ % 2
                w2t = A.t[f"moe_w2c{pp}"]
                P.emit("sync", lambda e, w2t=w2t, e_=e_, f_=f_, dv=dv: e.dma_start(
                    out=w2t[:].rearrange("p (j c) -> p j c", j=4),
                    in_=w2b[e_, dv * 4:dv * 4 + 4, f_, :, :].rearrange("j p c -> p j c")),
                    reads=[f"mw2_{e_}_blk"], writes=[f"moe_w2c{pp}"], dma=True)
                for sc in range(NSC):
                    P.emit("tensor", lambda e, f_=f_, sc=sc, w2t=w2t, st=(f_ == 0): e.matmul(
                        ph[sc][:, 0:512], h1t[f_][:, sc * 128:(sc + 1) * 128], w2t[:], start=st, stop=False),
                        reads=[f"moe_h1_{f_}", f"moe_w2c{pp}"], writes=[f"ph{sc}"])
            for sc in range(NSC):
                P.emit("tensor", lambda e, sc=sc, dv=dv: e.matmul(
                    ph[sc][:, 0:512], orbf[0:1, 0:128], b2bf[0:1, dv * 512:(dv + 1) * 512], start=False, stop=True),
                    reads=["moe_b2bf", "ones_row_bf"], writes=[f"ph{sc}"])
                P.emit("scalar", lambda e, sc=sc, dv=dv: e.activation(out=eot[(sc, dv)][:], in_=ph[sc][:, 0:512], func=AF.Copy),
                       reads=[f"ph{sc}"], writes=[f"moe_eo{sc}_{dv}"])
        for tokc in range(8):
            for sc in range(NSC):
                P.emit("vector", lambda e, e_=e_, sc=sc, tokc=tokc: e.tensor_scalar(
                    idxsh[:], idxf[:, e_ * 4 + sc:e_ * 4 + sc + 1], float(-512 * tokc), None, op0=ALU.add),
                    reads=["moe_idxf"], writes=["moe_idxsh"])
                P.emit("vector", lambda e, e_=e_, sc=sc: e.tensor_scalar(
                    pg[:, sc * 512:(sc + 1) * 512], iorow[:],
                    idxsh[:], gval[:, e_ * 4 + sc:e_ * 4 + sc + 1],
                    op0=ALU.is_equal, op1=ALU.mult),
                    reads=["moe_iorow"], sreads=["moe_idxsh", "moe_gval"], writes=["moe_pg"])
            for dmc in range(NDM):
                dv = dmc // 4
                pk = "ph4" if dmc % 2 == 0 else "ph5"
                pht_ = ph[4] if dmc % 2 == 0 else ph[5]
                for sc in range(NSC):
                    eo = eot[(sc, dv)]
                    P.emit("tensor", lambda e, eo=eo, sc=sc, dmc=dmc, pht_=pht_, st=(sc == 0), sp=(sc == NSC - 1):
                           e.matmul(pht_[:, 0:512], eo[:, (dmc % 4) * 128:(dmc % 4 + 1) * 128],
                                    pg[:, sc * 512:(sc + 1) * 512], start=st, stop=sp),
                           reads=[f"moe_eo{sc}_{dv}", "moe_pg"], writes=[pk])
                xo = A.t[xk[dmc]]
                s = slice(tokc * 512, (tokc + 1) * 512)
                P.emit("vector", lambda e, xo=xo, s=s, pht_=pht_: e.scalar_tensor_tensor(
                    xo[:, s], pht_[:, 0:512], 0.5, xo[:, s], op0=ALU.mult, op1=ALU.add),
                    reads=[pk, xk[dmc]], writes=[xk[dmc]])
        P.fence()
        A.free_to(mkT)
    P.fence()
    A.free_to(mkE)
    A.free_to(mkTOP)


def build_sl3_standalone():
    return build_sl1_standalone()


def build_sl6_standalone():
    return build_sl4_standalone()


def build_fused():
    nc = bass.Bass("TRN2", target_bir_lowering=False)
    xin = nc.dram_tensor("xT", [D_MODEL, SEQ], F32, kind="ExternalInput").ap()
    xout = nc.dram_tensor("yT", [D_MODEL, SEQ], F32, kind="ExternalOutput").ap()
    xres = nc.dram_tensor("hbm_xres", [D_MODEL, SEQ], F32).ap()
    Wd = {}
    for shapes in (MOE_W_SHAPES, ATTN_W_SHAPES, L1FFN_W_SHAPES, MB_W_SHAPES):
        for k, sh in shapes.items():
            Wd[k] = nc.dram_tensor(k, sh, F32, kind="ExternalInput").ap()
    with ExitStack() as stack:
        A = Alloc(nc, stack)
        P = Prog(nc)
        g = build_common(P, A)
        A.sb("wstage1k", [128, 1024], F32)
        A.sb("wstage1k_bf", [128, 1024], BF16)
        xkeys = [f"x{i}" for i in range(NDM)]
        for i in range(NDM):
            x = A.sb(xkeys[i], [128, SEQ], F32)
            P.emit("sync", lambda e, x=x, i=i: e.dma_start(out=x[:], in_=xin[i * 128:(i + 1) * 128, :]),
                   writes=[xkeys[i]], dma=True)
        for pn in ["ph0", "ph1", "ph2", "ph3", "ph4", "ph5"]:
            A.ps(pn, [128, 512], F32)
        A.ps("phbf", [128, 1024], BF16)
        sl_l0ffn(P, A, g, Wd, xkeys, first_call=True)
        sl_l0attn(P, A, g, Wd, xkeys)
        sl_l0ffn(P, A, g, Wd, xkeys, first_call=False)
        sl_l1ffn(P, A, g, Wd, xkeys, first_call=True)
        xres2 = nc.dram_tensor("hbm_xres2", [D_MODEL, SEQ], F32).ap()
        sl_l1attn(P, A, g, Wd, xkeys, xres, xres2)
        for i in range(NDM):
            x = A.sb(xkeys[i], [128, SEQ], F32)
            P.emit("sync", lambda e, x=x, i=i: e.dma_start(out=x[:], in_=xres2[i * 128:(i + 1) * 128, :]),
                   reads=["mb_xout"], writes=[xkeys[i]], dma=True)
        sl_l1ffn(P, A, g, Wd, xkeys, first_call=False)
        for i in range(NDM):
            x = A.t[xkeys[i]]
            P.emit("gpsimd", lambda e, x=x, i=i: e.dma_start(out=xout[i * 128:(i + 1) * 128, :], in_=x[:]),
                   reads=[xkeys[i]], writes=[f"outdma{i}"], dma=True)
        P.barrier_read("gpsimd", [f"outdma{i}" for i in range(NDM)])
        finalize(P, nc, A)
        print(f"SBUF bytes: {A.bytes/1e6:.2f}MB, ops: {len(P.ops)}", flush=True)
    return nc


def kernel(**inputs):
    x = np.asarray(inputs["L0_attn_x"], dtype=np.float32)
    Watt = prep_l0attn_weights(inputs)
    W4 = prep_l1ffn_weights(inputs)
    W5 = prep_l1attn_weights(inputs)
    global DEBUG_TAPS
    DEBUG_TAPS = []
    nc = build_fused()
    ims = []
    for c in range(N_CORES):
        im = dict(prep_l0ffn_weights(inputs, c))
        im.update(Watt)
        im.update(W4)
        im.update(W5)
        im["xT"] = np.ascontiguousarray(x[c].T)
        ims.append(im)
    res = run_bass_kernel_spmd(nc, ims, core_ids=list(range(N_CORES)))
    return np.stack([np.asarray(res.results[c]["yT"], np.float32).T for c in range(N_CORES)], axis=0)


def kernel_unfused(**inputs):
    x = np.asarray(inputs["L0_attn_x"], dtype=np.float32)
    xT = [np.ascontiguousarray(x[c].T) for c in range(N_CORES)]
    Wmoe = [prep_l0ffn_weights(inputs, c) for c in range(N_CORES)]
    Watt = prep_l0attn_weights(inputs)
    W4 = prep_l1ffn_weights(inputs)
    W5 = prep_l1attn_weights(inputs)

    def run(nc, wlist):
        ims = [dict(wlist[c]) for c in range(N_CORES)]
        for c in range(N_CORES):
            ims[c]["xT"] = xT[c]
        res = run_bass_kernel_spmd(nc, ims, core_ids=list(range(N_CORES)))
        for c in range(N_CORES):
            xT[c] = np.ascontiguousarray(np.asarray(res.results[c]["yT"], np.float32))

    global DEBUG_TAPS
    DEBUG_TAPS = []
    run(build_sl1_standalone(), Wmoe)
    run(build_sl2_standalone(), [Watt] * N_CORES)
    run(build_sl3_standalone(), Wmoe)
    run(build_sl4_standalone(), [W4] * N_CORES)
    run(build_sl5_standalone(), [W5] * N_CORES)
    run(build_sl6_standalone(), [W4] * N_CORES)
    return np.stack([xT[c].T for c in range(N_CORES)], axis=0)




def sl_l1attn(P, A, g, W, xk, xres, xout_ap, alloc_x=None, fused=False):
    nc = P.nc
    TC = 512
    NTC = SEQ // TC
    ph = [A.t[f"ph{i}"] for i in range(6)]
    phbf = A.t["phbf"]
    dirs = ("f", "b")
    hxi = {d: nc.dram_tensor(f"hbm_xi_{d}", [2048, SEQ], BF16).ap() for d in dirs}
    hz = {d: nc.dram_tensor(f"hbm_z_{d}", [2048, SEQ], BF16).ap() for d in dirs}
    hmo = {d: nc.dram_tensor(f"hbm_mo_{d}", [1024, SEQ], BF16).ap() for d in dirs}
    hinw = {d: nc.dram_tensor(f"hbm_inw_{d}", [32, 8, 128, 128], BF16).ap() for d in dirs}
    houtw = {d: nc.dram_tensor(f"hbm_outw_{d}", [8, 16, 128, 128], BF16).ap() for d in dirs}
    hprj = nc.dram_tensor("hbm_prj", [8, 16, 128, 128], BF16).ap()
    for d in dirs:
        cast_to_hbm_blocked(P, A, f"inw{d}", W[f"mb_inWT_{d}"], 1024, 4096, hinw[d], "wstage1k")
        cast_to_hbm_blocked(P, A, f"outw{d}", W[f"mb_outWT_{d}"], 2048, 1024, houtw[d], "wstage1k")
    cast_to_hbm_blocked(P, A, "prj", W["mb_projT"], 2048, 1024, hprj, "wstage1k")
    if alloc_x is not None:
        alloc_x()

    mk1 = A.mark()
    for i in range(NDM):
        A.sb(f"mb_xc{i}", [128, 512], BF16, side="right")
    xlnc = [f"mb_xc{i}" for i in range(NDM)]
    for pp in range(2):
        for i in range(NDM):
            A.sb(f"mb_wc{pp}_{i}", [128, 128], BF16, side="right")
    ev = A.sb("mb_ev", [128, 512], BF16, side="right")
    for tc in range(8):
        s = slice(tc * 512, (tc + 1) * 512)
        ln_chunk(P, A, g, xk, tc, xlnc)
        for d in dirs:
            for mc in range(32):
                pp = mc % 2
                for i in range(NDM):
                    wc = A.t[f"mb_wc{pp}_{i}"]
                    P.emit("sync" if i % 2 == 0 else "gpsimd",
                           lambda e, wc=wc, d=d, mc=mc, i=i: e.dma_start(out=wc[:], in_=hinw[d][mc, i, :, :]),
                           reads=[f"inw{d}_blk"], writes=[f"mb_wc{pp}_{i}"], dma=True)
                for i in range(NDM):
                    wc = A.t[f"mb_wc{pp}_{i}"]
                    xc = A.t[xlnc[i]]
                    P.emit("tensor", lambda e, wc=wc, xc=xc, st=(i == 0), sp=(i == NDM - 1):
                           e.matmul(ph[0][:, 0:512], wc[:], xc[:], start=st, stop=sp),
                           reads=[f"mb_wc{pp}_{i}", xlnc[i]], writes=["ph0"])
                if d == "f":
                    P.emit("scalar", lambda e: e.activation(out=ev[:], in_=ph[0][:, 0:512], func=AF.Copy),
                           reads=["ph0"], writes=["mb_ev"])
                else:
                    P.emit("scalar", lambda e: e.activation(out=ev[:, ::-1], in_=ph[0][:, 0:512], func=AF.Copy),
                           reads=["ph0"], writes=["mb_ev"])
                xi_or_z = hxi[d] if mc < 16 else hz[d]
                row = (mc % 16) * 128
                col = s if d == "f" else slice((7 - tc) * 512, (8 - tc) * 512)
                P.emit("gpsimd", lambda e, xi_or_z=xi_or_z, row=row, col=col: e.dma_start(
                    out=xi_or_z[row:row + 128, col], in_=ev[:]),
                    reads=["mb_ev"], writes=["mb_spill"], dma=True)
    P.fence()
    A.free_to(mk1)
    for i in range(NDM):
        x = A.t[xk[i]]
        P.emit("sync", lambda e, x=x, i=i: e.dma_start(out=xres[i * 128:(i + 1) * 128, :], in_=x[:]),
               reads=[xk[i]], writes=["mb_xres"], dma=True)
    P.fence()
    A.free_left(*xk)
    mkP = A.mark()
    for d in dirs:
        cw = A.sb(f"mb_cw_{d}", [128, 64], F32, side="right")
        P.emit("sync", lambda e, cw=cw, d=d: e.dma_start(out=cw[:], in_=W[f"mb_convw_{d}"][:, :]),
               writes=[f"mb_cw_{d}"], dma=True)
        cb = A.sb(f"mb_cb_{d}", [128, 16], F32, side="right")
        P.emit("sync", lambda e, cb=cb, d=d: e.dma_start(out=cb[:], in_=W[f"mb_convb_{d}"][:, :]),
               writes=[f"mb_cb_{d}"], dma=True)
        cbh = A.sb(f"mb_cbh_{d}", [128, 16], F32, side="right")
        P.emit("vector", lambda e, cbh=cbh, cb=cb: e.tensor_scalar(cbh[:], cb[:], 0.5, None, op0=ALU.mult),
               reads=[f"mb_cb_{d}"], writes=[f"mb_cbh_{d}"])
        load_weight_bf16(P, A, f"mb_xpT_{d}", W[f"mb_xpT_{d}"], [128, 96 * 16], stage="wstage1k", side="right")
        load_weight_bf16(P, A, f"mb_dtWT_{d}", W[f"mb_dtWT_{d}"], [64, 2048], stage="wstage1k", side="right")
        alog = A.sb(f"mb_A_{d}", [128, 256], F32, side="right")
        P.emit("sync", lambda e, alog=alog, d=d: e.dma_start(out=alog[:], in_=W[f"mb_Alog_{d}"][:, :]),
               writes=[f"mb_A_{d}"], dma=True)
        P.emit("scalar", lambda e, alog=alog: e.activation(out=alog[:], in_=alog[:], func=AF.Exp),
               reads=[f"mb_A_{d}"], writes=[f"mb_A_{d}"])
        P.emit("vector", lambda e, alog=alog: e.tensor_scalar(alog[:], alog[:], -1.0, None, op0=ALU.mult),
               reads=[f"mb_A_{d}"], writes=[f"mb_A_{d}"])
        dsb = A.sb(f"mb_D_{d}", [128, 16], F32, side="right")
        P.emit("sync", lambda e, dsb=dsb, d=d: e.dma_start(out=dsb[:], in_=W[f"mb_D_{d}"][:, :]),
               writes=[f"mb_D_{d}"], dma=True)
        dtb = A.sb(f"mb_dtb_{d}", [128, 16], F32, side="right")
        P.emit("sync", lambda e, dtb=dtb, d=d: e.dma_start(out=dtb[:], in_=W[f"mb_dtb_{d}"][:, :]),
               writes=[f"mb_dtb_{d}"], dma=True)
    load_weight_bf16(P, A, "mb_seln", W["mb_seln"], [32, 4096], stage="wstage1k", side="right")
    pjb = A.sb("mb_pjb", [128, 8], F32, side="right")
    P.emit("sync", lambda e: e.dma_start(out=pjb[:], in_=W["mb_pjb"][:, :]), writes=["mb_pjb"], dma=True)

    def p2_dir(d):
        mk2 = A.mark()
        xct = [A.sb(f"mbx{i}", [128, 512], BF16, side="right") for i in range(16)]
        mkxi = A.mark()
        xit = [A.sb(f"mbxi{i}", [128, 515], BF16, side="right") for i in range(16)]
        cw, cbh = A.t[f"mb_cw_{d}"], A.t[f"mb_cbh_{d}"]
        cb = A.t[f"mb_cb_{d}"]
        alog, dsb, dtb = A.t[f"mb_A_{d}"], A.t[f"mb_D_{d}"], A.t[f"mb_dtb_{d}"]
        xpT = A.t[f"mb_xpT_{d}"]
        dtWT = A.t[f"mb_dtWT_{d}"]
        seln = A.t["mb_seln"]
        convt = A.sb("mb_cvt", [128, 512], F32, side="right")
        convt2 = A.sb("mb_cvt2", [128, 512], F32, side="right")
        xdbl = A.sb("mb_xdbl", [64, 512], BF16, side="right")
        bcsb = A.sb("mb_bcsb", [32, 512], BF16, side="right")
        BB = A.sb("mb_BB", [128, 16 * 512], BF16, side="right")
        CB = A.sb("mb_CB", [128, 16 * 512], BF16, side="right")
        hint = A.sb("mb_hint", [128, 16 * 512], BF16, side="right")
        carry = A.sb("mb_carry", [128, 256], BF16, side="right")
        dts = A.sb("mb_dts", [128, 512], BF16, side="right")
        dtx = A.sb("mb_dtx", [128, 512], BF16, side="right")
        dA = A.sb("mb_dA", [128, 512], F32, side="right")
        ut = A.sb("mb_ut", [128, 512], F32, side="right")
        yt = A.sb("mb_yt", [128, 512], F32, side="right")
        zt = A.sb("mb_zt", [128, 512], BF16, side="right")
        zs = A.sb("mb_zs", [128, 512], F32, side="right")
        for pp in range(2):
            A.sb(f"mb_owc{pp}", [128, 128], BF16, side="right")
        for tc in range(NTC):
            s = slice(tc * 512, (tc + 1) * 512)
            for i in range(16):
                xi_ = xit[i]
                if tc == 0:
                    P.emit("vector", lambda e, xi_=xi_: e.memset(xi_[:, 0:3], 0.0), writes=[f"mbxi{i}"])
                    P.emit("sync", lambda e, xi_=xi_, i=i: e.dma_start(
                        out=xi_[:, 3:515], in_=hxi[d][i * 128:(i + 1) * 128, 0:512]),
                        reads=["mb_spill"], writes=[f"mbxi{i}"], dma=True)
                else:
                    P.emit("sync", lambda e, xi_=xi_, i=i, tc=tc: e.dma_start(
                        out=xi_[:, 0:515], in_=hxi[d][i * 128:(i + 1) * 128, tc * 512 - 3:(tc + 1) * 512]),
                        reads=["mb_spill"], writes=[f"mbxi{i}"], dma=True)
            for i in range(16):
                xi_ = xit[i]
                xc_ = xct[i]
                P.emit("vector", lambda e, xi_=xi_, i=i: e.tensor_scalar(
                    convt[:], xi_[:, 0:512], cw[:, 4 * i:4 * i + 1], None, op0=ALU.mult),
                    reads=[f"mbxi{i}"], sreads=[f"mb_cw_{d}"], writes=["mb_cvt"])
                for j in (1, 2, 3):
                    src_t = convt if j % 2 == 1 else convt2
                    dst_t = convt2 if j % 2 == 1 else convt
                    sk = "mb_cvt" if j % 2 == 1 else "mb_cvt2"
                    dk = "mb_cvt2" if j % 2 == 1 else "mb_cvt"
                    P.emit("vector", lambda e, xi_=xi_, i=i, j=j, src_t=src_t, dst_t=dst_t: e.scalar_tensor_tensor(
                        dst_t[:], xi_[:, j:512 + j], cw[:, 4 * i + j:4 * i + j + 1], src_t[:],
                        op0=ALU.mult, op1=ALU.add),
                        reads=[f"mbxi{i}", sk], sreads=[f"mb_cw_{d}"], writes=[dk])
                P.emit("vector", lambda e, i=i: e.tensor_scalar(
                    convt[:], convt2[:], cb[:, i:i + 1], None, op0=ALU.add),
                    reads=["mb_cvt2"], sreads=[f"mb_cb_{d}"], writes=["mb_cvt"])
                P.emit("scalar", lambda e, i=i: e.activation(out=convt2[:], in_=convt2[:], func=AF.Tanh,
                                                             scale=0.5, bias=cbh[:, i:i + 1]),
                       reads=["mb_cvt2"], sreads=[f"mb_cbh_{d}"], writes=["mb_cvt2"])
                P.emit("vector", lambda e: e.tensor_scalar(convt2[:], convt2[:], 0.5, 0.5, op0=ALU.mult, op1=ALU.add),
                       reads=["mb_cvt2"], writes=["mb_cvt2"])
                P.emit("vector", lambda e, xc_=xc_, i=i: e.tensor_tensor(xc_[:], convt[:], convt2[:], op=ALU.mult),
                       reads=["mb_cvt", "mb_cvt2"], writes=[f"mbx{i}"])
            if DEBUG_ON and d == "f" and tc == 0:
                tap(P, A, "xc0", "mbx0", lambda: xct[0][:])
                tap(P, A, "xc15", "mbx15", lambda: xct[15][:])
            for i in range(16):
                xc_ = xct[i]
                P.emit("tensor", lambda e, xc_=xc_, i=i, st=(i == 0), sp=(i == 15): e.matmul(
                    ph[1][0:64, 0:512], xpT[:, i * 96:i * 96 + 64], xc_[:], start=st, stop=sp),
                    reads=[f"mb_xpT_{d}", f"mbx{i}"], writes=["ph1"])
                P.emit("tensor", lambda e, xc_=xc_, i=i, st=(i == 0), sp=(i == 15): e.matmul(
                    ph[2][0:32, 0:512], xpT[:, i * 96 + 64:i * 96 + 96], xc_[:], start=st, stop=sp),
                    reads=[f"mb_xpT_{d}", f"mbx{i}"], writes=["ph2"])
            P.emit("scalar", lambda e: e.activation(out=xdbl[:], in_=ph[1][0:64, 0:512], func=AF.Copy),
                   reads=["ph1"], writes=["mb_xdbl"])
            P.emit("scalar", lambda e: e.activation(out=bcsb[:], in_=ph[2][0:32, 0:512], func=AF.Copy),
                   reads=["ph2"], writes=["mb_bcsb"])
            for n in range(16):
                P.emit("tensor", lambda e, n=n: e.matmul(ph[3][:, 0:512], seln[:, n * 128:(n + 1) * 128], bcsb[:],
                                                         start=True, stop=True),
                       reads=["mb_seln", "mb_bcsb"], writes=["ph3"])
                P.emit("scalar", lambda e, n=n: e.activation(out=BB[:, n * 512:(n + 1) * 512], in_=ph[3][:, 0:512], func=AF.Copy),
                       reads=["ph3"], writes=["mb_BB"])
                P.emit("tensor", lambda e, n=n: e.matmul(ph[4][:, 0:512], seln[:, (16 + n) * 128:(17 + n) * 128], bcsb[:],
                                                         start=True, stop=True),
                       reads=["mb_seln", "mb_bcsb"], writes=["ph4"])
                P.emit("scalar", lambda e, n=n: e.activation(out=CB[:, n * 512:(n + 1) * 512], in_=ph[4][:, 0:512], func=AF.Copy),
                       reads=["ph4"], writes=["mb_CB"])
            for i in range(16):
                xc_ = xct[i]
                P.emit("tensor", lambda e, i=i: e.matmul(ph[5][:, 0:512], dtWT[:, i * 128:(i + 1) * 128], xdbl[:],
                                                         start=True, stop=True),
                       reads=[f"mb_dtWT_{d}", "mb_xdbl"], writes=["ph5"])
                P.emit("scalar", lambda e, i=i: e.activation(out=dA[:], in_=ph[5][:, 0:512], func=AF.Exp,
                                                             bias=dtb[:, i:i + 1]),
                       reads=["ph5"], sreads=[f"mb_dtb_{d}"], writes=["mb_dA"])
                P.emit("scalar", lambda e: e.activation(out=dts[:], in_=dA[:], func=AF.Ln, bias=1.0),
                       reads=["mb_dA"], writes=["mb_dts"])
                P.emit("vector", lambda e, xc_=xc_, i=i: e.tensor_tensor(dtx[:], dts[:], xc_[:], op=ALU.mult),
                       reads=["mb_dts", f"mbx{i}"], writes=["mb_dtx"])
                if DEBUG_ON and d == "f" and tc == 0 and i == 0:
                    tap(P, A, "dts0", "mb_dts", lambda: dts[:])
                for n in range(16):
                    P.emit("scalar", lambda e, i=i, n=n: e.activation(out=dA[:], in_=dts[:], func=AF.Exp,
                                                                      scale=alog[:, i * 16 + n:i * 16 + n + 1]),
                           reads=["mb_dts"], sreads=[f"mb_A_{d}"], writes=["mb_dA"])
                    P.emit("vector", lambda e, n=n: e.tensor_tensor(ut[:], dtx[:], BB[:, n * 512:(n + 1) * 512], op=ALU.mult),
                           reads=["mb_dtx", "mb_BB"], writes=["mb_ut"])
                    if tc == 0:
                        P.emit("vector", lambda e, i=i, n=n: e.tensor_tensor_scan(
                            hint[:, n * 512:(n + 1) * 512], dA[:], ut[:], 0.0, op0=ALU.mult, op1=ALU.add),
                            reads=["mb_dA", "mb_ut"], writes=["mb_hint"])
                    else:
                        P.emit("vector", lambda e, i=i, n=n: e.tensor_tensor_scan(
                            hint[:, n * 512:(n + 1) * 512], dA[:], ut[:], carry[:, i * 16 + n:i * 16 + n + 1],
                            op0=ALU.mult, op1=ALU.add),
                            reads=["mb_dA", "mb_ut"], sreads=["mb_carry"], writes=["mb_hint"])
                if tc < NTC - 1:
                    P.emit("vector", lambda e, i=i: e.tensor_copy(
                        carry[:, i * 16:(i + 1) * 16], hint[:, 511::512]),
                        reads=["mb_hint"], writes=["mb_carry"])
                P.emit("vector", lambda e, i=i: e.tensor_tensor(hint[:], hint[:], CB[:], op=ALU.mult),
                       reads=["mb_hint", "mb_CB"], writes=["mb_hint"])
                P.emit("vector", lambda e: e.tensor_reduce(
                    yt[:], hint[:].rearrange("p (n t) -> p t n", n=16), axis=AX.X, op=ALU.add),
                    reads=["mb_hint"], writes=["mb_yt"])
                P.emit("vector", lambda e, xc_=xc_, i=i: e.scalar_tensor_tensor(
                    yt[:], xc_[:], dsb[:, i:i + 1], yt[:], op0=ALU.mult, op1=ALU.add),
                    reads=[f"mbx{i}", "mb_yt"], sreads=[f"mb_D_{d}"], writes=["mb_yt"])
                if DEBUG_ON and d == "f" and tc == 0 and i == 0:
                    tap(P, A, "yt0", "mb_yt", lambda: yt[:])
                P.emit("sync", lambda e, i=i, s=s: e.dma_start(out=zt[:], in_=hz[d][i * 128:(i + 1) * 128, s]),
                       reads=["mb_spill"], writes=["mb_zt"], dma=True)
                P.emit("scalar", lambda e: e.activation(out=zs[:], in_=zt[:], func=AF.Tanh, scale=0.5),
                       reads=["mb_zt"], writes=["mb_zs"])
                P.emit("vector", lambda e: e.tensor_scalar(zs[:], zs[:], 0.5, 0.5, op0=ALU.mult, op1=ALU.add),
                       reads=["mb_zs"], writes=["mb_zs"])
                P.emit("vector", lambda e: e.tensor_tensor(zs[:], zs[:], zt[:], op=ALU.mult),
                       reads=["mb_zs", "mb_zt"], writes=["mb_zs"])
                P.emit("vector", lambda e, xc_=xc_, i=i: e.tensor_tensor(xc_[:], yt[:], zs[:], op=ALU.mult),
                       reads=["mb_yt", "mb_zs"], writes=[f"mbx{i}"])
                if DEBUG_ON and d == "f" and tc == 0 and i == 0:
                    tap(P, A, "zs0", "mb_zs", lambda: zs[:])
                    tap(P, A, "yg0", "mbx0", lambda: xc_[:])
            for mc in range(8):
                for i in range(16):
                    pp = i % 2
                    owc = A.t[f"mb_owc{pp}"]
                    P.emit("sync", lambda e, owc=owc, d=d, mc=mc, i=i: e.dma_start(
                        out=owc[:], in_=houtw[d][mc, i, :, :]),
                        reads=[f"outw{d}_blk"], writes=[f"mb_owc{pp}"], dma=True)
                    xc_ = xct[i]
                    P.emit("tensor", lambda e, owc=owc, xc_=xc_, st=(i == 0), sp=(i == 15): e.matmul(
                        ph[0][:, 0:512], owc[:], xc_[:], start=st, stop=sp),
                        reads=[f"mb_owc{pp}", f"mbx{i}"], writes=["ph0"])
                if d == "f":
                    P.emit("scalar", lambda e: e.activation(out=dA[:], in_=ph[0][:, 0:512], func=AF.Copy),
                           reads=["ph0"], writes=["mb_dA"])
                else:
                    P.emit("scalar", lambda e: e.activation(out=dA[:, ::-1], in_=ph[0][:, 0:512], func=AF.Copy),
                           reads=["ph0"], writes=["mb_dA"])
                col = s if d == "f" else slice((NTC - 1 - tc) * 512, (NTC - tc) * 512)
                P.emit("vector", lambda e: e.tensor_copy(zt[:], dA[:]), reads=["mb_dA"], writes=["mb_zt"])
                P.emit("gpsimd", lambda e, mc=mc, col=col, d=d: e.dma_start(
                    out=hmo[d][mc * 128:(mc + 1) * 128, col], in_=zt[:]),
                    reads=["mb_zt"], writes=["mb_mo"], dma=True)
        P.fence()
        A.free_to(mk2)

    for d in dirs:
        p2_dir(d)
    mk3 = A.mark()
    mot = [A.sb(f"mb_mot{i}", [128, 512], BF16, side="right") for i in range(16)]
    xtl = [A.sb(f"mb_xt{i}", [128, 512], F32, side="right") for i in range(NDM)]
    for pp in range(2):
        A.sb(f"mb_pjc{pp}", [128, 128], BF16, side="right")
    for tc in range(8):
        s = slice(tc * 512, (tc + 1) * 512)
        for i in range(16):
            src_d = "f" if i < 8 else "b"
            row = (i % 8) * 128
            P.emit("sync", lambda e, i=i, src_d=src_d, row=row, s=s: e.dma_start(
                out=mot[i][:], in_=hmo[src_d][row:row + 128, s]),
                reads=["mb_mo"], writes=[f"mb_mot{i}"], dma=True)
        for dmc in range(NDM):
            P.emit("sync", lambda e, dmc=dmc, s=s: e.dma_start(
                out=xtl[dmc][:], in_=xres[dmc * 128:(dmc + 1) * 128, s]),
                reads=["mb_xres"], writes=[f"mb_xt{dmc}"], dma=True)
        for dmc in range(NDM):
            pb = "ph4" if dmc % 2 == 0 else "ph5"
            pht_ = ph[4] if dmc % 2 == 0 else ph[5]
            for i in range(16):
                pp = i % 2
                pjc = A.t[f"mb_pjc{pp}"]
                P.emit("gpsimd", lambda e, pjc=pjc, dmc=dmc, i=i: e.dma_start(
                    out=pjc[:], in_=hprj[dmc, i, :, :]),
                    reads=["prj_blk"], writes=[f"mb_pjc{pp}"], dma=True)
                P.emit("tensor", lambda e, pjc=pjc, i=i, pht_=pht_, st=(i == 0), sp=(i == 15): e.matmul(
                    pht_[:, 0:512], pjc[:], mot[i][:], start=st, stop=sp),
                    reads=[f"mb_pjc{pp}", f"mb_mot{i}"], writes=[pb])
            P.emit("vector", lambda e, dmc=dmc, pht_=pht_: e.scalar_tensor_tensor(
                xtl[dmc][:], pht_[:, 0:512], pjb[:, dmc:dmc + 1], xtl[dmc][:], op0=ALU.add, op1=ALU.add),
                reads=[pb, f"mb_xt{dmc}"], sreads=["mb_pjb"], writes=[f"mb_xt{dmc}"])
            P.emit("gpsimd", lambda e, dmc=dmc, s=s: e.dma_start(
                out=xout_ap[dmc * 128:(dmc + 1) * 128, s], in_=xtl[dmc][:]),
                reads=[f"mb_xt{dmc}"], writes=["mb_xout"], dma=True)
    P.fence()
    A.free_to(mk3)
    A.free_to(mkP)


def prep_l0ffn_weights(inputs, core_id):
    W = {}
    rw = np.asarray(inputs["L0_ffn_router_w"])
    W["moe_rw"] = np.ascontiguousarray(rw.reshape(8, 128, 8).transpose(1, 0, 2).reshape(128, 64))
    W["moe_rb"] = np.ascontiguousarray(np.asarray(inputs["L0_ffn_router_b"]).reshape(1, 8))
    W["moe_w1"] = np.ascontiguousarray(np.asarray(inputs["L0_ffn_w1"]))
    W["moe_w2"] = np.ascontiguousarray(np.asarray(inputs["L0_ffn_w2"]))
    W["moe_b1"] = np.ascontiguousarray(np.asarray(inputs["L0_ffn_b1"]).reshape(8, 32, 128).transpose(0, 2, 1))
    W["moe_b2"] = np.ascontiguousarray(np.asarray(inputs["L0_ffn_b2"]))
    L = np.zeros((128, 128), np.float32)
    for k in range(128):
        for p in range(128):
            if k // 16 == p // 16 and (k % 16) < (p % 16):
                L[k, p] = 1.0
    W["moe_L"] = L
    S = np.zeros((128, 8), np.float32)
    for p in range(128):
        S[p, p // 16] = 1.0
    W["moe_S"] = S
    W["moe_SE"] = np.ascontiguousarray(S.T)
    oh = np.zeros((128, 8), np.float32)
    for e_ in range(8):
        oh[e_ * 16 + 2 * core_id, e_] = 1.0
    W["moe_ownhot"] = oh
    return W


MOE_W_SHAPES = dict(
    moe_rw=[128, 64], moe_rb=[1, 8], moe_w1=[8, 1024, 4096], moe_w2=[8, 4096, 1024],
    moe_b1=[8, 128, 32], moe_b2=[8, 1024], moe_L=[128, 128], moe_S=[128, 8],
    moe_SE=[8, 128], moe_ownhot=[128, 8])


def build_sl1_standalone(n_cores=8):
    nc = bass.Bass("TRN2", target_bir_lowering=False)
    xin = nc.dram_tensor("xT", [D_MODEL, SEQ], F32, kind="ExternalInput").ap()
    xout = nc.dram_tensor("yT", [D_MODEL, SEQ], F32, kind="ExternalOutput").ap()
    Wd = {k: nc.dram_tensor(k, sh, F32, kind="ExternalInput").ap()
          for k, sh in MOE_W_SHAPES.items()}
    with ExitStack() as stack:
        A = Alloc(nc, stack)
        P = Prog(nc)
        g = build_common(P, A)
        A.sb("wstage1k", [128, 1024], F32)
        A.sb("wstage1k_bf", [128, 1024], BF16)
        xkeys = [f"x{i}" for i in range(NDM)]
        for i in range(NDM):
            x = A.sb(xkeys[i], [128, SEQ], F32)
            P.emit("sync", lambda e, x=x, i=i: e.dma_start(out=x[:], in_=xin[i * 128:(i + 1) * 128, :]),
                   writes=[xkeys[i]], dma=True)
        for pn in ["ph0", "ph1", "ph2", "ph3", "ph4", "ph5"]:
            A.ps(pn, [128, 512], F32)
        A.ps("phbf", [128, 1024], BF16)
        sl_l0ffn(P, A, g, Wd, xkeys, first_call=True)
        for nm, tk in DEBUG_TAPS:
            t = A.t[tk]
            dt_ = nc.dram_tensor("tap_" + nm, list(t[:].shape), t[:].dtype, kind="ExternalOutput").ap()
            P.emit("gpsimd", lambda e, dt_=dt_, t=t: e.dma_start(out=dt_, in_=t[:]),
                   reads=[tk], writes=["tapdma_" + nm], dma=True)
        for i in range(NDM):
            x = A.t[xkeys[i]]
            P.emit("gpsimd", lambda e, x=x, i=i: e.dma_start(out=xout[i * 128:(i + 1) * 128, :], in_=x[:]),
                   reads=[xkeys[i]], writes=[f"outdma{i}"], dma=True)
        P.barrier_read("gpsimd", [f"outdma{i}" for i in range(NDM)])
        finalize(P, nc, A)
        print(f"SBUF bytes: {A.bytes/1e6:.2f}MB, ops: {len(P.ops)}", flush=True)
    return nc


def prep_l1attn_weights(inputs):
    W = {}
    for d, pre in (("f", "L1_attn_fwd"), ("b", "L1_attn_bwd")):
        W[f"mb_inWT_{d}"] = np.ascontiguousarray(np.asarray(inputs[f"{pre}_in_W"]).T)
        cwr = np.asarray(inputs[f"{pre}_conv_w"])
        W[f"mb_convw_{d}"] = np.ascontiguousarray(
            cwr.reshape(16, 128, 4).transpose(1, 0, 2).reshape(128, 64))
        W[f"mb_convb_{d}"] = np.ascontiguousarray(
            np.asarray(inputs[f"{pre}_conv_b"]).reshape(16, 128).T)
        xp = np.asarray(inputs[f"{pre}_xproj_W"]).T
        W[f"mb_xpT_{d}"] = np.ascontiguousarray(xp.reshape(16, 128, 96).transpose(1, 0, 2).reshape(128, 1536))
        W[f"mb_dtWT_{d}"] = np.ascontiguousarray(np.asarray(inputs[f"{pre}_dt_W"]).T)
        W[f"mb_dtb_{d}"] = np.ascontiguousarray(np.asarray(inputs[f"{pre}_dt_b"]).reshape(16, 128).T)
        W[f"mb_Alog_{d}"] = np.ascontiguousarray(
            np.asarray(inputs[f"{pre}_A_log"]).reshape(16, 128, 16).transpose(1, 0, 2).reshape(128, 256))
        W[f"mb_D_{d}"] = np.ascontiguousarray(np.asarray(inputs[f"{pre}_D"]).reshape(16, 128).T)
        W[f"mb_outWT_{d}"] = np.ascontiguousarray(np.asarray(inputs[f"{pre}_out_W"]).T)
    W["mb_projT"] = np.ascontiguousarray(np.asarray(inputs["L1_attn_proj_W"]).T)
    W["mb_pjb"] = np.ascontiguousarray(np.asarray(inputs["L1_attn_proj_b"]).reshape(8, 128).T)
    seln = np.zeros((32, 32 * 128), np.float32)
    for r in range(32):
        seln[r, r * 128:(r + 1) * 128] = 1.0
    W["mb_seln"] = seln
    return W


MB_W_SHAPES = dict(
    mb_projT=[2048, 1024], mb_pjb=[128, 8], mb_seln=[32, 4096])
for _d in ("f", "b"):
    MB_W_SHAPES.update({
        f"mb_inWT_{_d}": [1024, 4096], f"mb_convw_{_d}": [128, 64], f"mb_convb_{_d}": [128, 16],
        f"mb_xpT_{_d}": [128, 1536], f"mb_dtWT_{_d}": [64, 2048], f"mb_dtb_{_d}": [128, 16],
        f"mb_Alog_{_d}": [128, 256], f"mb_D_{_d}": [128, 16], f"mb_outWT_{_d}": [2048, 1024]})


def build_sl5_standalone():
    nc = bass.Bass("TRN2", target_bir_lowering=False)
    xin = nc.dram_tensor("xT", [D_MODEL, SEQ], F32, kind="ExternalInput").ap()
    xout = nc.dram_tensor("yT", [D_MODEL, SEQ], F32, kind="ExternalOutput").ap()
    xres = nc.dram_tensor("hbm_xres", [D_MODEL, SEQ], F32).ap()
    Wd = {k: nc.dram_tensor(k, sh, F32, kind="ExternalInput").ap()
          for k, sh in MB_W_SHAPES.items()}
    with ExitStack() as stack:
        A = Alloc(nc, stack)
        P = Prog(nc)
        g = build_common(P, A)
        A.sb("wstage1k", [128, 1024], F32)
        A.sb("wstage1k_bf", [128, 1024], BF16)
        for pn in ["ph0", "ph1", "ph2", "ph3", "ph4", "ph5"]:
            A.ps(pn, [128, 512], F32)
        A.ps("phbf", [128, 1024], BF16)
        P2 = P
        xkeys = [f"x{i}" for i in range(NDM)]
        def alloc_x():
            for i in range(NDM):
                x = A.sb(xkeys[i], [128, SEQ], F32)
                P.emit("sync", lambda e, x=x, i=i: e.dma_start(out=x[:], in_=xin[i * 128:(i + 1) * 128, :]),
                       writes=[xkeys[i]], dma=True)
        sl_l1attn(P, A, g, Wd, xkeys, xres, xout, alloc_x=alloc_x)
        P.barrier_read("gpsimd", ["mb_xout"])
        finalize(P, nc, A)
        print(f"SBUF bytes: {A.bytes/1e6:.2f}MB, ops: {len(P.ops)}", flush=True)
    return nc


# version 04
# speedup vs baseline: 1.3908x; 1.2132x over previous
import math
from contextlib import ExitStack

import numpy as np
import concourse.bass as bass
import concourse.mybir as mybir
from concourse.bass_utils import run_bass_kernel_spmd
from concourse.bass import IndirectOffsetOnAxis

F32 = mybir.dt.float32
BF16 = mybir.dt.bfloat16
I32 = mybir.dt.int32
U32 = mybir.dt.uint32
AF = mybir.ActivationFunctionType
ALU = mybir.AluOpType
AX = mybir.AxisListType

ENGINES = ("sync", "tensor", "vector", "scalar", "gpsimd")
DEBUG_TAPS = []

DEBUG_ON = False
DEBUG_HBM = False
SIZES = {}
PARANOID = False


def tapd(P, A, name, key, ap_fn):
    if DEBUG_ON:
        ap = ap_fn()
        d = P.nc.dram_tensor("tap_" + name, list(ap.shape), ap.dtype, kind="ExternalOutput").ap()
        P.emit("gpsimd", lambda e: e.dma_start(out=d, in_=ap), reads=[key],
               writes=["tap_" + name], dma=True)


def tap(P, A, name, key, ap_fn):
    if DEBUG_ON:
        ap = ap_fn()
        tk = "tapt_" + name
        t = A.sb(tk, [ap.shape[0], int(np.prod(ap.shape[1:]))], ap.dtype)
        P.emit("vector", lambda e: e.tensor_copy(t[:], ap), reads=[key], writes=[tk])
        d = P.nc.dram_tensor("tap_" + name, list(t[:].shape), ap.dtype, kind="ExternalOutput").ap()
        P.emit("gpsimd", lambda e: e.dma_start(out=d, in_=t[:]), reads=[tk],
               writes=["tap_" + name], dma=True)

D_MODEL = 1024
SEQ = 4096
N_CORES = 8
NDM = D_MODEL // 128


class Prog:
    def __init__(self, nc):
        self.nc = nc
        self.ops = []
        self.lastw = {}
        self.readers = {}
        self.eng_last = {}

    def emit(self, eng, fn, reads=(), writes=(), sreads=(), dma=False, tag=""):
        i = len(self.ops)
        deps = set()
        sdep = set()
        for k in sreads:
            j = self.lastw.get(k)
            if j is not None:
                deps.add(j)
                sdep.add(j)
        for k in reads:
            j = self.lastw.get(k)
            if j is not None:
                deps.add(j)
        for k in writes:
            j = self.lastw.get(k)
            if j is not None:
                deps.add(j)
            deps |= self.readers.get(k, set())
        pruned = set()
        for j in deps:
            oj = self.ops[j]
            if (not PARANOID and oj["eng"] == eng and not oj["dma"] and not dma
                    and j not in sdep and not oj["small"]):
                continue
            pruned.add(j)
        small = any(SIZES.get(k, 1 << 30) <= 16 for k in writes)
        self.ops.append(dict(eng=eng, fn=fn, dma=dma, deps=pruned, tag=tag, small=small))
        self.eng_last[eng] = i
        for k in reads:
            self.readers.setdefault(k, set()).add(i)
        for k in sreads:
            self.readers.setdefault(k, set()).add(i)
        for k in writes:
            self.lastw[k] = i
            self.readers[k] = set()
        return i

    def barrier_read(self, eng, keys, tag="final"):
        self.emit(eng, lambda e: e.nop(), reads=tuple(keys), tag=tag)

    def fence(self):
        lasts = dict(self.eng_last)
        for eng in ENGINES:
            deps = {j for e2, j in lasts.items() if e2 != eng}
            i = len(self.ops)
            self.ops.append(dict(eng=eng, fn=lambda e: e.nop(), dma=False,
                                 deps=deps, tag="fence", small=False))
            self.eng_last[eng] = i

    def materialize(self, block, sems):
        needed = set()
        for op in self.ops:
            needed |= op["deps"]
        sig_at = {}
        counters = {k: 0 for k in sems}
        for i, op in enumerate(self.ops):
            if op["dma"] or i in needed:
                k = (op["eng"], "d" if op["dma"] else "c")
                counters[k] += 16 if op["dma"] else 1
                sig_at[i] = (k, counters[k])
        prewaits = [None] * len(self.ops)
        bound = {}
        for i, op in enumerate(self.ops):
            if not op["dma"]:
                continue
            k = (op["eng"], "d")
            b = bound.get(k)
            if b is not None:
                prewaits[i] = b
            if i in needed:
                bound[k] = sig_at[i][1]
        waits = []
        marks = {eng: {} for eng in ENGINES}
        idxs = {eng: [] for eng in ENGINES}
        for i, op in enumerate(self.ops):
            w = {}
            for j in op["deps"]:
                k, v = sig_at[j]
                if w.get(k, -1) < v:
                    w[k] = v
            eng = op["eng"]
            if prewaits[i] is not None:
                kq = (eng, "d")
                if w.get(kq, -1) < prewaits[i]:
                    w[kq] = prewaits[i]
            ww = []
            for k, v in w.items():
                if marks[eng].get(k, -1) >= v:
                    continue
                marks[eng][k] = v
                ww.append((k, v))
            waits.append(ww)
            idxs[eng].append(i)

        def mk(eng):
            def body(e):
                for i in idxs[eng]:
                    for (k, v) in waits[i]:
                        e.wait_ge(sems[k], v)
                    inst = self.ops[i]["fn"](e)
                    if i in sig_at:
                        k, _ = sig_at[i]
                        inst.then_inc(sems[k], 16 if k[1] == "d" else 1)
            return body

        for eng in ENGINES:
            getattr(block, eng)(mk(eng))
        return counters


class Alloc:
    def __init__(self, nc, stack):
        self.nc = nc
        self.stack = stack
        self.bytes = 0
        self.t = {}
        self.cms = {}
        self.rorder = []

    def sb(self, name, shape, dtype, side=None):
        assert name not in self.t, name
        SIZES[name] = int(np.prod(shape[1:]))
        self.colbytes = getattr(self, "colbytes", {})
        self.colbytes[name] = (int(np.prod(shape[1:])) * mybir.dt.size(dtype), side)
        self.bytes += 128 * int(np.prod(shape[1:])) * mybir.dt.size(dtype)
        self.uid = getattr(self, "uid", 0) + 1
        bname = f"{name}_u{self.uid}"
        cm = self.nc.sbuf_tensor(bname, shape, dtype, side=side) if side else self.nc.sbuf_tensor(bname, shape, dtype)
        h = cm.__enter__()
        self.t[name] = h
        self.cms[name] = cm
        if side == "right":
            self.rorder.append(name)
        return h

    def mark(self):
        return len(self.rorder)

    def free_left(self, *names):
        for name in reversed(names):
            self.cms.pop(name).__exit__(None, None, None)
            del self.t[name]

    def free_to(self, mark):
        while len(self.rorder) > mark:
            name = self.rorder.pop()
            self.cms.pop(name).__exit__(None, None, None)
            del self.t[name]

    def ps(self, name, shape, dtype):
        h = self.stack.enter_context(self.nc.psum_tensor(name, shape, dtype))
        self.t[name] = h
        return h


def build_common(P, A):
    g = {}
    g["ident_bf"] = A.sb("ident_bf", [128, 128], BF16)
    g["ident_f"] = A.sb("ident_f", [128, 128], F32)
    g["ones_col_f"] = A.sb("ones_col_f", [128, 1], F32)
    g["ones_col_bf"] = A.sb("ones_col_bf", [128, 1], BF16)
    g["ones_row_f"] = A.sb("ones_row_f", [1, 128], F32)
    g["ones_row_bf"] = A.sb("ones_row_bf", [1, 128], BF16)
    g["iota_i"] = A.sb("iota_i", [128, 128], I32)
    g["iota_f"] = A.sb("iota_f", [128, 128], F32)
    g["pid_f"] = A.sb("pid_f", [128, 1], F32)
    ii, pf, io = g["iota_i"], g["pid_f"], g["iota_f"]
    P.emit("gpsimd", lambda e: e.iota(ii[:], pattern=[[1, 128]], base=0, channel_multiplier=0),
           writes=["iota_i"])
    P.emit("gpsimd", lambda e: e.iota(pf[:], pattern=[[0, 1]], base=0, channel_multiplier=1,
                                      allow_small_or_imprecise_dtypes=True), writes=["pid_f"])
    P.emit("vector", lambda e: e.tensor_copy(io[:], ii[:]), reads=["iota_i"], writes=["iota_f"])
    ibf, iff = g["ident_bf"], g["ident_f"]
    P.emit("vector", lambda e: e.tensor_scalar(ibf[:], io[:], pf[:], None, op0=ALU.is_equal),
           reads=["iota_f"], sreads=["pid_f"], writes=["ident_bf"])
    P.emit("vector", lambda e: e.tensor_scalar(iff[:], io[:], pf[:], None, op0=ALU.is_equal),
           reads=["iota_f"], sreads=["pid_f"], writes=["ident_f"])
    g["ones_all_f"] = A.sb("ones_all_f", [128, 128], F32)
    A.sb("ln_sqs0", [128, 512], F32)
    A.sb("ln_sqs1", [128, 512], F32)
    A.sb("lnrow_m", [1, 512], F32)
    A.sb("lnrow_v", [1, 512], F32)
    A.sb("lnrow_r", [1, 512], F32)
    A.sb("ln_t", [128, 512], F32)
    for n in ("ones_col_f", "ones_col_bf", "ones_row_f", "ones_row_bf", "ones_all_f"):
        h = g[n]
        P.emit("vector", lambda e, h=h: e.memset(h[:], 1.0), writes=[n])
    return g


def load_weight_bf16(P, A, name, ap, shape, eng="sync", stage=None, side=None):
    if stage is None:
        f = A.sb(name + "_f", shape, F32)
        fkey = name + "_f"
    else:
        f = A.t[stage]
        fkey = stage
    b = A.sb(name, shape, BF16, side=side)
    stw = int(f[:].shape[1])
    for c0 in range(0, shape[1], stw):
        c1 = min(c0 + stw, shape[1])
        P.emit(eng, lambda e, c0=c0, c1=c1: e.dma_start(out=f[0:shape[0], 0:c1 - c0], in_=ap[:, c0:c1]),
               writes=[fkey], dma=True)
        P.emit("scalar", lambda e, c0=c0, c1=c1: e.activation(out=b[:, c0:c1], in_=f[0:shape[0], 0:c1 - c0], func=AF.Copy),
               reads=[fkey], writes=[name])
    return b


def ln_chunk(P, A, g, xk, tc, dst_names, phS="ph3", phR="ph4", rev=False,
             dst_key_suffix="", width=512):
    if "ln_sqs0" not in A.t:
        A.sb("ln_sqs0", [128, 512], F32)
        A.sb("ln_sqs1", [128, 512], F32)
        A.sb("lnrow_m", [1, 512], F32)
        A.sb("lnrow_v", [1, 512], F32)
        A.sb("lnrow_r", [1, 512], F32)
        A.sb("ln_t", [128, 512], F32)
    s = slice(tc * width, (tc + 1) * width)
    pS, pR = A.t[phS], A.t[phR]
    m, v, r = A.t["lnrow_m"], A.t["lnrow_v"], A.t["lnrow_r"]
    oc = g["ones_col_f"]
    orf = g["ones_row_f"]
    for i in range(NDM):
        sqn = f"ln_sqs{i % 2}"
        sq = A.t[sqn]
        xi = A.t[xk[i]]
        P.emit("scalar", lambda e, sq=sq, xi=xi, s=s: e.activation(out=sq[0:128, 0:width], in_=xi[:, s], func=AF.Square),
               reads=[xk[i]], writes=[sqn])
        P.emit("tensor", lambda e, xi=xi, s=s, st=(i == 0): e.matmul(
            pS[0:1, 0:width], oc[:], xi[:, s], start=st, stop=(i == NDM - 1)),
            reads=[xk[i], "ones_col_f"], writes=[phS])
        P.emit("tensor", lambda e, sq=sq, st=(i == 0), sp=(i == NDM - 1): e.matmul(
            pR[0:1, 0:width], oc[:], sq[0:128, 0:width], start=st, stop=sp),
            reads=[sqn, "ones_col_f"], writes=[phR])
    P.emit("vector", lambda e: e.tensor_scalar(m[0:1, 0:width], pS[0:1, 0:width], 1.0 / D_MODEL, None, op0=ALU.mult),
           reads=[phS], writes=["lnrow_m"])
    P.emit("vector", lambda e: e.tensor_tensor(v[0:1, 0:width], m[0:1, 0:width], m[0:1, 0:width], op=ALU.mult),
           reads=["lnrow_m"], writes=["lnrow_v"])
    P.emit("vector", lambda e: e.scalar_tensor_tensor(
        v[0:1, 0:width], pR[0:1, 0:width], 1.0 / D_MODEL, v[0:1, 0:width], op0=ALU.mult, op1=ALU.subtract),
        reads=[phR, "lnrow_v"], writes=["lnrow_v"])
    P.emit("vector", lambda e: e.tensor_scalar(v[0:1, 0:width], v[0:1, 0:width], 1e-5, None, op0=ALU.add),
           reads=["lnrow_v"], writes=["lnrow_v"])
    P.emit("scalar", lambda e: e.activation(out=v[0:1, 0:width], in_=v[0:1, 0:width], func=AF.Ln),
           reads=["lnrow_v"], writes=["lnrow_v"])
    P.emit("vector", lambda e: e.tensor_scalar(v[0:1, 0:width], v[0:1, 0:width], -0.5, None, op0=ALU.mult),
           reads=["lnrow_v"], writes=["lnrow_v"])
    P.emit("scalar", lambda e: e.activation(out=r[0:1, 0:width], in_=v[0:1, 0:width], func=AF.Exp),
           reads=["lnrow_v"], writes=["lnrow_r"])
    P.emit("tensor", lambda e: e.matmul(pS[:, 0:width], orf[0:1, :], m[0:1, 0:width], start=True, stop=True),
           reads=["lnrow_m", "ones_row_f"], writes=[phS])
    P.emit("tensor", lambda e: e.matmul(pR[:, 0:width], orf[0:1, :], r[0:1, 0:width], start=True, stop=True),
           reads=["lnrow_r", "ones_row_f"], writes=[phR])
    t = A.t["ln_t"]
    for i in range(NDM):
        xi = A.t[xk[i]]
        d = A.t[dst_names[i]]
        dk = dst_names[i] + dst_key_suffix
        P.emit("vector", lambda e, xi=xi, s=s: e.tensor_tensor(t[:, 0:width], xi[:, s], pS[:, 0:width], op=ALU.subtract),
               reads=[xk[i], phS], writes=["ln_t"])
        if rev:
            P.emit("vector", lambda e, d=d: e.tensor_tensor(d[:, width - 1::-1], t[:, 0:width], pR[:, 0:width], op=ALU.mult),
                   reads=["ln_t", phR], writes=[dk])
        else:
            P.emit("vector", lambda e, d=d: e.tensor_tensor(d[:, 0:width], t[:, 0:width], pR[:, 0:width], op=ALU.mult),
                   reads=["ln_t", phR], writes=[dk])


def sl_l1ffn(P, A, g, W, xk, first_call=True):
    CLAMP = float(np.log(1.0 / 0.01))
    sfx = "a" if first_call else "b"
    mkF = A.mark()
    if True:
        if "wstage" not in A.t:
            A.sb("wstage", [128, 1024], F32, side="right")
        for i in range(NDM):
            load_weight_bf16(P, A, f"l1WT{i}", W["l1_WT"][i * 128:(i + 1) * 128, :], [128, 1024], stage="wstage", side="right")
            load_weight_bf16(P, A, f"l1WpT{i}", W["l1_WpT"][i * 128:(i + 1) * 128, :], [128, 256], stage="wstage", side="right")
            load_weight_bf16(P, A, f"l1WdT{i}", W["l1_WdT"][i * 128:(i + 1) * 128, :], [128, 15], stage="wstage", side="right")
        load_weight_bf16(P, A, "l1WuT", W["l1_WuT"][:, :], [15, 1024], stage="wstage", side="right")
        bp = A.sb("l1bp", [128, 2], F32, side="right")
        P.emit("sync", lambda e: e.dma_start(out=bp[:], in_=W["l1_bp"][:, :]), writes=["l1bp"], dma=True)
        simf = A.sb("l1simf", [128, 8], F32, side="right")
        P.emit("sync", lambda e: e.dma_start(out=simf[:], in_=W["l1_sim"][:, :]), writes=["l1simf"], dma=True)
        tempf = A.sb("l1temp", [1, 1], F32, side="right")
        P.emit("sync", lambda e: e.dma_start(out=tempf[:], in_=W["l1_temp"][:, :]), writes=["l1temp"], dma=True)
        load_weight_bf16(P, A, "l1ea", W["l1_ea"][:, :], [4, 15], stage="wstage", side="right")
        A.sb("l1_xd", [15, 512], F32, side="right")
        A.sb("l1_xds", [15, 512], BF16, side="right")
        A.sb("l1_proj", [128, 1024], F32, side="right")
        A.sb("l1_projn", [128, 1024], BF16, side="right")
        A.sb("l1_psq", [128, 512], F32, side="right")
        A.sb("l1_rstd", [1, 512], F32, side="right")
        A.sb("l1_logit", [4, 512], F32, side="right")
        A.sb("l1_tokg", [128, 4], F32, side="right")
        A.sb("l1_retT", [4, 512], BF16, side="right")
        A.sb("l1_s4", [1, 1], F32, side="right")
        A.sb("l1_zero4", [128, 4], F32, side="right")
        A.sb("l1_nm", [128, 1], F32, side="right")
        A.sb("l1_sum", [128, 1], F32, side="right")
        A.sb("l1_rec", [128, 1], F32, side="right")
        A.sb("l1_mx2", [128, 1], F32, side="right")
        A.sb("l1_cum", [128, 4], F32, side="right")
        A.sb("l1_msk", [128, 4], F32, side="right")
        A.sb("l1_lw", [128, 4], F32, side="right")
        A.sb("l1_rettm", [128, 4], BF16, side="right")
        A.sb("l1_nrm", [1, 8], F32, side="right")
        z4 = A.t["l1_zero4"]
        P.emit("vector", lambda e: e.memset(z4[:], 0.0), writes=["l1_zero4"])
        for i in range(NDM):
            A.sb(f"l1xc{i}", [128, 512], BF16, side="right")

    simn = A.sb(f"l1simn{sfx}", [128, 8], BF16, side="right")
    ssq = A.t["l1_psq"]
    simf = A.t["l1simf"]
    tempf = A.t["l1temp"]
    s4 = A.t["l1_s4"]
    ph0, ph1, ph2, ph3 = A.t["ph0"], A.t["ph1"], A.t["ph2"], A.t["ph3"]
    phbf = A.t["phbf"]
    ocf = g["ones_col_f"]
    orf = g["ones_row_f"]
    nrm = A.t["l1_nrm"]

    P.emit("vector", lambda e: e.tensor_scalar(s4[:], tempf[:], CLAMP, None, op0=ALU.min),
           reads=["l1temp"], writes=["l1_s4"])
    P.emit("scalar", lambda e: e.activation(out=s4[:], in_=s4[:], func=AF.Exp),
           reads=["l1_s4"], writes=["l1_s4"])
    P.emit("scalar", lambda e: e.activation(out=ssq[:, 0:8], in_=simf[:], func=AF.Square),
           reads=["l1simf"], writes=["l1_psq"])
    P.emit("tensor", lambda e: e.matmul(ph2[0:1, 0:8], ocf[:], ssq[:, 0:8], start=True, stop=True),
           reads=["l1_psq", "ones_col_f"], writes=["ph2"])
    P.emit("vector", lambda e: e.tensor_copy(nrm[0:1, 0:8], ph2[0:1, 0:8]),
           reads=["ph2"], writes=["l1_nrm"])
    P.emit("vector", lambda e: e.tensor_tensor(nrm[0:1, 0:4], nrm[0:1, 0:4], nrm[0:1, 4:8], op=ALU.add),
           reads=["l1_nrm"], writes=["l1_nrm"])
    P.emit("vector", lambda e: e.tensor_scalar(nrm[0:1, 0:4], nrm[0:1, 0:4], 1e-24, None, op0=ALU.max),
           reads=["l1_nrm"], writes=["l1_nrm"])
    P.emit("scalar", lambda e: e.activation(out=nrm[0:1, 0:4], in_=nrm[0:1, 0:4], func=AF.Ln),
           reads=["l1_nrm"], writes=["l1_nrm"])
    P.emit("vector", lambda e: e.tensor_scalar(nrm[0:1, 0:4], nrm[0:1, 0:4], -0.5, None, op0=ALU.mult),
           reads=["l1_nrm"], writes=["l1_nrm"])
    P.emit("scalar", lambda e: e.activation(out=nrm[0:1, 0:4], in_=nrm[0:1, 0:4], func=AF.Exp),
           reads=["l1_nrm"], writes=["l1_nrm"])
    P.emit("vector", lambda e: e.tensor_copy(nrm[0:1, 4:8], nrm[0:1, 0:4]),
           reads=["l1_nrm"], writes=["l1_nrm"])
    P.emit("vector", lambda e: e.tensor_scalar(nrm[0:1, :], nrm[0:1, :], s4[:], None, op0=ALU.mult),
           reads=["l1_nrm"], sreads=["l1_s4"], writes=["l1_nrm"])
    P.emit("tensor", lambda e: e.matmul(ph2[:, 8:16], orf[0:1, :], nrm[0:1, :], start=True, stop=True),
           reads=["l1_nrm", "ones_row_f"], writes=["ph2"])
    P.emit("vector", lambda e: e.tensor_tensor(simn[:], simf[:], ph2[:, 8:16], op=ALU.mult),
           reads=["l1simf", "ph2"], writes=[f"l1simn{sfx}"])

    xlnc = [f"l1xc{i}" for i in range(NDM)]
    xd = A.t["l1_xd"]
    xds = A.t["l1_xds"]
    proj = A.t["l1_proj"]
    projn = A.t["l1_projn"]
    psq = A.t["l1_psq"]
    rstd = A.t["l1_rstd"]
    logit = A.t["l1_logit"]
    retT = A.t["l1_retT"]
    ea = A.t["l1ea"]
    bp = A.t["l1bp"]
    z4 = A.t["l1_zero4"]

    for tc in range(8):
        ln_chunk(P, A, g, xk, tc, xlnc)
        for i in range(NDM):
            wd = A.t[f"l1WdT{i}"]
            xc = A.t[xlnc[i]]
            P.emit("tensor", lambda e, wd=wd, xc=xc, st=(i == 0), sp=(i == NDM - 1):
                   e.matmul(ph1[0:15, 0:512], wd[:], xc[:], start=st, stop=sp),
                   reads=[f"l1WdT{i}", xlnc[i]], writes=["ph1"])
        P.emit("vector", lambda e: e.tensor_copy(xd[:], ph1[0:15, 0:512]),
               reads=["ph1"], writes=["l1_xd"])
        for pc in range(2):
            for i in range(NDM):
                wp = A.t[f"l1WpT{i}"]
                xc = A.t[xlnc[i]]
                P.emit("tensor", lambda e, wp=wp, xc=xc, pc=pc, st=(i == 0), sp=(i == NDM - 1):
                       e.matmul(ph2[:, 0:512], wp[:, pc * 128:(pc + 1) * 128], xc[:], start=st, stop=sp),
                       reads=[f"l1WpT{i}", xlnc[i]], writes=["ph2"])
            P.emit("vector", lambda e, pc=pc: e.tensor_scalar(
                proj[:, pc * 512:(pc + 1) * 512], ph2[:, 0:512], bp[:, pc:pc + 1], None, op0=ALU.add),
                reads=["ph2"], sreads=["l1bp"], writes=["l1_proj"])
        for pc in range(2):
            P.emit("scalar", lambda e, pc=pc: e.activation(out=psq[:], in_=proj[:, pc * 512:(pc + 1) * 512], func=AF.Square),
                   reads=["l1_proj"], writes=["l1_psq"])
            P.emit("tensor", lambda e, st=(pc == 0), sp=(pc == 1):
                   e.matmul(ph2[0:1, 0:512], ocf[:], psq[:], start=st, stop=sp),
                   reads=["l1_psq", "ones_col_f"], writes=["ph2"])
        P.emit("vector", lambda e: e.tensor_scalar(rstd[:], ph2[0:1, 0:512], 1e-24, None, op0=ALU.max),
               reads=["ph2"], writes=["l1_rstd"])
        P.emit("scalar", lambda e: e.activation(out=rstd[:], in_=rstd[:], func=AF.Ln),
               reads=["l1_rstd"], writes=["l1_rstd"])
        P.emit("vector", lambda e: e.tensor_scalar(rstd[:], rstd[:], -0.5, None, op0=ALU.mult),
               reads=["l1_rstd"], writes=["l1_rstd"])
        P.emit("scalar", lambda e: e.activation(out=rstd[:], in_=rstd[:], func=AF.Exp),
               reads=["l1_rstd"], writes=["l1_rstd"])
        P.emit("tensor", lambda e: e.matmul(ph2[:, 0:512], orf[0:1, :], rstd[0:1, :], start=True, stop=True),
               reads=["l1_rstd", "ones_row_f"], writes=["ph2"])
        for pc in range(2):
            P.emit("vector", lambda e, pc=pc: e.tensor_tensor(
                projn[:, pc * 512:(pc + 1) * 512], proj[:, pc * 512:(pc + 1) * 512], ph2[:, 0:512], op=ALU.mult),
                reads=["l1_proj", "ph2"], writes=["l1_projn"])
        for pc in range(2):
            P.emit("tensor", lambda e, pc=pc, st=(pc == 0), sp=(pc == 1):
                   e.matmul(ph2[0:4, 0:512], simn[:, pc * 4:(pc + 1) * 4], projn[:, pc * 512:(pc + 1) * 512], start=st, stop=sp),
                   reads=[f"l1simn{sfx}", "l1_projn"], writes=["ph2"])
        P.emit("vector", lambda e: e.tensor_copy(logit[:], ph2[0:4, 0:512]),
               reads=["ph2"], writes=["l1_logit"])
        tokg = A.t["l1_tokg"]
        nm, sm, rec, mx2 = A.t["l1_nm"], A.t["l1_sum"], A.t["l1_rec"], A.t["l1_mx2"]
        cum, msk, lw, rtm = A.t["l1_cum"], A.t["l1_msk"], A.t["l1_lw"], A.t["l1_rettm"]
        for sb in range(4):
            cs = slice(sb * 128, (sb + 1) * 128)
            P.emit("tensor", lambda e, cs=cs: e.transpose(ph2[:, 16:20], logit[:, cs], g["ident_f"][0:4, 0:4]),
                   reads=["l1_logit", "ident_f"], writes=["ph2"])
            P.emit("vector", lambda e: e.tensor_reduce(nm[:], ph2[:, 16:20], axis=AX.X, op=ALU.max, negate=True),
                   reads=["ph2"], writes=["l1_nm"])
            P.emit("scalar", lambda e: e.activation(out=tokg[:], in_=ph2[:, 16:20], func=AF.Exp, bias=nm[:]),
                   reads=["ph2"], sreads=["l1_nm"], writes=["l1_tokg"])
            P.emit("vector", lambda e: e.tensor_reduce(sm[:], tokg[:], axis=AX.X, op=ALU.add),
                   reads=["l1_tokg"], writes=["l1_sum"])
            P.emit("vector", lambda e: e.reciprocal(rec[:], sm[:]), reads=["l1_sum"], writes=["l1_rec"])
            P.emit("vector", lambda e: e.tensor_scalar(lw[:], tokg[:], rec[:], None, op0=ALU.mult),
                   reads=["l1_tokg"], sreads=["l1_rec"], writes=["l1_lw"])
            P.emit("vector", lambda e: e.tensor_reduce(mx2[:], lw[:], axis=AX.X, op=ALU.max),
                   reads=["l1_lw"], writes=["l1_mx2"])
            P.emit("vector", lambda e: e.tensor_scalar(msk[:], lw[:], mx2[:], None, op0=ALU.is_equal),
                   reads=["l1_lw"], sreads=["l1_mx2"], writes=["l1_msk"])
            P.emit("vector", lambda e: e.tensor_tensor_scan(cum[:], z4[:], msk[:], 0.0, op0=ALU.add, op1=ALU.add),
                   reads=["l1_zero4", "l1_msk"], writes=["l1_cum"])
            P.emit("vector", lambda e: e.tensor_scalar(cum[:], cum[:], 1.0, None, op0=ALU.is_equal),
                   reads=["l1_cum"], writes=["l1_cum"])
            P.emit("vector", lambda e: e.tensor_tensor(cum[:], cum[:], msk[:], op=ALU.mult),
                   reads=["l1_cum", "l1_msk"], writes=["l1_cum"])
            P.emit("vector", lambda e: e.tensor_tensor(rtm[:], cum[:], lw[:], op=ALU.mult),
                   reads=["l1_cum", "l1_lw"], writes=["l1_rettm"])
            if tc == 0 and sb == 0 and DEBUG_ON:
                for nm2, src2 in [("lw", lw), ("msk", msk), ("cum", cum), ("tokg", tokg), ("rtm", rtm)]:
                    cp = A.sb("dbg_" + nm2, [128, 4], F32)
                    P.emit("vector", lambda e, cp=cp, src2=src2: e.tensor_copy(cp[:], src2[:]),
                           reads=["l1_" + nm2 if nm2 != "rtm" else "l1_rettm"], writes=["dbg_" + nm2])
                    tap(P, A, nm2, "dbg_" + nm2, lambda cp=cp: cp[:])
            P.emit("tensor", lambda e: e.transpose(phbf[0:4, 0:128], rtm[:], g["ident_bf"][:]),
                   reads=["l1_rettm", "ident_bf"], writes=["phbf"])
            P.emit("vector", lambda e, cs=cs: e.tensor_copy(retT[0:4, cs], phbf[0:4, 0:128]),
                   reads=["phbf"], writes=["l1_retT"])
        P.emit("tensor", lambda e: e.matmul(ph1[0:15, 0:512], ea[:], retT[:], start=True, stop=True),
               reads=["l1ea", "l1_retT"], writes=["ph1"])
        P.emit("vector", lambda e: e.tensor_tensor(xds[:], ph1[0:15, 0:512], xd[:], op=ALU.mult),
               reads=["ph1", "l1_xd"], writes=["l1_xds"])
        if tc == 0:
            tap(P, A, "xd", "l1_xd", lambda: A.t["l1_xd"][:])
            tap(P, A, "xds", "l1_xds", lambda: A.t["l1_xds"][:])
            tap(P, A, "retT", "l1_retT", lambda: A.t["l1_retT"][:])
            tap(P, A, "logit", "l1_logit", lambda: A.t["l1_logit"][:])
            tap(P, A, "proj", "l1_proj", lambda: A.t["l1_proj"][:])
            tap(P, A, "projn", "l1_projn", lambda: A.t["l1_projn"][:])
            tap(P, A, "simn", f"l1simn{sfx}", lambda: A.t[f"l1simn{sfx}"][:])
            tap(P, A, "xln0", "l1xc0", lambda: A.t["l1xc0"][:])
        for oc in range(NDM):
            for i in range(NDM):
                wt = A.t[f"l1WT{i}"]
                xc = A.t[xlnc[i]]
                P.emit("tensor", lambda e, wt=wt, xc=xc, oc=oc, st=(i == 0):
                       e.matmul(ph0[:, 0:512], wt[:, oc * 128:(oc + 1) * 128], xc[:], start=st, stop=False),
                       reads=[f"l1WT{i}", xlnc[i]], writes=["ph0"])
            wu = A.t["l1WuT"]
            P.emit("tensor", lambda e, oc=oc: e.matmul(ph0[:, 0:512], wu[:, oc * 128:(oc + 1) * 128], xds[:], start=False, stop=True),
                   reads=["l1WuT", "l1_xds"], writes=["ph0"])
            xo = A.t[xk[oc]]
            s = slice(tc * 512, (tc + 1) * 512)
            P.emit("vector", lambda e, xo=xo, s=s: e.scalar_tensor_tensor(
                xo[:, s], ph0[:, 0:512], 0.5, xo[:, s], op0=ALU.mult, op1=ALU.add),
                reads=["ph0", xk[oc]], writes=[xk[oc]])
    P.fence()
    A.free_to(mkF)


def prep_l1ffn_weights(inputs):
    W = {}
    W["l1_WT"] = np.ascontiguousarray(inputs["L1_ffn_W"].T)
    W["l1_WpT"] = np.ascontiguousarray(inputs["L1_ffn_Wp"].T)
    W["l1_bp"] = np.ascontiguousarray(np.asarray(inputs["L1_ffn_bp"]).reshape(2, 128).T)
    W["l1_sim"] = np.ascontiguousarray(
        np.asarray(inputs["L1_ffn_sim"]).reshape(2, 128, 4).transpose(1, 0, 2).reshape(128, 8))
    W["l1_temp"] = np.ascontiguousarray(np.asarray(inputs["L1_ffn_temp"]).reshape(1, 1))
    wd = np.concatenate([np.asarray(inputs[f"L1_ffn_Wd{i}"]) for i in range(4)], axis=0)
    W["l1_WdT"] = np.ascontiguousarray(wd.T)
    wu = np.concatenate([np.asarray(inputs[f"L1_ffn_Wu{i}"]) for i in range(4)], axis=1)
    W["l1_WuT"] = np.ascontiguousarray(wu.T)
    ea = np.zeros((4, 15), np.float32)
    for e_i, (a, b) in enumerate([(0, 1), (1, 3), (3, 7), (7, 15)]):
        ea[e_i, a:b] = 1.0
    W["l1_ea"] = ea
    return W


L1FFN_W_SHAPES = dict(
    l1_WT=[1024, 1024], l1_WpT=[1024, 256], l1_bp=[128, 2], l1_sim=[128, 8],
    l1_temp=[1, 1], l1_WdT=[1024, 15], l1_WuT=[15, 1024], l1_ea=[4, 15])


def finalize(P, nc, A):
    with nc.Block() as block, ExitStack() as semstack:
        sems = {}
        for eng in ENGINES:
            for kk in ("c", "d"):
                sems[(eng, kk)] = semstack.enter_context(nc.semaphore(f"s_{eng}_{kk}"))
        P.materialize(block, sems)


def build_sl4_standalone():
    nc = bass.Bass("TRN2", target_bir_lowering=False)
    xin = nc.dram_tensor("xT", [D_MODEL, SEQ], F32, kind="ExternalInput").ap()
    xout = nc.dram_tensor("yT", [D_MODEL, SEQ], F32, kind="ExternalOutput").ap()
    Wd = {k: nc.dram_tensor(k, sh, F32, kind="ExternalInput").ap()
          for k, sh in L1FFN_W_SHAPES.items()}
    with ExitStack() as stack:
        A = Alloc(nc, stack)
        P = Prog(nc)
        g = build_common(P, A)
        xk = [f"x{i}" for i in range(NDM)]
        for i in range(NDM):
            x = A.sb(xk[i], [128, SEQ], F32)
            P.emit("sync", lambda e, x=x, i=i: e.dma_start(out=x[:], in_=xin[i * 128:(i + 1) * 128, :]),
                   writes=[xk[i]], dma=True)
        for pn in ["ph0", "ph1", "ph2", "ph3", "ph4"]:
            A.ps(pn, [128, 512], F32)
        A.ps("phbf", [128, 1024], BF16)
        sl_l1ffn(P, A, g, Wd, xk, first_call=True)
        for nm, tk in DEBUG_TAPS:
            t = A.t[tk]
            dt = nc.dram_tensor("tap_" + nm, list(t[:].shape), t[:].dtype, kind="ExternalOutput").ap()
            P.emit("gpsimd", lambda e, dt=dt, t=t: e.dma_start(out=dt, in_=t[:]),
                   reads=[tk], writes=["tapdma_" + nm], dma=True)
        for i in range(NDM):
            x = A.t[xk[i]]
            P.emit("gpsimd", lambda e, x=x, i=i: e.dma_start(out=xout[i * 128:(i + 1) * 128, :], in_=x[:]),
                   reads=[xk[i]], writes=[f"outdma{i}"], dma=True)
        P.barrier_read("gpsimd", [f"outdma{i}" for i in range(NDM)])
        finalize(P, nc, A)
        print(f"SBUF bytes: {A.bytes/1e6:.2f}MB, ops: {len(P.ops)}", flush=True)
    return nc




def cast_to_hbm_blocked(P, A, name, src_ap, rows, cols, dst_ap, stage):
    st_ = A.t[stage]
    bsn = stage + "_bf"
    bs = A.t[bsn]
    for rc in range(rows // 128):
        for cc in range(cols // 1024):
            P.emit("sync", lambda e, rc=rc, cc=cc: e.dma_start(
                out=st_[:, 0:1024], in_=src_ap[rc * 128:(rc + 1) * 128, cc * 1024:(cc + 1) * 1024]),
                writes=[stage], dma=True)
            P.emit("scalar", lambda e: e.activation(out=bs[:, 0:1024], in_=st_[:, 0:1024], func=AF.Copy),
                   reads=[stage], writes=[bsn])
            for j in range(8):
                mc = cc * 8 + j
                P.emit("gpsimd", lambda e, mc=mc, rc=rc, j=j: e.dma_start(
                    out=dst_ap[mc, rc, :, :], in_=bs[:, j * 128:(j + 1) * 128]),
                    reads=[bsn], writes=[name + "_blk"], dma=True)


def sl_l0attn(P, A, g, W, xk):
    nc = P.nc
    NH = 16
    ph0, ph1, ph2, ph3, ph4, ph5 = [A.t[f"ph{i}"] for i in range(6)]
    ocf, orf = g["ones_col_f"], g["ones_row_f"]
    kind = "ExternalOutput" if DEBUG_HBM else "Internal"
    hq = nc.dram_tensor("hbm_q", [1024, SEQ], BF16, kind=kind).ap()
    hk = nc.dram_tensor("hbm_k", [1024, SEQ], BF16, kind=kind).ap()
    hv = nc.dram_tensor("hbm_v", [32, 16, 128, 64], BF16, kind=kind).ap()
    havn = nc.dram_tensor("hbm_avn", [1024, SEQ], BF16, kind=kind).ap()
    if DEBUG_HBM:
        hpodbg = nc.dram_tensor("hbm_podbg", [8, 8, 128, 512], BF16, kind="ExternalOutput").ap()
        hpsdbg = nc.dram_tensor("hbm_psdbg", [8, 8, 128, 512], F32, kind="ExternalOutput").ap()
    else:
        hpodbg = hpsdbg = None
    hwq = nc.dram_tensor("hbm_wq", [24, 8, 128, 128], BF16).ap()
    hwqs = nc.dram_tensor("hbm_wqs", [16, 8, 128, 128], BF16).ap()
    if "wstage1k" not in A.t:
        A.sb("wstage1k", [128, 1024], F32)
        A.sb("wstage1k_bf", [128, 1024], BF16)
    pa = A.sb("att_pa_sb", [1, 32], F32, side="right")
    P.emit("sync", lambda e: e.dma_start(out=pa[:], in_=W["att_pa"][:, :]), writes=["att_pa"], dma=True)
    P.emit("vector", lambda e: e.tensor_scalar(pa[0:1, 0:32], pa[0:1, 0:32], 0.01, None, op0=ALU.max),
           reads=["att_pa"], writes=["att_pa"])
    P.emit("vector", lambda e: e.tensor_scalar(pa[0:1, 0:16], pa[0:1, 0:16], -1.0, None, op0=ALU.mult),
           reads=["att_pa"], writes=["att_pa"])
    fr = A.sb("att_fr_sb", [128, 1], F32, side="right")
    P.emit("sync", lambda e: e.dma_start(out=fr[:], in_=W["att_freqs"][:, :]), writes=["att_fr"], dma=True)
    sg = A.sb("att_sg_sb", [128, 1], F32, side="right")
    P.emit("sync", lambda e: e.dma_start(out=sg[:], in_=W["att_sgn"][:, :]), writes=["att_sg"], dma=True)
    cast_to_hbm_blocked(P, A, "awq", W["att_qkvT"], 1024, 3072, hwq, "wstage1k")
    cast_to_hbm_blocked(P, A, "awqs", W["att_qkvTs"], 1024, 2048, hwqs, "wstage1k")
    mkA = A.mark()
    crep = A.sb("att_crep", [128, SEQ], BF16, side="right")
    srep = A.sb("att_srep", [128, SEQ], BF16, side="right")
    angi = A.sb("att_angi", [128, 512], I32, side="right")
    tmpa = A.sb("att_tmpa", [128, 512], F32, side="right")
    tmpb = A.sb("att_tmpb", [128, 512], F32, side="right")
    for i in range(NDM):
        A.sb(f"att_xc{i}", [128, 512], BF16, side="right")
    for pp in range(2):
        for i in range(NDM):
            A.sb(f"att_wc{pp}_{i}", [128, 128], BF16, side="right")
            A.sb(f"att_ws{pp}_{i}", [128, 128], BF16, side="right")
    for pp in range(2):
        A.sb(f"att_vwc{pp}", [128, 512], BF16, side="right")
    ro1 = A.sb("att_ro1", [128, 512], F32, side="right")
    ro2 = A.sb("att_ro2", [128, 512], F32, side="right")
    qkb = A.sb("att_qk", [128, 512], BF16, side="right")
    vsb = A.sb("att_vsb", [128, 512], BF16, side="right")
    for tc in range(8):
        for (dst, dk, phase) in ((srep, "att_srep", 0.0), (crep, "att_crep", math.pi / 2)):
            P.emit("gpsimd", lambda e, tc=tc: e.iota(angi[:], pattern=[[1, 512]], base=tc * 512, channel_multiplier=0),
                   writes=["att_angi"])
            P.emit("vector", lambda e: e.tensor_copy(tmpa[:], angi[:]), reads=["att_angi"], writes=["att_tmpa"])
            P.emit("vector", lambda e: e.tensor_scalar(tmpa[:], tmpa[:], fr[:], None, op0=ALU.mult),
                   reads=["att_tmpa"], sreads=["att_fr"], writes=["att_tmpa"])
            P.emit("vector", lambda e, phase=phase: e.tensor_scalar(
                tmpa[:], tmpa[:], 1.0 / (2 * math.pi), phase / (2 * math.pi), op0=ALU.mult, op1=ALU.add),
                reads=["att_tmpa"], writes=["att_tmpa"])
            P.emit("vector", lambda e: e.tensor_copy(angi[:], tmpa[:]), reads=["att_tmpa"], writes=["att_angi"])
            P.emit("vector", lambda e: e.tensor_copy(tmpb[:], angi[:]), reads=["att_angi"], writes=["att_tmpb"])
            P.emit("vector", lambda e: e.tensor_tensor(tmpa[:], tmpa[:], tmpb[:], op=ALU.subtract),
                   reads=["att_tmpa", "att_tmpb"], writes=["att_tmpa"])
            P.emit("vector", lambda e: e.tensor_scalar(tmpb[:], tmpa[:], 0.0, None, op0=ALU.is_lt),
                   reads=["att_tmpa"], writes=["att_tmpb"])
            P.emit("vector", lambda e: e.tensor_tensor(tmpa[:], tmpa[:], tmpb[:], op=ALU.add),
                   reads=["att_tmpa", "att_tmpb"], writes=["att_tmpa"])
            P.emit("vector", lambda e: e.tensor_scalar(tmpa[:], tmpa[:], 2 * math.pi, None, op0=ALU.mult),
                   reads=["att_tmpa"], writes=["att_tmpa"])
            P.emit("vector", lambda e: e.tensor_scalar(tmpb[:], tmpa[:], math.pi, None, op0=ALU.is_gt),
                   reads=["att_tmpa"], writes=["att_tmpb"])
            P.emit("vector", lambda e: e.scalar_tensor_tensor(tmpa[:], tmpb[:], -2 * math.pi, tmpa[:],
                                                              op0=ALU.mult, op1=ALU.add),
                   reads=["att_tmpb", "att_tmpa"], writes=["att_tmpa"])
            P.emit("scalar", lambda e, dst=dst, tc=tc: e.activation(out=dst[:, tc * 512:(tc + 1) * 512], in_=tmpa[:], func=AF.Sin),
                   reads=["att_tmpa"], writes=[dk])
    P.emit("vector", lambda e: e.tensor_scalar(srep[:], srep[:], sg[:], None, op0=ALU.mult),
           reads=["att_srep"], sreads=["att_sg"], writes=["att_srep"])
    xlnc = [f"att_xc{i}" for i in range(NDM)]
    for tc in range(8):
        s = slice(tc * 512, (tc + 1) * 512)
        ln_chunk(P, A, g, xk, tc, xlnc)
        for mc in range(16):
            pp = mc % 2
            for i in range(NDM):
                wc = A.t[f"att_wc{pp}_{i}"]
                ws = A.t[f"att_ws{pp}_{i}"]
                P.emit("sync", lambda e, wc=wc, mc=mc, i=i: e.dma_start(out=wc[:], in_=hwq[mc, i, :, :]),
                       reads=["awq_blk"], writes=[f"att_wc{pp}_{i}"], dma=True)
                P.emit("gpsimd", lambda e, ws=ws, mc=mc, i=i: e.dma_start(out=ws[:], in_=hwqs[mc, i, :, :]),
                       reads=["awqs_blk"], writes=[f"att_ws{pp}_{i}"], dma=True)
            for i in range(NDM):
                wc = A.t[f"att_wc{pp}_{i}"]
                ws = A.t[f"att_ws{pp}_{i}"]
                xc = A.t[xlnc[i]]
                P.emit("tensor", lambda e, wc=wc, xc=xc, st=(i == 0), sp=(i == NDM - 1):
                       e.matmul(ph0[:, 0:512], wc[:], xc[:], start=st, stop=sp),
                       reads=[f"att_wc{pp}_{i}", xlnc[i]], writes=["ph0"])
                P.emit("tensor", lambda e, ws=ws, xc=xc, st=(i == 0), sp=(i == NDM - 1):
                       e.matmul(ph1[:, 0:512], ws[:], xc[:], start=st, stop=sp),
                       reads=[f"att_ws{pp}_{i}", xlnc[i]], writes=["ph1"])
            scl = 0.125 if mc < 8 else 1.0
            P.emit("vector", lambda e, s=s, scl=scl: e.scalar_tensor_tensor(
                ro1[:], ph0[:, 0:512], scl, crep[:, s], op0=ALU.mult, op1=ALU.mult),
                reads=["ph0", "att_crep"], writes=["att_ro1"])
            P.emit("vector", lambda e, s=s, scl=scl: e.scalar_tensor_tensor(
                ro2[:], ph1[:, 0:512], scl, srep[:, s], op0=ALU.mult, op1=ALU.mult),
                reads=["ph1", "att_srep"], writes=["att_ro2"])
            P.emit("vector", lambda e: e.tensor_tensor(qkb[:], ro1[:], ro2[:], op=ALU.add),
                   reads=["att_ro1", "att_ro2"], writes=["att_qk"])
            dst = hq if mc < 8 else hk
            row = (mc % 8) * 128
            P.emit("gpsimd", lambda e, dst=dst, row=row, s=s: e.dma_start(
                out=dst[row:row + 128, s], in_=qkb[:]),
                reads=["att_qk"], writes=["hbm_qk"], dma=True)
        for vc in range(2):
            for i in range(NDM):
                pp = i % 2
                vw = A.t[f"att_vwc{pp}"]
                P.emit("sync", lambda e, vw=vw, vc=vc, i=i: e.dma_start(
                    out=vw[:].rearrange("p (j c) -> p j c", j=4),
                    in_=hwq[16 + vc * 4:16 + vc * 4 + 4, i, :, :].rearrange("j p c -> p j c")),
                    reads=["awq_blk"], writes=[f"att_vwc{pp}"], dma=True)
                for sub in range(4):
                    xc = A.t[xlnc[i]]
                    P.emit("tensor", lambda e, vw=vw, xc=xc, sub=sub, st=(i == 0), sp=(i == NDM - 1):
                           e.matmul(ph2[:, 0:512] if sub == 0 else (ph4[:, 0:512] if sub == 1 else (ph5[:, 0:512] if sub == 2 else ph3[:, 0:512])),
                                    xc[:, sub * 128:(sub + 1) * 128], vw[:], start=st, stop=sp),
                           reads=[f"att_vwc{pp}", xlnc[i]],
                           writes=[["ph2", "ph4", "ph5", "ph3"][sub]])
            for sub in range(4):
                ktc = tc * 4 + sub
                pht_v = [ph2, ph4, ph5, ph3][sub]
                pk = ["ph2", "ph4", "ph5", "ph3"][sub]
                P.emit("scalar", lambda e, pht_v=pht_v: e.activation(out=vsb[:], in_=pht_v[:, 0:512], func=AF.Copy),
                       reads=[pk], writes=["att_vsb"])
                for j in range(8):
                    h = vc * 8 + j
                    P.emit("sync" if j % 2 == 0 else "gpsimd",
                           lambda e, ktc=ktc, h=h, j=j: e.dma_start(
                               out=hv[ktc, h, :, :], in_=vsb[:, j * 64:(j + 1) * 64]),
                           reads=["att_vsb"], writes=["hbm_v"], dma=True)
    P.fence()
    A.free_to(mkA)
    mkB = A.mark()
    kt = A.sb("att_kt", [64, SEQ], BF16, side="right")
    qt = A.sb("att_qt", [64, 512], BF16, side="right")
    dh = A.sb("att_dh", [128, 2016], I32, side="right")
    et = A.sb("att_et", [128, 2016], F32, side="right")
    ekb = A.sb("att_ekb", [128, 8064], BF16, side="right")
    at0 = A.sb("att_at0", [128, 512], BF16, side="right")
    at1 = A.sb("att_at1", [128, 512], BF16, side="right")
    at2 = A.sb("att_at2", [128, 512], BF16, side="right")
    at3 = A.sb("att_at3", [128, 512], BF16, side="right")
    v0 = A.sb("att_v0", [128, 65], BF16, side="right")
    v1 = A.sb("att_v1", [128, 65], BF16, side="right")
    rec = A.sb("att_rec", [128, 512], F32, side="right")
    avs = A.sb("att_avs", [64, 512], F32, side="right")
    avn0 = A.sb("att_avn0", [64, 512], BF16, side="right")
    avn1 = A.sb("att_avn1", [64, 512], BF16, side="right")
    P.emit("vector", lambda e: e.memset(v0[:, 64:65], 1.0), writes=["att_v0"])
    P.emit("vector", lambda e: e.memset(v1[:, 64:65], 1.0), writes=["att_v1"])
    ats = [at0, at1]
    ats2 = [at2, at3]
    vts = [v0, v1]
    avns = [avn0, avn1]
    for h in range(NH):
        P.emit("sync", lambda e, h=h: e.dma_start(out=kt[:], in_=hk[h * 64:(h + 1) * 64, :]),
               reads=["hbm_qk"], writes=["att_kt"], dma=True)
        P.emit("tensor", lambda e, h=h: e.matmul(ph3[:, 0:1], orf[0:1, :], pa[0:1, 16 + h:17 + h], start=True, stop=True),
               reads=["att_pa", "ones_row_f"], writes=["ph3"])
        P.emit("tensor", lambda e, h=h: e.matmul(ph3[:, 1:2], orf[0:1, :], pa[0:1, h:h + 1], start=True, stop=True),
               reads=["att_pa", "ones_row_f"], writes=["ph3"])
        for qq in range(4):
            base = -31 * 128 + qq * 2016
            P.emit("gpsimd", lambda e, base=base: e.iota(dh[:], pattern=[[1, 2016]], base=base, channel_multiplier=-1),
                   writes=["att_dh"])
            P.emit("scalar", lambda e: e.activation(out=et[:], in_=dh[:], func=AF.Abs),
                   reads=["att_dh"], writes=["att_et"])
            P.emit("vector", lambda e: e.tensor_scalar(et[:], et[:], ph3[:, 0:1], None, op0=ALU.mult),
                   reads=["att_et"], sreads=["ph3"], writes=["att_et"])
            P.emit("scalar", lambda e: e.activation(out=et[:], in_=et[:], func=AF.Ln, bias=1.0),
                   reads=["att_et"], writes=["att_et"])
            P.emit("vector", lambda e: e.tensor_scalar(et[:], et[:], ph3[:, 1:2], None, op0=ALU.mult),
                   reads=["att_et"], sreads=["ph3"], writes=["att_et"])
            P.emit("scalar", lambda e, qq=qq: e.activation(out=ekb[:, qq * 2016:(qq + 1) * 2016], in_=et[:], func=AF.Exp),
                   reads=["att_et"], writes=["att_ekb"])
        for qc in range(8):
            P.emit("sync", lambda e, h=h, qc=qc: e.dma_start(out=qt[:], in_=hq[h * 64:(h + 1) * 64, qc * 512:(qc + 1) * 512]),
                   reads=["hbm_qk"], writes=["att_qt"], dma=True)
            for ktc in range(32):
                pb = "ph0" if ktc % 2 == 0 else "ph1"
                pbt = ph0 if ktc % 2 == 0 else ph1
                ab = ktc % 2
                P.emit("tensor", lambda e, ktc=ktc, pbt=pbt: e.matmul(
                    pbt[:, 0:512], kt[:, ktc * 128:(ktc + 1) * 128], qt[:], start=True, stop=True),
                    reads=["att_kt", "att_qt"], writes=[pb])
                P.emit("scalar", lambda e, pbt=pbt, ab=ab: e.activation(out=ats[ab][:], in_=pbt[:, 0:512], func=AF.Exp),
                       reads=[pb], writes=[f"att_at{ab}"])
                off = (4 * qc - ktc + 31) * 128
                P.emit("vector", lambda e, ab=ab, off=off: e.tensor_tensor(
                    ats2[ab][:], ats[ab][:], ekb[:, off:off + 512], op=ALU.mult),
                    reads=[f"att_at{ab}", "att_ekb"], writes=[f"att_at2{ab}" if False else f"att_at{ab + 2}"])
                P.emit("sync" if ab == 0 else "gpsimd",
                       lambda e, ktc=ktc, h=h, ab=ab: e.dma_start(out=vts[ab][:, 0:64], in_=hv[ktc, h, :, :]),
                       reads=["hbm_v"], writes=[f"att_v{ab}"], dma=True)
                P.emit("tensor", lambda e, ab=ab, st=(ktc == 0), sp=(ktc == 31): e.matmul(
                    ph2[0:65, 0:512], vts[ab][:], ats2[ab][:], start=st, stop=sp),
                    reads=[f"att_v{ab}", f"att_at{ab + 2}"], writes=["ph2"])
            P.emit("vector", lambda e: e.reciprocal(rec[64:65, :], ph2[64:65, 0:512]),
                   reads=["ph2"], writes=["att_rec"])
            P.emit("tensor", lambda e: e.matmul(ph3[0:64, 0:512], g["ones_all_f"][64:65, 0:64], rec[64:65, :], start=True, stop=True),
                   reads=["att_rec", "ones_all_f"], writes=["ph3"])
            P.emit("vector", lambda e: e.tensor_copy(avs[:], ph2[0:64, 0:512]),
                   reads=["ph2"], writes=["att_avs"])
            ab2 = qc % 2
            avn = avns[ab2]
            P.emit("vector", lambda e, avn=avn: e.tensor_tensor(
                avn[:, :], avs[:], ph3[0:64, 0:512], op=ALU.mult),
                reads=["att_avs", "ph3"], writes=[f"att_avn{ab2}"])
            P.emit("gpsimd", lambda e, avn=avn, h=h, qc=qc: e.dma_start(
                out=havn[h * 64:(h + 1) * 64, qc * 512:(qc + 1) * 512], in_=avn[:, :]),
                reads=[f"att_avn{ab2}"], writes=["havn"], dma=True)
    P.fence()
    A.free_to(mkB)
    mkC = A.mark()
    for i in range(NDM):
        load_weight_bf16(P, A, f"att_wo{i}", W["att_outT"][i * 128:(i + 1) * 128, :], [128, 1024],
                         stage="wstage1k", side="right")
    for i in range(8):
        A.sb(f"att_po{i}", [128, 512], BF16, side="right")
    pos = [A.t[f"att_po{i}"] for i in range(8)]
    for qc in range(8):
        for hc in range(NDM):
            P.emit("sync", lambda e, hc=hc, qc=qc: e.dma_start(
                out=pos[hc][:, :], in_=havn[hc * 128:(hc + 1) * 128, qc * 512:(qc + 1) * 512]),
                reads=["havn"], writes=[f"att_po{hc}"], dma=True)
            if DEBUG_HBM:
                P.emit("gpsimd", lambda e, hc=hc, qc=qc: e.dma_start(
                    out=hpodbg[qc, hc, :, :], in_=pos[hc][:, :]),
                    reads=[f"att_po{hc}"], writes=["podbg"], dma=True)
        for dmc in range(NDM):
            pb = "ph4" if dmc % 2 == 0 else "ph5"
            pht_ = ph4 if dmc % 2 == 0 else ph5
            for hc in range(NDM):
                wo = A.t[f"att_wo{hc}"]
                P.emit("tensor", lambda e, wo=wo, hc=hc, dmc=dmc, pht_=pht_, st=(hc == 0), sp=(hc == 7):
                       e.matmul(pht_[:, 0:512], wo[:, dmc * 128:(dmc + 1) * 128], pos[hc][:, :], start=st, stop=sp),
                       reads=[f"att_wo{hc}", f"att_po{hc}"], writes=[pb])
            if DEBUG_HBM:
                dbge = A.t["att_rec2"] if "att_rec2" in A.t else A.sb("att_rec2", [128, 512], F32, side="right")
                P.emit("scalar", lambda e, pht_=pht_: e.activation(out=dbge[:], in_=pht_[:, 0:512], func=AF.Copy),
                       reads=[pb], writes=["att_rec2"])
                P.emit("gpsimd", lambda e, qc=qc, dmc=dmc, dbge=dbge: e.dma_start(
                    out=hpsdbg[qc, dmc, :, :], in_=dbge[:]),
                    reads=["att_rec2"], writes=["psdbg"], dma=True)
            xo = A.t[xk[dmc]]
            s = slice(qc * 512, (qc + 1) * 512)
            if DEBUG_HBM:
                P.emit("vector", lambda e, xo=xo, s=s, dbge=dbge: e.tensor_tensor(xo[:, s], xo[:, s], dbge[:], op=ALU.add),
                       reads=["att_rec2", xk[dmc]], writes=[xk[dmc]])
            else:
                P.emit("vector", lambda e, xo=xo, s=s, pht_=pht_: e.tensor_tensor(xo[:, s], xo[:, s], pht_[:, 0:512], op=ALU.add),
                       reads=[pb, xk[dmc]], writes=[xk[dmc]])
    P.fence()
    A.free_to(mkC)


def prep_l0attn_weights(inputs):
    W = {}
    qkv = np.asarray(inputs["L0_attn_qkv_w"])
    W["att_qkvT"] = np.ascontiguousarray(qkv.T)
    perm = np.arange(2048)
    for h in range(32):
        base = h * 64
        perm[base:base + 64] = np.concatenate([np.arange(base + 32, base + 64), np.arange(base, base + 32)])
    W["att_qkvTs"] = np.ascontiguousarray(qkv.T[:, :2048][:, perm])
    W["att_outT"] = np.ascontiguousarray(np.asarray(inputs["L0_attn_out_w"]).T)
    W["att_freqs"] = np.ascontiguousarray(np.tile(np.asarray(inputs["L0_attn_rope_freqs"]), 4).reshape(128, 1))
    sgn = np.tile(np.concatenate([-np.ones(32, np.float32), np.ones(32, np.float32)]), 2).reshape(128, 1)
    W["att_sgn"] = np.ascontiguousarray(sgn)
    W["att_pa"] = np.ascontiguousarray(np.concatenate(
        [np.asarray(inputs["L0_attn_bias_p"]).reshape(16), np.asarray(inputs["L0_attn_bias_a"]).reshape(16)]).reshape(1, 32))
    return W


ATTN_W_SHAPES = dict(
    att_qkvT=[1024, 3072], att_qkvTs=[1024, 2048], att_outT=[1024, 1024],
    att_freqs=[128, 1], att_sgn=[128, 1], att_pa=[1, 32])


def build_sl2_standalone():
    nc = bass.Bass("TRN2", target_bir_lowering=False)
    xin = nc.dram_tensor("xT", [D_MODEL, SEQ], F32, kind="ExternalInput").ap()
    xout = nc.dram_tensor("yT", [D_MODEL, SEQ], F32, kind="ExternalOutput").ap()
    Wd = {k: nc.dram_tensor(k, sh, F32, kind="ExternalInput").ap()
          for k, sh in ATTN_W_SHAPES.items()}
    with ExitStack() as stack:
        A = Alloc(nc, stack)
        P = Prog(nc)
        g = build_common(P, A)
        xkeys = [f"x{i}" for i in range(NDM)]
        for i in range(NDM):
            x = A.sb(xkeys[i], [128, SEQ], F32)
            P.emit("sync", lambda e, x=x, i=i: e.dma_start(out=x[:], in_=xin[i * 128:(i + 1) * 128, :]),
                   writes=[xkeys[i]], dma=True)
        for pn in ["ph0", "ph1", "ph2", "ph3", "ph4", "ph5"]:
            A.ps(pn, [128, 512], F32)
        sl_l0attn(P, A, g, Wd, xkeys)
        for i in range(NDM):
            x = A.t[xkeys[i]]
            P.emit("gpsimd", lambda e, x=x, i=i: e.dma_start(out=xout[i * 128:(i + 1) * 128, :], in_=x[:]),
                   reads=[xkeys[i]], writes=[f"outdma{i}"], dma=True)
        P.barrier_read("gpsimd", [f"outdma{i}" for i in range(NDM)])
        finalize(P, nc, A)
        print(f"SBUF bytes: {A.bytes/1e6:.2f}MB, ops: {len(P.ops)}", flush=True)
    return nc




def sl_l0ffn(P, A, g, W, xk, first_call=True):
    nc = P.nc
    E = 8
    CAP = 512
    NSC = CAP // 128
    ph = [A.t[f"ph{i}"] for i in range(6)]
    phbf = A.t["phbf"]
    ocf, orf = g["ones_col_f"], g["ones_row_f"]
    sfx = "a" if first_call else "b"
    if first_call:
        A.t["_moe_dram"] = dict(
            w1b=nc.dram_tensor("hbm_w1b", [8, 32, 8, 128, 128], BF16).ap(),
            w2b=nc.dram_tensor("hbm_w2b", [8, 8, 32, 128, 128], BF16).ap(),
            pos_d=nc.dram_tensor("moe_pos_d", [8, SEQ], F32).ap(),
            lnst=nc.dram_tensor("hbm_lnst_a", [SEQ, 1024], BF16).ap(),
            lnst_b=nc.dram_tensor("hbm_lnst_b", [SEQ, 1024], BF16).ap(),
            cin=nc.dram_tensor("moe_cin_a", [8, SEQ], F32).ap(),
            cout=nc.dram_tensor("moe_cout_a", [64, SEQ], F32).ap(),
            cin_b=nc.dram_tensor("moe_cin_b", [8, SEQ], F32).ap(),
            cout_b=nc.dram_tensor("moe_cout_b", [64, SEQ], F32).ap(),
        )
        md0 = A.t["_moe_dram"]
        md0["b2bf"] = nc.dram_tensor("hbm_b2bf", [8, 1024], BF16).ap()
        P.emit("sync", lambda e: e.dma_start(out=A.t["wstage1k"][0:8, 0:1024], in_=W["moe_b2"][:, :]),
               writes=["wstage1k"], dma=True)
        P.emit("scalar", lambda e: e.activation(out=A.t["wstage1k_bf"][0:8, 0:1024], in_=A.t["wstage1k"][0:8, 0:1024], func=AF.Copy),
               reads=["wstage1k"], writes=["wstage1k_bf"])
        P.emit("gpsimd", lambda e: e.dma_start(out=md0["b2bf"][:, :], in_=A.t["wstage1k_bf"][0:8, 0:1024]),
               reads=["wstage1k_bf"], writes=["hbm_b2bf"], dma=True)
        for e_ in range(E):
            cast_to_hbm_blocked(P, A, f"mw1_{e_}", W["moe_w1"][e_, :, :], 1024, 4096,
                                A.t["_moe_dram"]["w1b"][e_], "wstage1k")
            cast_to_hbm_blocked(P, A, f"mw2_{e_}", W["moe_w2"][e_, :, :], 4096, 1024,
                                A.t["_moe_dram"]["w2b"][e_], "wstage1k")
        rw = A.sb("moe_rw_s", [128, 64], F32, side="right")
        P.emit("sync", lambda e: e.dma_start(out=rw[:], in_=W["moe_rw"][:, :]), writes=["moe_rw_s"], dma=True)
        rb = A.sb("moe_rb_s", [1, 8], F32, side="right")
        P.emit("sync", lambda e: e.dma_start(out=rb[:], in_=W["moe_rb"][:, :]), writes=["moe_rb_s"], dma=True)
        cwrow = A.sb("moe_cwr", [1, 8], F32, side="right")
        rbcol = A.sb("moe_rbc", [8, 1], F32, side="right")
        ph2_ = A.t["ph2"]
        for i in range(NDM):
            P.emit("tensor", lambda e, i=i, st=(i == 0), sp=(i == NDM - 1): e.matmul(
                ph2_[0:8, 40:41], rw[:, i * 8:(i + 1) * 8], A.t["ones_col_f"][:], start=st, stop=sp),
                reads=["moe_rw_s", "ones_col_f"], writes=["ph2"])
        cwc = A.sb("moe_cwc", [8, 1], F32, side="right")
        P.emit("vector", lambda e: e.tensor_scalar(cwc[:], ph2_[0:8, 40:41], -1.0, None, op0=ALU.mult),
               reads=["ph2"], writes=["moe_cwc"])
        P.emit("tensor", lambda e: e.transpose(ph2_[0:1, 44:52], cwc[:], A.t["ident_f"][0:8, 0:8]),
               reads=["moe_cwc", "ident_f"], writes=["ph2"])
        P.emit("vector", lambda e: e.tensor_copy(cwrow[:], ph2_[0:1, 44:52]), reads=["ph2"], writes=["moe_cwr"])
        P.emit("tensor", lambda e: e.transpose(ph2_[0:8, 52:53], rb[:], A.t["ident_f"][0:1, 0:1]),
               reads=["moe_rb_s", "ident_f"], writes=["ph2"])
        P.emit("vector", lambda e: e.tensor_copy(rbcol[:], ph2_[0:8, 52:53]), reads=["ph2"], writes=["moe_rbc"])
        lmat = A.sb("moe_L_s", [128, 128], F32, side="right")
        P.emit("sync", lambda e: e.dma_start(out=lmat[:], in_=W["moe_L"][:, :]), writes=["moe_L_s"], dma=True)
        smat = A.sb("moe_S_s", [128, 8], F32, side="right")
        P.emit("sync", lambda e: e.dma_start(out=smat[:], in_=W["moe_S"][:, :]), writes=["moe_S_s"], dma=True)
        semat = A.sb("moe_SE_s", [8, 128], F32, side="right")
        P.emit("sync", lambda e: e.dma_start(out=semat[:], in_=W["moe_SE"][:, :]), writes=["moe_SE_s"], dma=True)
        ohot = A.sb("moe_oh", [128, 8], F32, side="right")
        P.emit("sync", lambda e: e.dma_start(out=ohot[:], in_=W["moe_ownhot"][:, :]), writes=["moe_oh"], dma=True)

        iotok = A.sb("moe_iotok", [128, 32], F32, side="right")
        P.emit("gpsimd", lambda e: e.iota(iotok[:], pattern=[[1, 32]], base=0, channel_multiplier=32,
                                          allow_small_or_imprecise_dtypes=True), writes=["moe_iotok"])
        iorow = A.sb("moe_iorow", [128, 512], F32, side="right")
        P.emit("gpsimd", lambda e: e.iota(iorow[:], pattern=[[1, 512]], base=0, channel_multiplier=0,
                                          allow_small_or_imprecise_dtypes=True), writes=["moe_iorow"])

    md = A.t["_moe_dram"]
    lnst = md["lnst"] if first_call else md["lnst_b"]
    cin = md["cin"] if first_call else md["cin_b"]
    cout = md["cout"] if first_call else md["cout_b"]
    rw, rb, lmat, smat, semat, ohot = [A.t[k] for k in
                                       ["moe_rw_s", "moe_rb_s", "moe_L_s", "moe_S_s", "moe_SE_s", "moe_oh"]]
    iotok, iorow = A.t["moe_iotok"], A.t["moe_iorow"]

    mkTOP = A.mark()
    idxu = A.sb("moe_idxu", [128, 4 * E], U32, side="right")
    gval = A.sb("moe_gval", [128, 4 * E], F32, side="right")
    idxf = A.sb("moe_idxf", [128, 4 * E], F32, side="right")
    mkS = A.mark()
    sct = A.sb("moe_sct", [8, SEQ], F32, side="right")
    mkA = A.mark()
    for i in range(NDM):
        A.sb(f"moe_xc{i}", [128, 512], BF16, side="right")
    xlnc = [f"moe_xc{i}" for i in range(NDM)]
    lgsb = A.sb("moe_lg", [8, 512], F32, side="right")
    tokp = A.sb("moe_tokp", [128, 8], F32, side="right")
    tnm = A.sb("moe_tnm", [128, 1], F32, side="right")
    tsum = A.sb("moe_tsum", [128, 1], F32, side="right")
    trec = A.sb("moe_trec", [128, 1], F32, side="right")
    tsc = A.sb("moe_tsc", [128, 8], F32, side="right")
    tmaj = A.sb("moe_tmaj", [128, 1024], BF16, side="right")
    cwrow, rbcol = A.t["moe_cwr"], A.t["moe_rbc"]
    for tc in range(8):
        s_ = slice(tc * 512, (tc + 1) * 512)
        ln_chunk(P, A, g, xk, tc, xlnc)
        for i in range(NDM):
            xi_ = A.t[xk[i]]
            P.emit("tensor", lambda e, xi_=xi_, i=i, s_=s_, st=(i == 0):
                   e.matmul(ph[2][0:8, 0:512], rw[:, i * 8:(i + 1) * 8], xi_[:, s_], start=st, stop=False),
                   reads=["moe_rw_s", xk[i]], writes=["ph2"])
        P.emit("tensor", lambda e: e.matmul(ph[2][0:8, 0:512], cwrow[0:1, :], A.t["lnrow_m"][0:1, 0:512], start=False, stop=True),
               reads=["moe_cwr", "lnrow_m"], writes=["ph2"])
        P.emit("tensor", lambda e: e.matmul(ph[5][0:8, 0:512], g["ones_row_f"][0:1, 0:8], A.t["lnrow_r"][0:1, 0:512], start=True, stop=True),
               reads=["ones_row_f", "lnrow_r"], writes=["ph5"])
        P.emit("vector", lambda e: e.tensor_copy(lgsb[:], ph[2][0:8, 0:512]), reads=["ph2"], writes=["moe_lg"])
        P.emit("vector", lambda e: e.tensor_tensor(lgsb[:], lgsb[:], ph[5][0:8, 0:512], op=ALU.mult),
               reads=["moe_lg", "ph5"], writes=["moe_lg"])
        P.emit("vector", lambda e: e.tensor_scalar(lgsb[:], lgsb[:], rbcol[:], None, op0=ALU.add),
               reads=["moe_lg"], sreads=["moe_rbc"], writes=["moe_lg"])
        for sb_ in range(4):
            cs = slice(sb_ * 128, (sb_ + 1) * 128)
            P.emit("tensor", lambda e, cs=cs: e.transpose(ph[2][:, 16:24], lgsb[:, cs], g["ident_f"][0:8, 0:8]),
                   reads=["moe_lg", "ident_f"], writes=["ph2"])
            P.emit("vector", lambda e: e.tensor_copy(tokp[:], ph[2][:, 16:24]),
                   reads=["ph2"], writes=["moe_tokp"])
            P.emit("vector", lambda e: e.tensor_reduce(tnm[:], tokp[:], axis=AX.X, op=ALU.max, negate=True),
                   reads=["moe_tokp"], writes=["moe_tnm"])
            P.emit("scalar", lambda e: e.activation(out=tokp[:], in_=tokp[:], func=AF.Exp, bias=tnm[:]),
                   reads=["moe_tokp"], sreads=["moe_tnm"], writes=["moe_tokp"])
            P.emit("vector", lambda e: e.tensor_reduce(tsum[:], tokp[:], axis=AX.X, op=ALU.add),
                   reads=["moe_tokp"], writes=["moe_tsum"])
            P.emit("vector", lambda e: e.reciprocal(trec[:], tsum[:]), reads=["moe_tsum"], writes=["moe_trec"])
            P.emit("vector", lambda e: e.tensor_scalar(tsc[:], tokp[:], trec[:], None, op0=ALU.mult),
                   reads=["moe_tokp"], sreads=["moe_trec"], writes=["moe_tsc"])
            P.emit("tensor", lambda e: e.transpose(ph[5][0:8, 0:128], tsc[:], g["ident_f"][:]),
                   reads=["moe_tsc", "ident_f"], writes=["ph5"])
            P.emit("vector", lambda e, tc=tc, cs=cs: e.tensor_copy(sct[0:8, tc * 512 + cs.start:tc * 512 + cs.stop], ph[5][0:8, 0:128]),
                   reads=["ph5"], writes=["moe_sct"])
        for sub in range(4):
            for i in range(NDM):
                xc = A.t[xlnc[i]]
                P.emit("tensor", lambda e, xc=xc, sub=sub: e.transpose(phbf[:, 0:128], xc[:, sub * 128:(sub + 1) * 128], g["ident_bf"][:]),
                       reads=[xlnc[i], "ident_bf"], writes=["phbf"])
                P.emit("vector", lambda e, i=i: e.tensor_copy(tmaj[:, i * 128:(i + 1) * 128], phbf[:, 0:128]),
                       reads=["phbf"], writes=["moe_tmaj"])
            P.emit("gpsimd", lambda e, tc=tc, sub=sub: e.dma_start(
                out=lnst[(tc * 4 + sub) * 128:(tc * 4 + sub + 1) * 128, :], in_=tmaj[:]),
                reads=["moe_tmaj"], writes=["moe_lnst"], dma=True)
    P.emit("gpsimd", lambda e: e.dma_start(out=cin[:, :], in_=sct[:]),
           reads=["moe_sct"], writes=["moe_cin"], dma=True)
    P.emit("gpsimd", lambda e: e.collective_compute(
        "AllGather", ALU.bypass, replica_groups=[list(range(8))],
        ins=[cin.opt()], outs=[cout.opt()]),
        reads=["moe_cin"], writes=["moe_cout"], tag="collective")
    P.fence()
    A.free_to(mkA)
    mkR = A.mark()
    lo = A.sb("moe_lo", [8, 1], F32, side="right")
    hi = A.sb("moe_hi", [8, 1], F32, side="right")
    mid = A.sb("moe_mid", [8, 1], F32, side="right")
    cge = A.sb("moe_cge", [8, 1], U32, side="right")
    cnt = A.sb("moe_cnt", [128, 1], F32, side="right")
    th128 = A.sb("moe_th128", [128, 1], F32, side="right")
    tneed = A.sb("moe_tneed", [8, 1], F32, side="right")
    rtot = A.sb("moe_rtot", [128, 1], F32, side="right")
    ooff = A.sb("moe_ooff", [8, 1], F32, side="right")
    th8 = A.sb("moe_th8", [8, 1], F32, side="right")
    tn8 = A.sb("moe_tn8", [8, 1], F32, side="right")
    zc = A.sb("moe_zc", [128, 1], F32, side="right")
    mkR1 = A.mark()
    scg = A.sb("moe_scg", [128, 2048], F32, side="right")
    msk = A.sb("moe_msk", [128, 2048], F32, side="right")
    P.emit("vector", lambda e: e.memset(zc[:], 0.0), writes=["moe_zc"])
    for p in range(128):
        e_, gg = p // 16, p % 16
        r_, c0 = 8 * (gg // 2) + e_, (gg % 2) * 2048
        P.emit("sync" if p % 2 == 0 else "gpsimd",
               lambda e, p=p, r_=r_, c0=c0: e.dma_start(out=scg[p:p + 1, :], in_=cout[r_:r_ + 1, c0:c0 + 2048]),
               reads=["moe_cout"], writes=["moe_scg"], dma=True)
    if DEBUG_ON:
        tapd(P, A, "scg2", "moe_scg", lambda: scg[:])
    P.emit("vector", lambda e: e.memset(lo[:], 0.0), writes=["moe_lo"])
    P.emit("vector", lambda e: e.memset(hi[:], 1.0), writes=["moe_hi"])
    P.emit("vector", lambda e: e.memset(mid[:], 0.5), writes=["moe_mid"])
    for it in range(56):
        P.emit("tensor", lambda e: e.matmul(ph[3][:, 0:1], semat[:], mid[:], start=True, stop=True),
               reads=["moe_SE_s", "moe_mid"], writes=["ph3"])
        P.emit("vector", lambda e: e.tensor_scalar(msk[:], scg[:], ph[3][:, 0:1], None, op0=ALU.is_gt,
                                                   op1=ALU.add, accum_out=cnt[:]),
               reads=["moe_scg"], sreads=["ph3"], writes=["moe_msk", "moe_cnt"])
        P.emit("tensor", lambda e: e.matmul(ph[3][0:8, 1:2], smat[:], cnt[:], start=True, stop=True),
               reads=["moe_S_s", "moe_cnt"], writes=["ph3"])
        P.emit("vector", lambda e: e.tensor_scalar(cge[:], ph[3][0:8, 1:2], 2048.0, None, op0=ALU.is_ge),
               reads=["ph3"], writes=["moe_cge"])
        P.emit("vector", lambda e: e.select(lo[:], cge[:], mid[:], lo[:]),
               reads=["moe_cge", "moe_mid", "moe_lo"], writes=["moe_lo"])
        P.emit("vector", lambda e: e.select(hi[:], cge[:], hi[:], mid[:]),
               reads=["moe_cge", "moe_hi", "moe_mid"], writes=["moe_hi"])
        P.emit("vector", lambda e: e.tensor_tensor(mid[:], lo[:], hi[:], op=ALU.add),
               reads=["moe_lo", "moe_hi"], writes=["moe_mid"])
        P.emit("vector", lambda e: e.tensor_scalar(mid[:], mid[:], 0.5, None, op0=ALU.mult),
               reads=["moe_mid"], writes=["moe_mid"])
    P.emit("tensor", lambda e: e.matmul(ph[3][:, 0:1], semat[:], hi[:], start=True, stop=True),
           reads=["moe_SE_s", "moe_hi"], writes=["ph3"])
    P.emit("vector", lambda e: e.tensor_copy(th128[:], ph[3][:, 0:1]), reads=["ph3"], writes=["moe_th128"])
    P.emit("vector", lambda e: e.tensor_scalar(msk[:], scg[:], th128[:], None, op0=ALU.is_gt,
                                               op1=ALU.add, accum_out=cnt[:]),
           reads=["moe_scg"], sreads=["moe_th128"], writes=["moe_msk", "moe_cnt"])
    P.emit("tensor", lambda e: e.matmul(ph[3][0:8, 1:2], smat[:], cnt[:], start=True, stop=True),
           reads=["moe_S_s", "moe_cnt"], writes=["ph3"])
    P.emit("vector", lambda e: e.tensor_scalar(tneed[:], ph[3][0:8, 1:2], -1.0, 2048.0, op0=ALU.mult, op1=ALU.add),
           reads=["ph3"], writes=["moe_tneed"])
    P.emit("vector", lambda e: e.tensor_scalar(msk[:], scg[:], th128[:], None, op0=ALU.is_equal),
           reads=["moe_scg"], sreads=["moe_th128"], writes=["moe_msk"])
    P.emit("vector", lambda e: e.tensor_tensor_scan(scg[:], zc[:].broadcast_to([128, 2048]), msk[:], 0.0,
                                                    op0=ALU.add, op1=ALU.add),
           reads=["moe_zc", "moe_msk"], writes=["moe_scg"])
    P.emit("vector", lambda e: e.tensor_copy(rtot[:], scg[:, 2047:2048]), reads=["moe_scg"], writes=["moe_rtot"])
    P.emit("tensor", lambda e: e.matmul(ph[3][:, 2:3], lmat[:], rtot[:], start=True, stop=True),
           reads=["moe_L_s", "moe_rtot"], writes=["ph3"])
    P.emit("vector", lambda e: e.tensor_copy(rtot[:], ph[3][:, 2:3]), reads=["ph3"], writes=["moe_rtot"])
    P.emit("tensor", lambda e: e.matmul(ph[3][0:8, 3:4], ohot[:], rtot[:], start=True, stop=True),
           reads=["moe_oh", "moe_rtot"], writes=["ph3"])
    P.emit("vector", lambda e: e.tensor_copy(ooff[:], ph[3][0:8, 3:4]), reads=["ph3"], writes=["moe_ooff"])
    P.fence()
    A.free_to(mkR1)
    selA = A.sb("moe_selA", [8, SEQ], F32, side="right")
    selB = A.sb("moe_selB", [8, SEQ], F32, side="right")
    P.emit("vector", lambda e: e.tensor_copy(th8[:], hi[:]), reads=["moe_hi"], writes=["moe_th8"])
    P.emit("vector", lambda e: e.scalar_tensor_tensor(tn8[:], tneed[:], 1.0, ooff[:], op0=ALU.mult, op1=ALU.subtract),
           reads=["moe_tneed", "moe_ooff"], writes=["moe_tn8"])
    P.emit("vector", lambda e: e.tensor_scalar(selB[:], sct[:], th8[:], None, op0=ALU.is_equal),
           reads=["moe_sct"], sreads=["moe_th8"], writes=["moe_selB"])
    P.emit("vector", lambda e: e.tensor_tensor_scan(selA[:], zc[0:8, :].broadcast_to([8, SEQ]), selB[:], 0.0,
                                                    op0=ALU.add, op1=ALU.add),
           reads=["moe_zc", "moe_selB"], writes=["moe_selA"])
    P.emit("vector", lambda e: e.tensor_scalar(selA[:], selA[:], tn8[:], None, op0=ALU.is_le),
           reads=["moe_selA"], sreads=["moe_tn8"], writes=["moe_selA"])
    P.emit("vector", lambda e: e.tensor_tensor(selB[:], selB[:], selA[:], op=ALU.mult),
           reads=["moe_selB", "moe_selA"], writes=["moe_selB"])
    P.emit("vector", lambda e: e.tensor_scalar(selA[:], sct[:], th8[:], None, op0=ALU.is_gt),
           reads=["moe_sct"], sreads=["moe_th8"], writes=["moe_selA"])
    P.emit("vector", lambda e: e.tensor_tensor(selA[:], selA[:], selB[:], op=ALU.add),
           reads=["moe_selA", "moe_selB"], writes=["moe_selA"])
    P.emit("vector", lambda e: e.tensor_tensor_scan(selB[:], zc[0:8, :].broadcast_to([8, SEQ]), selA[:], 0.0,
                                                    op0=ALU.add, op1=ALU.add),
           reads=["moe_zc", "moe_selA"], writes=["moe_selB"])
    P.emit("vector", lambda e: e.tensor_tensor(selB[:], selB[:], selA[:], op=ALU.mult),
           reads=["moe_selB", "moe_selA"], writes=["moe_selB"])
    P.emit("vector", lambda e: e.tensor_scalar(selB[:], selB[:], -1.0, None, op0=ALU.add),
           reads=["moe_selB"], writes=["moe_selB"])
    P.emit("gpsimd", lambda e: e.dma_start(out=md["pos_d"][:, :], in_=selB[:]),
           reads=["moe_selB"], writes=["moe_pos_d"], dma=True)
    if DEBUG_ON:
        tapd(P, A, "selA", "moe_selA", lambda: selA[:])
        tapd(P, A, "sct", "moe_sct", lambda: sct[:])
    if DEBUG_ON:
        dbg = A.sb("moe_dbg", [8, 8], F32)
        for ci, (tt, kk) in enumerate([(hi, "moe_hi"), (tneed, "moe_tneed"), (ooff, "moe_ooff"),
                                       (lo, "moe_lo"), (th8, "moe_th8"), (tn8, "moe_tn8")]):
            P.emit("vector", lambda e, tt=tt, ci=ci: e.tensor_copy(dbg[0:8, ci:ci + 1], tt[:]),
                   reads=[kk], writes=["moe_dbg"])
        tap(P, A, "dbg8", "moe_dbg", lambda: A.t["moe_dbg"][:])
        selcnt = A.sb("moe_selcnt", [8, 1], F32)
        P.emit("vector", lambda e: e.tensor_scalar(selA[:], selA[:], 1.0, None, op0=ALU.mult,
                                                   op1=ALU.add, accum_out=selcnt[:]),
               reads=["moe_selA"], writes=["moe_selA", "moe_selcnt"])
        P.emit("vector", lambda e: e.tensor_copy(dbg[0:8, 6:7], selcnt[:]), reads=["moe_selcnt"], writes=["moe_dbg"])
    P.fence()
    A.free_to(mkR)
    posp = A.sb("moe_posp", [128, 32], F32, side="right")
    gsp = A.sb("moe_gsp", [128, 32], F32, side="right")
    payl = A.sb("moe_payl", [128, 64], F32, side="right")
    ohw = A.sb("moe_ohw", [128, 512], F32, side="right")
    for e_ in range(E):
        P.emit("sync", lambda e, e_=e_: e.dma_start(
            out=posp[:], in_=md["pos_d"][e_, :].rearrange("(p w) -> p w", p=128)),
            reads=["moe_pos_d"], writes=["moe_posp"], dma=True)
        P.emit("gpsimd", lambda e, e_=e_, cin=cin: e.dma_start(
            out=gsp[:], in_=cin[e_, :].rearrange("(p w) -> p w", p=128)),
            reads=["moe_cin"], writes=["moe_gsp"], dma=True)
        P.emit("vector", lambda e: e.tensor_copy(payl[:, 0:64:2], iotok[:]),
               reads=["moe_iotok"], writes=["moe_payl"])
        P.emit("vector", lambda e: e.tensor_copy(payl[:, 1:64:2], gsp[:]),
               reads=["moe_gsp"], writes=["moe_payl"])
        for w in range(32):
            P.emit("vector", lambda e, w=w: e.tensor_scalar(ohw[:], iorow[:], posp[:, w:w + 1], None, op0=ALU.is_equal),
                   reads=["moe_iorow"], sreads=["moe_posp"], writes=["moe_ohw"])
            for sc in range(NSC):
                P.emit("tensor", lambda e, w=w, sc=sc, st=(w == 0 and sc == 0), sp=(w == 31 and sc == NSC - 1): e.matmul(
                    ph[4][:, 2 * sc:2 * sc + 2], ohw[:, sc * 128:(sc + 1) * 128], payl[:, 2 * w:2 * w + 2],
                    start=st, stop=sp),
                    reads=["moe_ohw", "moe_payl"], writes=["ph4"])
        for sc in range(NSC):
            col = e_ * 4 + sc
            P.emit("vector", lambda e, sc=sc, col=col: e.tensor_copy(idxf[:, col:col + 1], ph[4][:, 2 * sc:2 * sc + 1]),
                   reads=["ph4"], writes=["moe_idxf"])
            P.emit("vector", lambda e, sc=sc, col=col: e.tensor_copy(gval[:, col:col + 1], ph[4][:, 2 * sc + 1:2 * sc + 2]),
                   reads=["ph4"], writes=["moe_gval"])
            P.emit("vector", lambda e, col=col: e.tensor_copy(idxu[:, col:col + 1], idxf[:, col:col + 1]),
                   reads=["moe_idxf"], writes=["moe_idxu"])
    if DEBUG_ON:
        tap(P, A, "idxf", "moe_idxf", lambda: A.t["moe_idxf"][:])
        tap(P, A, "gval", "moe_gval", lambda: A.t["moe_gval"][:])
    P.fence()
    A.free_to(mkS)
    mkE = A.mark()
    idxsh = A.sb("moe_idxsh", [128, 1], F32, side="right")
    h1t = [A.sb(f"moe_h1_{f_}", [128, CAP], BF16, side="right") for f_ in range(32)]
    for pp in range(2):
        for i in range(NDM):
            A.sb(f"moe_w1c{pp}_{i}", [128, 128], BF16, side="right")
    for pp in range(2):
        A.sb(f"moe_w2c{pp}", [128, 512], BF16, side="right")
    b1sb = A.sb("moe_b1_s", [128, 32], F32, side="right")
    b2bf = A.sb("moe_b2bf", [1, 1024], BF16, side="right")
    w1b, w2b = md["w1b"], md["w2b"]
    orbf = g["ones_row_bf"]
    for e_ in range(E):
        P.emit("sync", lambda e, e_=e_: e.dma_start(out=b1sb[:], in_=W["moe_b1"][e_, :, :]),
               writes=["moe_b1_s"], dma=True)
        P.emit("sync", lambda e, e_=e_: e.dma_start(out=b2bf[:], in_=md["b2bf"][e_:e_ + 1, :]),
               reads=["hbm_b2bf"], writes=["moe_b2bf"], dma=True)
        mkT = A.mark()
        tokm = A.sb("moe_tokm", [128, 1024], BF16, side="right")
        tksT = [A.sb(f"moe_tk{i}", [128, CAP], BF16, side="right") for i in range(NDM)]
        for sc in range(NSC):
            P.emit("gpsimd", lambda e, e_=e_, sc=sc: e.indirect_dma_start(
                out=tokm[:], out_offset=None, in_=lnst[:, :],
                in_offset=IndirectOffsetOnAxis(ap=idxu[:, e_ * 4 + sc:e_ * 4 + sc + 1], axis=0)),
                reads=["moe_lnst", "moe_idxu"], writes=["moe_tokm"], dma=True)
            for i in range(NDM):
                P.emit("tensor", lambda e, i=i: e.transpose(phbf[:, 0:128], tokm[:, i * 128:(i + 1) * 128], g["ident_bf"][:]),
                       reads=["moe_tokm", "ident_bf"], writes=["phbf"])
                P.emit("vector", lambda e, i=i, sc=sc: e.tensor_copy(tksT[i][:, sc * 128:(sc + 1) * 128], phbf[:, 0:128]),
                       reads=["phbf"], writes=[f"moe_tk{i}"])
        for f_ in range(32):
            pp = f_ % 2
            for i in range(NDM):
                wt = A.t[f"moe_w1c{pp}_{i}"]
                P.emit("sync", lambda e, wt=wt, e_=e_, f_=f_, i=i: e.dma_start(
                    out=wt[:], in_=w1b[e_, f_, i, :, :]),
                    reads=[f"mw1_{e_}_blk"], writes=[f"moe_w1c{pp}_{i}"], dma=True)
            for i in range(NDM):
                wt = A.t[f"moe_w1c{pp}_{i}"]
                P.emit("tensor", lambda e, wt=wt, i=i, st=(i == 0), sp=(i == NDM - 1): e.matmul(
                    ph[0][:, 0:CAP], wt[:], tksT[i][:], start=st, stop=sp),
                    reads=[f"moe_w1c{pp}_{i}", f"moe_tk{i}"], writes=["ph0"])
            P.emit("scalar", lambda e, f_=f_: e.activation(out=h1t[f_][:], in_=ph[0][:, 0:CAP], func=AF.Relu,
                                                           bias=b1sb[:, f_:f_ + 1]),
                   reads=["ph0"], sreads=["moe_b1_s"], writes=[f"moe_h1_{f_}"])
            if e_ == 0 and f_ == 0 and DEBUG_ON:
                tap(P, A, "w1c_07", "moe_w1c0_7", lambda: A.t["moe_w1c0_7"][:])
        if e_ == 0 and DEBUG_ON:
            tap(P, A, "tk7", "moe_tk7", lambda: tksT[7][:])
            tap(P, A, "h1_0", "moe_h1_0", lambda: h1t[0][:])
        P.fence()
        A.free_to(mkT)
        eot = {}
        for sc in range(NSC):
            for dv in range(2):
                eot[(sc, dv)] = A.sb(f"moe_eo{sc}_{dv}", [128, 512], BF16, side="right")
        pg = A.sb("moe_pg", [128, 512 * NSC], BF16, side="right")
        for dv in range(2):
            for f_ in range(32):
                pp = f_ % 2
                w2t = A.t[f"moe_w2c{pp}"]
                P.emit("sync", lambda e, w2t=w2t, e_=e_, f_=f_, dv=dv: e.dma_start(
                    out=w2t[:].rearrange("p (j c) -> p j c", j=4),
                    in_=w2b[e_, dv * 4:dv * 4 + 4, f_, :, :].rearrange("j p c -> p j c")),
                    reads=[f"mw2_{e_}_blk"], writes=[f"moe_w2c{pp}"], dma=True)
                for sc in range(NSC):
                    P.emit("tensor", lambda e, f_=f_, sc=sc, w2t=w2t, st=(f_ == 0): e.matmul(
                        ph[sc][:, 0:512], h1t[f_][:, sc * 128:(sc + 1) * 128], w2t[:], start=st, stop=False),
                        reads=[f"moe_h1_{f_}", f"moe_w2c{pp}"], writes=[f"ph{sc}"])
            for sc in range(NSC):
                P.emit("tensor", lambda e, sc=sc, dv=dv: e.matmul(
                    ph[sc][:, 0:512], orbf[0:1, 0:128], b2bf[0:1, dv * 512:(dv + 1) * 512], start=False, stop=True),
                    reads=["moe_b2bf", "ones_row_bf"], writes=[f"ph{sc}"])
                P.emit("scalar", lambda e, sc=sc, dv=dv: e.activation(out=eot[(sc, dv)][:], in_=ph[sc][:, 0:512], func=AF.Copy),
                       reads=[f"ph{sc}"], writes=[f"moe_eo{sc}_{dv}"])
        for tokc in range(8):
            for sc in range(NSC):
                P.emit("vector", lambda e, e_=e_, sc=sc, tokc=tokc: e.tensor_scalar(
                    idxsh[:], idxf[:, e_ * 4 + sc:e_ * 4 + sc + 1], float(-512 * tokc), None, op0=ALU.add),
                    reads=["moe_idxf"], writes=["moe_idxsh"])
                P.emit("vector", lambda e, e_=e_, sc=sc: e.tensor_scalar(
                    pg[:, sc * 512:(sc + 1) * 512], iorow[:],
                    idxsh[:], gval[:, e_ * 4 + sc:e_ * 4 + sc + 1],
                    op0=ALU.is_equal, op1=ALU.mult),
                    reads=["moe_iorow"], sreads=["moe_idxsh", "moe_gval"], writes=["moe_pg"])
            for dmc in range(NDM):
                dv = dmc // 4
                pk = "ph4" if dmc % 2 == 0 else "ph5"
                pht_ = ph[4] if dmc % 2 == 0 else ph[5]
                for sc in range(NSC):
                    eo = eot[(sc, dv)]
                    P.emit("tensor", lambda e, eo=eo, sc=sc, dmc=dmc, pht_=pht_, st=(sc == 0), sp=(sc == NSC - 1):
                           e.matmul(pht_[:, 0:512], eo[:, (dmc % 4) * 128:(dmc % 4 + 1) * 128],
                                    pg[:, sc * 512:(sc + 1) * 512], start=st, stop=sp),
                           reads=[f"moe_eo{sc}_{dv}", "moe_pg"], writes=[pk])
                xo = A.t[xk[dmc]]
                s = slice(tokc * 512, (tokc + 1) * 512)
                P.emit("vector", lambda e, xo=xo, s=s, pht_=pht_: e.scalar_tensor_tensor(
                    xo[:, s], pht_[:, 0:512], 0.5, xo[:, s], op0=ALU.mult, op1=ALU.add),
                    reads=[pk, xk[dmc]], writes=[xk[dmc]])
        P.fence()
        A.free_to(mkT)
    P.fence()
    A.free_to(mkE)
    A.free_to(mkTOP)


def build_sl3_standalone():
    return build_sl1_standalone()


def build_sl6_standalone():
    return build_sl4_standalone()


def build_fused():
    nc = bass.Bass("TRN2", target_bir_lowering=False)
    xin = nc.dram_tensor("xT", [D_MODEL, SEQ], F32, kind="ExternalInput").ap()
    xout = nc.dram_tensor("yT", [D_MODEL, SEQ], F32, kind="ExternalOutput").ap()
    xres = nc.dram_tensor("hbm_xres", [D_MODEL, SEQ], F32).ap()
    Wd = {}
    for shapes in (MOE_W_SHAPES, ATTN_W_SHAPES, L1FFN_W_SHAPES, MB_W_SHAPES):
        for k, sh in shapes.items():
            Wd[k] = nc.dram_tensor(k, sh, F32, kind="ExternalInput").ap()
    with ExitStack() as stack:
        A = Alloc(nc, stack)
        P = Prog(nc)
        g = build_common(P, A)
        A.sb("wstage1k", [128, 1024], F32)
        A.sb("wstage1k_bf", [128, 1024], BF16)
        xkeys = [f"x{i}" for i in range(NDM)]
        for i in range(NDM):
            x = A.sb(xkeys[i], [128, SEQ], F32)
            P.emit("sync", lambda e, x=x, i=i: e.dma_start(out=x[:], in_=xin[i * 128:(i + 1) * 128, :]),
                   writes=[xkeys[i]], dma=True)
        for pn in ["ph0", "ph1", "ph2", "ph3", "ph4", "ph5"]:
            A.ps(pn, [128, 512], F32)
        A.ps("phbf", [128, 1024], BF16)
        sl_l0ffn(P, A, g, Wd, xkeys, first_call=True)
        sl_l0attn(P, A, g, Wd, xkeys)
        sl_l0ffn(P, A, g, Wd, xkeys, first_call=False)
        sl_l1ffn(P, A, g, Wd, xkeys, first_call=True)
        xres2 = nc.dram_tensor("hbm_xres2", [D_MODEL, SEQ], F32).ap()
        sl_l1attn(P, A, g, Wd, xkeys, xres, xres2)
        for i in range(NDM):
            x = A.sb(xkeys[i], [128, SEQ], F32)
            P.emit("sync", lambda e, x=x, i=i: e.dma_start(out=x[:], in_=xres2[i * 128:(i + 1) * 128, :]),
                   reads=["mb_xout"], writes=[xkeys[i]], dma=True)
        sl_l1ffn(P, A, g, Wd, xkeys, first_call=False)
        for i in range(NDM):
            x = A.t[xkeys[i]]
            P.emit("gpsimd", lambda e, x=x, i=i: e.dma_start(out=xout[i * 128:(i + 1) * 128, :], in_=x[:]),
                   reads=[xkeys[i]], writes=[f"outdma{i}"], dma=True)
        P.barrier_read("gpsimd", [f"outdma{i}" for i in range(NDM)])
        finalize(P, nc, A)
        print(f"SBUF bytes: {A.bytes/1e6:.2f}MB, ops: {len(P.ops)}", flush=True)
    return nc


def kernel(**inputs):
    x = np.asarray(inputs["L0_attn_x"], dtype=np.float32)
    Watt = prep_l0attn_weights(inputs)
    W4 = prep_l1ffn_weights(inputs)
    W5 = prep_l1attn_weights(inputs)
    global DEBUG_TAPS
    DEBUG_TAPS = []
    nc = build_fused()
    ims = []
    for c in range(N_CORES):
        im = dict(prep_l0ffn_weights(inputs, c))
        im.update(Watt)
        im.update(W4)
        im.update(W5)
        im["xT"] = np.ascontiguousarray(x[c].T)
        ims.append(im)
    res = run_bass_kernel_spmd(nc, ims, core_ids=list(range(N_CORES)))
    return np.stack([np.asarray(res.results[c]["yT"], np.float32).T for c in range(N_CORES)], axis=0)


def kernel_unfused(**inputs):
    x = np.asarray(inputs["L0_attn_x"], dtype=np.float32)
    xT = [np.ascontiguousarray(x[c].T) for c in range(N_CORES)]
    Wmoe = [prep_l0ffn_weights(inputs, c) for c in range(N_CORES)]
    Watt = prep_l0attn_weights(inputs)
    W4 = prep_l1ffn_weights(inputs)
    W5 = prep_l1attn_weights(inputs)

    def run(nc, wlist):
        ims = [dict(wlist[c]) for c in range(N_CORES)]
        for c in range(N_CORES):
            ims[c]["xT"] = xT[c]
        res = run_bass_kernel_spmd(nc, ims, core_ids=list(range(N_CORES)))
        for c in range(N_CORES):
            xT[c] = np.ascontiguousarray(np.asarray(res.results[c]["yT"], np.float32))

    global DEBUG_TAPS
    DEBUG_TAPS = []
    run(build_sl1_standalone(), Wmoe)
    run(build_sl2_standalone(), [Watt] * N_CORES)
    run(build_sl3_standalone(), Wmoe)
    run(build_sl4_standalone(), [W4] * N_CORES)
    run(build_sl5_standalone(), [W5] * N_CORES)
    run(build_sl6_standalone(), [W4] * N_CORES)
    return np.stack([xT[c].T for c in range(N_CORES)], axis=0)




def sl_l1attn(P, A, g, W, xk, xres, xout_ap, alloc_x=None, fused=False):
    nc = P.nc
    TC = 512
    NTC = SEQ // TC
    ph = [A.t[f"ph{i}"] for i in range(6)]
    phbf = A.t["phbf"]
    dirs = ("f", "b")
    hxi = {d: nc.dram_tensor(f"hbm_xi_{d}", [2048, SEQ], BF16).ap() for d in dirs}
    hz = {d: nc.dram_tensor(f"hbm_z_{d}", [2048, SEQ], BF16).ap() for d in dirs}
    hmo = {d: nc.dram_tensor(f"hbm_mo_{d}", [1024, SEQ], BF16).ap() for d in dirs}
    hinw = {d: nc.dram_tensor(f"hbm_inw_{d}", [32, 8, 128, 128], BF16).ap() for d in dirs}
    houtw = {d: nc.dram_tensor(f"hbm_outw_{d}", [8, 16, 128, 128], BF16).ap() for d in dirs}
    hprj = nc.dram_tensor("hbm_prj", [8, 16, 128, 128], BF16).ap()
    for d in dirs:
        cast_to_hbm_blocked(P, A, f"inw{d}", W[f"mb_inWT_{d}"], 1024, 4096, hinw[d], "wstage1k")
        cast_to_hbm_blocked(P, A, f"outw{d}", W[f"mb_outWT_{d}"], 2048, 1024, houtw[d], "wstage1k")
    cast_to_hbm_blocked(P, A, "prj", W["mb_projT"], 2048, 1024, hprj, "wstage1k")
    if alloc_x is not None:
        alloc_x()

    mk1 = A.mark()
    for i in range(NDM):
        A.sb(f"mb_xc{i}", [128, 512], BF16, side="right")
    xlnc = [f"mb_xc{i}" for i in range(NDM)]
    wcb = [A.sb(f"mb_wcb{pp}", [128, 8 * 128], BF16, side="right") for pp in range(2)]
    evs = [A.sb(f"mb_ev{pp}", [128, 512], BF16, side="right") for pp in range(2)]
    for tc in range(8):
        s = slice(tc * 512, (tc + 1) * 512)
        ln_chunk(P, A, g, xk, tc, xlnc)
        for d in dirs:
            for mc in range(32):
                pp = mc % 2
                wc = wcb[pp]
                P.emit("sync", lambda e, wc=wc, d=d, mc=mc: e.dma_start(
                    out=wc[:].rearrange("p (r c) -> p r c", c=128), in_=hinw[d][mc].rearrange("r p c -> p r c")),
                    reads=[f"inw{d}_blk"], writes=[f"mb_wcb{pp}"], dma=True)
                for i in range(NDM):
                    xc = A.t[xlnc[i]]
                    P.emit("tensor", lambda e, wc=wc, xc=xc, i=i, st=(i == 0), sp=(i == NDM - 1):
                           e.matmul(ph[0][:, 0:512], wc[:, i * 128:(i + 1) * 128], xc[:], start=st, stop=sp),
                           reads=[f"mb_wcb{pp}", xlnc[i]], writes=["ph0"])
                ev = evs[pp]
                if d == "f":
                    P.emit("scalar", lambda e, ev=ev: e.activation(out=ev[:], in_=ph[0][:, 0:512], func=AF.Copy),
                           reads=["ph0"], writes=[f"mb_ev{pp}"])
                else:
                    P.emit("scalar", lambda e, ev=ev: e.activation(out=ev[:, ::-1], in_=ph[0][:, 0:512], func=AF.Copy),
                           reads=["ph0"], writes=[f"mb_ev{pp}"])
                xi_or_z = hxi[d] if mc < 16 else hz[d]
                row = (mc % 16) * 128
                col = s if d == "f" else slice((7 - tc) * 512, (8 - tc) * 512)
                P.emit("gpsimd", lambda e, xi_or_z=xi_or_z, row=row, col=col, ev=ev: e.dma_start(
                    out=xi_or_z[row:row + 128, col], in_=ev[:]),
                    reads=[f"mb_ev{pp}"], writes=["mb_spill"], dma=True)
    P.fence()
    A.free_to(mk1)
    for i in range(NDM):
        x = A.t[xk[i]]
        P.emit("sync", lambda e, x=x, i=i: e.dma_start(out=xres[i * 128:(i + 1) * 128, :], in_=x[:]),
               reads=[xk[i]], writes=["mb_xres"], dma=True)
    P.fence()
    A.free_left(*xk)
    mkP = A.mark()
    for d in dirs:
        cw = A.sb(f"mb_cw_{d}", [128, 64], F32, side="right")
        P.emit("sync", lambda e, cw=cw, d=d: e.dma_start(out=cw[:], in_=W[f"mb_convw_{d}"][:, :]),
               writes=[f"mb_cw_{d}"], dma=True)
        cb = A.sb(f"mb_cb_{d}", [128, 16], F32, side="right")
        P.emit("sync", lambda e, cb=cb, d=d: e.dma_start(out=cb[:], in_=W[f"mb_convb_{d}"][:, :]),
               writes=[f"mb_cb_{d}"], dma=True)
        cbh = A.sb(f"mb_cbh_{d}", [128, 16], F32, side="right")
        P.emit("vector", lambda e, cbh=cbh, cb=cb: e.tensor_scalar(cbh[:], cb[:], 0.5, None, op0=ALU.mult),
               reads=[f"mb_cb_{d}"], writes=[f"mb_cbh_{d}"])
        load_weight_bf16(P, A, f"mb_xpT_{d}", W[f"mb_xpT_{d}"], [128, 96 * 16], stage="wstage1k", side="right")
        load_weight_bf16(P, A, f"mb_dtWT_{d}", W[f"mb_dtWT_{d}"], [64, 2048], stage="wstage1k", side="right")
        alog = A.sb(f"mb_A_{d}", [128, 256], F32, side="right")
        P.emit("sync", lambda e, alog=alog, d=d: e.dma_start(out=alog[:], in_=W[f"mb_Alog_{d}"][:, :]),
               writes=[f"mb_A_{d}"], dma=True)
        P.emit("scalar", lambda e, alog=alog: e.activation(out=alog[:], in_=alog[:], func=AF.Exp),
               reads=[f"mb_A_{d}"], writes=[f"mb_A_{d}"])
        P.emit("vector", lambda e, alog=alog: e.tensor_scalar(alog[:], alog[:], -1.0, None, op0=ALU.mult),
               reads=[f"mb_A_{d}"], writes=[f"mb_A_{d}"])
        dsb = A.sb(f"mb_D_{d}", [128, 16], F32, side="right")
        P.emit("sync", lambda e, dsb=dsb, d=d: e.dma_start(out=dsb[:], in_=W[f"mb_D_{d}"][:, :]),
               writes=[f"mb_D_{d}"], dma=True)
        dtb = A.sb(f"mb_dtb_{d}", [128, 16], F32, side="right")
        P.emit("sync", lambda e, dtb=dtb, d=d: e.dma_start(out=dtb[:], in_=W[f"mb_dtb_{d}"][:, :]),
               writes=[f"mb_dtb_{d}"], dma=True)
    load_weight_bf16(P, A, "mb_seln", W["mb_seln"], [32, 4096], stage="wstage1k", side="right")
    pjb = A.sb("mb_pjb", [128, 8], F32, side="right")
    P.emit("sync", lambda e: e.dma_start(out=pjb[:], in_=W["mb_pjb"][:, :]), writes=["mb_pjb"], dma=True)

    def p2_dir(d):
        mk2 = A.mark()
        xct = [A.sb(f"mbx{i}", [128, 512], BF16, side="right") for i in range(16)]
        mkxi = A.mark()
        xifull = A.sb("mb_xif", [128, 16 * 515], BF16, side="right")
        zfull = A.sb("mb_zf", [128, 16 * 512], BF16, side="right")
        cw, cbh = A.t[f"mb_cw_{d}"], A.t[f"mb_cbh_{d}"]
        cb = A.t[f"mb_cb_{d}"]
        alog, dsb, dtb = A.t[f"mb_A_{d}"], A.t[f"mb_D_{d}"], A.t[f"mb_dtb_{d}"]
        xpT = A.t[f"mb_xpT_{d}"]
        dtWT = A.t[f"mb_dtWT_{d}"]
        seln = A.t["mb_seln"]
        convt = A.sb("mb_cvt", [128, 512], F32, side="right")
        convt2 = A.sb("mb_cvt2", [128, 512], F32, side="right")
        xdbl = A.sb("mb_xdbl", [64, 512], BF16, side="right")
        bcsb = A.sb("mb_bcsb", [32, 512], BF16, side="right")
        BB = A.sb("mb_BB", [128, 16 * 512], BF16, side="right")
        CB = A.sb("mb_CB", [128, 16 * 512], BF16, side="right")
        hint = A.sb("mb_hint", [128, 16 * 512], BF16, side="right")
        carry = A.sb("mb_carry", [128, 256], BF16, side="right")
        dtsb = [A.sb(f"mb_dts{q}", [128, 512], BF16, side="right") for q in range(2)]
        dtxb = [A.sb(f"mb_dtx{q}", [128, 512], BF16, side="right") for q in range(2)]
        dAb = [A.sb(f"mb_dA{q}", [128, 512], F32, side="right") for q in range(2)]
        ut = A.sb("mb_ut", [128, 512], BF16, side="right")
        yt = A.sb("mb_yt", [128, 512], F32, side="right")
        zs = A.sb("mb_zs", [128, 512], F32, side="right")
        owcb = [A.sb(f"mb_owc{pp}", [128, 16 * 128], BF16, side="right") for pp in range(2)]
        mob = [A.sb(f"mb_mob{pp}", [128, 512], BF16, side="right") for pp in range(2)]
        for tc in range(NTC):
            s = slice(tc * 512, (tc + 1) * 512)
            if tc == 0:
                P.emit("vector", lambda e: e.memset(xifull[:], 0.0), writes=["mb_xif"])
                P.emit("sync", lambda e: e.dma_start(
                    out=xifull[:].rearrange("p (i t) -> p i t", t=515)[:, :, 3:515],
                    in_=hxi[d][:, 0:512].rearrange("(i p) t -> p i t", p=128)),
                    reads=["mb_spill"], writes=["mb_xif"], dma=True)
            else:
                P.emit("sync", lambda e, tc=tc: e.dma_start(
                    out=xifull[:].rearrange("p (i t) -> p i t", t=515),
                    in_=hxi[d][:, tc * 512 - 3:(tc + 1) * 512].rearrange("(i p) t -> p i t", p=128)),
                    reads=["mb_spill"], writes=["mb_xif"], dma=True)
            P.emit("gpsimd", lambda e, s=s: e.dma_start(
                out=zfull[:].rearrange("p (i t) -> p i t", t=512),
                in_=hz[d][:, s].rearrange("(i p) t -> p i t", p=128)),
                reads=["mb_spill"], writes=["mb_zf"], dma=True)
            for i in range(16):
                xc_ = xct[i]
                xo_ = i * 515
                P.emit("vector", lambda e, xo_=xo_, i=i: e.tensor_scalar(
                    convt[:], xifull[:, xo_:xo_ + 512], cw[:, 4 * i:4 * i + 1], None, op0=ALU.mult),
                    reads=["mb_xif"], sreads=[f"mb_cw_{d}"], writes=["mb_cvt"])
                for j in (1, 2, 3):
                    src_t = convt if j % 2 == 1 else convt2
                    dst_t = convt2 if j % 2 == 1 else convt
                    sk = "mb_cvt" if j % 2 == 1 else "mb_cvt2"
                    dk = "mb_cvt2" if j % 2 == 1 else "mb_cvt"
                    P.emit("vector", lambda e, xo_=xo_, i=i, j=j, src_t=src_t, dst_t=dst_t: e.scalar_tensor_tensor(
                        dst_t[:], xifull[:, xo_ + j:xo_ + 512 + j], cw[:, 4 * i + j:4 * i + j + 1], src_t[:],
                        op0=ALU.mult, op1=ALU.add),
                        reads=["mb_xif", sk], sreads=[f"mb_cw_{d}"], writes=[dk])
                P.emit("vector", lambda e, i=i: e.tensor_scalar(
                    convt[:], convt2[:], cb[:, i:i + 1], None, op0=ALU.add),
                    reads=["mb_cvt2"], sreads=[f"mb_cb_{d}"], writes=["mb_cvt"])
                P.emit("scalar", lambda e, i=i: e.activation(out=convt2[:], in_=convt2[:], func=AF.Tanh,
                                                             scale=0.5, bias=cbh[:, i:i + 1]),
                       reads=["mb_cvt2"], sreads=[f"mb_cbh_{d}"], writes=["mb_cvt2"])
                P.emit("vector", lambda e: e.tensor_scalar(convt2[:], convt2[:], 0.5, 0.5, op0=ALU.mult, op1=ALU.add),
                       reads=["mb_cvt2"], writes=["mb_cvt2"])
                P.emit("vector", lambda e, xc_=xc_, i=i: e.tensor_tensor(xc_[:], convt[:], convt2[:], op=ALU.mult),
                       reads=["mb_cvt", "mb_cvt2"], writes=[f"mbx{i}"])
            if DEBUG_ON and d == "f" and tc == 0:
                tap(P, A, "xc0", "mbx0", lambda: xct[0][:])
                tap(P, A, "xc15", "mbx15", lambda: xct[15][:])
            for i in range(16):
                xc_ = xct[i]
                P.emit("tensor", lambda e, xc_=xc_, i=i, st=(i == 0), sp=(i == 15): e.matmul(
                    ph[1][0:64, 0:512], xpT[:, i * 96:i * 96 + 64], xc_[:], start=st, stop=sp),
                    reads=[f"mb_xpT_{d}", f"mbx{i}"], writes=["ph1"])
                P.emit("tensor", lambda e, xc_=xc_, i=i, st=(i == 0), sp=(i == 15): e.matmul(
                    ph[2][0:32, 0:512], xpT[:, i * 96 + 64:i * 96 + 96], xc_[:], start=st, stop=sp),
                    reads=[f"mb_xpT_{d}", f"mbx{i}"], writes=["ph2"])
            P.emit("scalar", lambda e: e.activation(out=xdbl[:], in_=ph[1][0:64, 0:512], func=AF.Copy),
                   reads=["ph1"], writes=["mb_xdbl"])
            P.emit("scalar", lambda e: e.activation(out=bcsb[:], in_=ph[2][0:32, 0:512], func=AF.Copy),
                   reads=["ph2"], writes=["mb_bcsb"])
            for n in range(16):
                P.emit("tensor", lambda e, n=n: e.matmul(ph[3][:, 0:512], seln[:, n * 128:(n + 1) * 128], bcsb[:],
                                                         start=True, stop=True),
                       reads=["mb_seln", "mb_bcsb"], writes=["ph3"])
                P.emit("scalar", lambda e, n=n: e.activation(out=BB[:, n * 512:(n + 1) * 512], in_=ph[3][:, 0:512], func=AF.Copy),
                       reads=["ph3"], writes=["mb_BB"])
                P.emit("tensor", lambda e, n=n: e.matmul(ph[4][:, 0:512], seln[:, (16 + n) * 128:(17 + n) * 128], bcsb[:],
                                                         start=True, stop=True),
                       reads=["mb_seln", "mb_bcsb"], writes=["ph4"])
                P.emit("scalar", lambda e, n=n: e.activation(out=CB[:, n * 512:(n + 1) * 512], in_=ph[4][:, 0:512], func=AF.Copy),
                       reads=["ph4"], writes=["mb_CB"])
            for i in range(16):
                xc_ = xct[i]
                iq = i % 2
                dts, dtx = dtsb[iq], dtxb[iq]
                P.emit("tensor", lambda e, i=i: e.matmul(ph[5][:, 0:512], dtWT[:, i * 128:(i + 1) * 128], xdbl[:],
                                                         start=True, stop=True),
                       reads=[f"mb_dtWT_{d}", "mb_xdbl"], writes=["ph5"])
                P.emit("scalar", lambda e, i=i, dts=dts: e.activation(out=dts[:], in_=ph[5][:, 0:512], func=AF.Exp,
                                                                      bias=dtb[:, i:i + 1]),
                       reads=["ph5"], sreads=[f"mb_dtb_{d}"], writes=[f"mb_dts{iq}"])
                P.emit("scalar", lambda e, dts=dts: e.activation(out=dts[:], in_=dts[:], func=AF.Ln, bias=1.0),
                       reads=[f"mb_dts{iq}"], writes=[f"mb_dts{iq}"])
                P.emit("vector", lambda e, xc_=xc_, i=i, dts=dts, dtx=dtx: e.tensor_tensor(dtx[:], dts[:], xc_[:], op=ALU.mult),
                       reads=[f"mb_dts{iq}", f"mbx{i}"], writes=[f"mb_dtx{iq}"])
                if DEBUG_ON and d == "f" and tc == 0 and i == 0:
                    tap(P, A, "dts0", f"mb_dts{iq}", lambda: dts[:])
                for n in range(16):
                    nq = n % 2
                    dA = dAb[nq]
                    P.emit("scalar", lambda e, i=i, n=n, dA=dA, dts=dts: e.activation(
                        out=dA[:], in_=dts[:], func=AF.Exp, scale=alog[:, i * 16 + n:i * 16 + n + 1]),
                        reads=[f"mb_dts{iq}"], sreads=[f"mb_A_{d}"], writes=[f"mb_dA{nq}"])
                    P.emit("vector", lambda e, n=n, dtx=dtx: e.tensor_tensor(ut[:], dtx[:], BB[:, n * 512:(n + 1) * 512], op=ALU.mult),
                           reads=[f"mb_dtx{iq}", "mb_BB"], writes=["mb_ut"])
                    if tc == 0:
                        P.emit("vector", lambda e, i=i, n=n, dA=dA: e.tensor_tensor_scan(
                            hint[:, n * 512:(n + 1) * 512], dA[:], ut[:], 0.0, op0=ALU.mult, op1=ALU.add),
                            reads=[f"mb_dA{nq}", "mb_ut"], writes=["mb_hint"])
                    else:
                        P.emit("vector", lambda e, i=i, n=n, dA=dA: e.tensor_tensor_scan(
                            hint[:, n * 512:(n + 1) * 512], dA[:], ut[:], carry[:, i * 16 + n:i * 16 + n + 1],
                            op0=ALU.mult, op1=ALU.add),
                            reads=[f"mb_dA{nq}", "mb_ut"], sreads=["mb_carry"], writes=["mb_hint"])
                if tc < NTC - 1:
                    P.emit("vector", lambda e, i=i: e.tensor_copy(
                        carry[:, i * 16:(i + 1) * 16], hint[:, 511::512]),
                        reads=["mb_hint"], writes=["mb_carry"])
                P.emit("vector", lambda e, i=i: e.tensor_tensor(hint[:], hint[:], CB[:], op=ALU.mult),
                       reads=["mb_hint", "mb_CB"], writes=["mb_hint"])
                P.emit("vector", lambda e: e.tensor_reduce(
                    yt[:], hint[:].rearrange("p (n t) -> p t n", n=16), axis=AX.X, op=ALU.add),
                    reads=["mb_hint"], writes=["mb_yt"])
                P.emit("vector", lambda e, xc_=xc_, i=i: e.scalar_tensor_tensor(
                    yt[:], xc_[:], dsb[:, i:i + 1], yt[:], op0=ALU.mult, op1=ALU.add),
                    reads=[f"mbx{i}", "mb_yt"], sreads=[f"mb_D_{d}"], writes=["mb_yt"])
                if DEBUG_ON and d == "f" and tc == 0 and i == 0:
                    tap(P, A, "yt0", "mb_yt", lambda: yt[:])
                zsl = slice(i * 512, (i + 1) * 512)
                P.emit("scalar", lambda e, zsl=zsl: e.activation(out=zs[:], in_=zfull[:, zsl], func=AF.Tanh, scale=0.5),
                       reads=["mb_zf"], writes=["mb_zs"])
                P.emit("vector", lambda e: e.tensor_scalar(zs[:], zs[:], 0.5, 0.5, op0=ALU.mult, op1=ALU.add),
                       reads=["mb_zs"], writes=["mb_zs"])
                P.emit("vector", lambda e, zsl=zsl: e.tensor_tensor(zs[:], zs[:], zfull[:, zsl], op=ALU.mult),
                       reads=["mb_zs", "mb_zf"], writes=["mb_zs"])
                P.emit("vector", lambda e, xc_=xc_, i=i: e.tensor_tensor(xc_[:], yt[:], zs[:], op=ALU.mult),
                       reads=["mb_yt", "mb_zs"], writes=[f"mbx{i}"])
                if DEBUG_ON and d == "f" and tc == 0 and i == 0:
                    tap(P, A, "zs0", "mb_zs", lambda: zs[:])
                    tap(P, A, "yg0", "mbx0", lambda: xc_[:])
            for mc in range(8):
                pp = mc % 2
                owc = owcb[pp]
                pbk = "ph0" if mc % 2 == 0 else "ph3"
                pbt = ph[0] if mc % 2 == 0 else ph[3]
                P.emit("sync", lambda e, owc=owc, d=d, mc=mc: e.dma_start(
                    out=owc[:].rearrange("p (i c) -> p i c", c=128), in_=houtw[d][mc].rearrange("i p c -> p i c")),
                    reads=[f"outw{d}_blk"], writes=[f"mb_owc{pp}"], dma=True)
                for i in range(16):
                    xc_ = xct[i]
                    P.emit("tensor", lambda e, owc=owc, xc_=xc_, i=i, pbt=pbt, st=(i == 0), sp=(i == 15): e.matmul(
                        pbt[:, 0:512], owc[:, i * 128:(i + 1) * 128], xc_[:], start=st, stop=sp),
                        reads=[f"mb_owc{pp}", f"mbx{i}"], writes=[pbk])
                mo_ = mob[pp]
                if d == "f":
                    P.emit("scalar", lambda e, mo_=mo_, pbt=pbt: e.activation(out=mo_[:], in_=pbt[:, 0:512], func=AF.Copy),
                           reads=[pbk], writes=[f"mb_mob{pp}"])
                else:
                    P.emit("scalar", lambda e, mo_=mo_, pbt=pbt: e.activation(out=mo_[:, ::-1], in_=pbt[:, 0:512], func=AF.Copy),
                           reads=[pbk], writes=[f"mb_mob{pp}"])
                col = s if d == "f" else slice((NTC - 1 - tc) * 512, (NTC - tc) * 512)
                P.emit("gpsimd", lambda e, mc=mc, col=col, d=d, mo_=mo_: e.dma_start(
                    out=hmo[d][mc * 128:(mc + 1) * 128, col], in_=mo_[:]),
                    reads=[f"mb_mob{pp}"], writes=["mb_mo"], dma=True)
        P.fence()
        A.free_to(mk2)

    for d in dirs:
        p2_dir(d)
    mk3 = A.mark()
    motf = {dd: A.sb(f"mb_motf_{dd}", [128, 8 * 512], BF16, side="right") for dd in dirs}
    xtf = A.sb("mb_xtf", [128, 8 * 512], F32, side="right")
    pjcb = [A.sb(f"mb_pjc{pp}", [128, 16 * 128], BF16, side="right") for pp in range(2)]
    for tc in range(8):
        s = slice(tc * 512, (tc + 1) * 512)
        for dd in dirs:
            P.emit("sync", lambda e, dd=dd, s=s: e.dma_start(
                out=motf[dd][:].rearrange("p (r t) -> p r t", t=512),
                in_=hmo[dd][:, s].rearrange("(r p) t -> p r t", p=128)),
                reads=["mb_mo"], writes=[f"mb_motf_{dd}"], dma=True)
        P.emit("sync", lambda e, s=s: e.dma_start(
            out=xtf[:].rearrange("p (r t) -> p r t", t=512),
            in_=xres[:, s].rearrange("(r p) t -> p r t", p=128)),
            reads=["mb_xres"], writes=["mb_xtf"], dma=True)
        for dmc in range(NDM):
            pb = "ph4" if dmc % 2 == 0 else "ph5"
            pht_ = ph[4] if dmc % 2 == 0 else ph[5]
            pp = dmc % 2
            pjc = pjcb[pp]
            P.emit("gpsimd", lambda e, pjc=pjc, dmc=dmc: e.dma_start(
                out=pjc[:].rearrange("p (i c) -> p i c", c=128), in_=hprj[dmc].rearrange("i p c -> p i c")),
                reads=["prj_blk"], writes=[f"mb_pjc{pp}"], dma=True)
            for i in range(16):
                src_d = "f" if i < 8 else "b"
                mslice = slice((i % 8) * 512, (i % 8) * 512 + 512)
                P.emit("tensor", lambda e, pjc=pjc, i=i, pht_=pht_, src_d=src_d, mslice=mslice,
                       st=(i == 0), sp=(i == 15): e.matmul(
                    pht_[:, 0:512], pjc[:, i * 128:(i + 1) * 128], motf[src_d][:, mslice], start=st, stop=sp),
                    reads=[f"mb_pjc{pp}", f"mb_motf_{src_d}"], writes=[pb])
            xsl = slice(dmc * 512, (dmc + 1) * 512)
            P.emit("vector", lambda e, dmc=dmc, pht_=pht_, xsl=xsl: e.scalar_tensor_tensor(
                xtf[:, xsl], pht_[:, 0:512], pjb[:, dmc:dmc + 1], xtf[:, xsl], op0=ALU.add, op1=ALU.add),
                reads=[pb, "mb_xtf"], sreads=["mb_pjb"], writes=["mb_xtf"])
            P.emit("gpsimd", lambda e, dmc=dmc, s=s, xsl=xsl: e.dma_start(
                out=xout_ap[dmc * 128:(dmc + 1) * 128, s], in_=xtf[:, xsl]),
                reads=["mb_xtf"], writes=["mb_xout"], dma=True)
    P.fence()
    A.free_to(mk3)
    A.free_to(mkP)


def prep_l0ffn_weights(inputs, core_id):
    W = {}
    rw = np.asarray(inputs["L0_ffn_router_w"])
    W["moe_rw"] = np.ascontiguousarray(rw.reshape(8, 128, 8).transpose(1, 0, 2).reshape(128, 64))
    W["moe_rb"] = np.ascontiguousarray(np.asarray(inputs["L0_ffn_router_b"]).reshape(1, 8))
    W["moe_w1"] = np.ascontiguousarray(np.asarray(inputs["L0_ffn_w1"]))
    W["moe_w2"] = np.ascontiguousarray(np.asarray(inputs["L0_ffn_w2"]))
    W["moe_b1"] = np.ascontiguousarray(np.asarray(inputs["L0_ffn_b1"]).reshape(8, 32, 128).transpose(0, 2, 1))
    W["moe_b2"] = np.ascontiguousarray(np.asarray(inputs["L0_ffn_b2"]))
    L = np.zeros((128, 128), np.float32)
    for k in range(128):
        for p in range(128):
            if k // 16 == p // 16 and (k % 16) < (p % 16):
                L[k, p] = 1.0
    W["moe_L"] = L
    S = np.zeros((128, 8), np.float32)
    for p in range(128):
        S[p, p // 16] = 1.0
    W["moe_S"] = S
    W["moe_SE"] = np.ascontiguousarray(S.T)
    oh = np.zeros((128, 8), np.float32)
    for e_ in range(8):
        oh[e_ * 16 + 2 * core_id, e_] = 1.0
    W["moe_ownhot"] = oh
    return W


MOE_W_SHAPES = dict(
    moe_rw=[128, 64], moe_rb=[1, 8], moe_w1=[8, 1024, 4096], moe_w2=[8, 4096, 1024],
    moe_b1=[8, 128, 32], moe_b2=[8, 1024], moe_L=[128, 128], moe_S=[128, 8],
    moe_SE=[8, 128], moe_ownhot=[128, 8])


def build_sl1_standalone(n_cores=8):
    nc = bass.Bass("TRN2", target_bir_lowering=False)
    xin = nc.dram_tensor("xT", [D_MODEL, SEQ], F32, kind="ExternalInput").ap()
    xout = nc.dram_tensor("yT", [D_MODEL, SEQ], F32, kind="ExternalOutput").ap()
    Wd = {k: nc.dram_tensor(k, sh, F32, kind="ExternalInput").ap()
          for k, sh in MOE_W_SHAPES.items()}
    with ExitStack() as stack:
        A = Alloc(nc, stack)
        P = Prog(nc)
        g = build_common(P, A)
        A.sb("wstage1k", [128, 1024], F32)
        A.sb("wstage1k_bf", [128, 1024], BF16)
        xkeys = [f"x{i}" for i in range(NDM)]
        for i in range(NDM):
            x = A.sb(xkeys[i], [128, SEQ], F32)
            P.emit("sync", lambda e, x=x, i=i: e.dma_start(out=x[:], in_=xin[i * 128:(i + 1) * 128, :]),
                   writes=[xkeys[i]], dma=True)
        for pn in ["ph0", "ph1", "ph2", "ph3", "ph4", "ph5"]:
            A.ps(pn, [128, 512], F32)
        A.ps("phbf", [128, 1024], BF16)
        sl_l0ffn(P, A, g, Wd, xkeys, first_call=True)
        for nm, tk in DEBUG_TAPS:
            t = A.t[tk]
            dt_ = nc.dram_tensor("tap_" + nm, list(t[:].shape), t[:].dtype, kind="ExternalOutput").ap()
            P.emit("gpsimd", lambda e, dt_=dt_, t=t: e.dma_start(out=dt_, in_=t[:]),
                   reads=[tk], writes=["tapdma_" + nm], dma=True)
        for i in range(NDM):
            x = A.t[xkeys[i]]
            P.emit("gpsimd", lambda e, x=x, i=i: e.dma_start(out=xout[i * 128:(i + 1) * 128, :], in_=x[:]),
                   reads=[xkeys[i]], writes=[f"outdma{i}"], dma=True)
        P.barrier_read("gpsimd", [f"outdma{i}" for i in range(NDM)])
        finalize(P, nc, A)
        print(f"SBUF bytes: {A.bytes/1e6:.2f}MB, ops: {len(P.ops)}", flush=True)
    return nc


def prep_l1attn_weights(inputs):
    W = {}
    for d, pre in (("f", "L1_attn_fwd"), ("b", "L1_attn_bwd")):
        W[f"mb_inWT_{d}"] = np.ascontiguousarray(np.asarray(inputs[f"{pre}_in_W"]).T)
        cwr = np.asarray(inputs[f"{pre}_conv_w"])
        W[f"mb_convw_{d}"] = np.ascontiguousarray(
            cwr.reshape(16, 128, 4).transpose(1, 0, 2).reshape(128, 64))
        W[f"mb_convb_{d}"] = np.ascontiguousarray(
            np.asarray(inputs[f"{pre}_conv_b"]).reshape(16, 128).T)
        xp = np.asarray(inputs[f"{pre}_xproj_W"]).T
        W[f"mb_xpT_{d}"] = np.ascontiguousarray(xp.reshape(16, 128, 96).transpose(1, 0, 2).reshape(128, 1536))
        W[f"mb_dtWT_{d}"] = np.ascontiguousarray(np.asarray(inputs[f"{pre}_dt_W"]).T)
        W[f"mb_dtb_{d}"] = np.ascontiguousarray(np.asarray(inputs[f"{pre}_dt_b"]).reshape(16, 128).T)
        W[f"mb_Alog_{d}"] = np.ascontiguousarray(
            np.asarray(inputs[f"{pre}_A_log"]).reshape(16, 128, 16).transpose(1, 0, 2).reshape(128, 256))
        W[f"mb_D_{d}"] = np.ascontiguousarray(np.asarray(inputs[f"{pre}_D"]).reshape(16, 128).T)
        W[f"mb_outWT_{d}"] = np.ascontiguousarray(np.asarray(inputs[f"{pre}_out_W"]).T)
    W["mb_projT"] = np.ascontiguousarray(np.asarray(inputs["L1_attn_proj_W"]).T)
    W["mb_pjb"] = np.ascontiguousarray(np.asarray(inputs["L1_attn_proj_b"]).reshape(8, 128).T)
    seln = np.zeros((32, 32 * 128), np.float32)
    for r in range(32):
        seln[r, r * 128:(r + 1) * 128] = 1.0
    W["mb_seln"] = seln
    return W


MB_W_SHAPES = dict(
    mb_projT=[2048, 1024], mb_pjb=[128, 8], mb_seln=[32, 4096])
for _d in ("f", "b"):
    MB_W_SHAPES.update({
        f"mb_inWT_{_d}": [1024, 4096], f"mb_convw_{_d}": [128, 64], f"mb_convb_{_d}": [128, 16],
        f"mb_xpT_{_d}": [128, 1536], f"mb_dtWT_{_d}": [64, 2048], f"mb_dtb_{_d}": [128, 16],
        f"mb_Alog_{_d}": [128, 256], f"mb_D_{_d}": [128, 16], f"mb_outWT_{_d}": [2048, 1024]})


def build_sl5_standalone():
    nc = bass.Bass("TRN2", target_bir_lowering=False)
    xin = nc.dram_tensor("xT", [D_MODEL, SEQ], F32, kind="ExternalInput").ap()
    xout = nc.dram_tensor("yT", [D_MODEL, SEQ], F32, kind="ExternalOutput").ap()
    xres = nc.dram_tensor("hbm_xres", [D_MODEL, SEQ], F32).ap()
    Wd = {k: nc.dram_tensor(k, sh, F32, kind="ExternalInput").ap()
          for k, sh in MB_W_SHAPES.items()}
    with ExitStack() as stack:
        A = Alloc(nc, stack)
        P = Prog(nc)
        g = build_common(P, A)
        A.sb("wstage1k", [128, 1024], F32)
        A.sb("wstage1k_bf", [128, 1024], BF16)
        for pn in ["ph0", "ph1", "ph2", "ph3", "ph4", "ph5"]:
            A.ps(pn, [128, 512], F32)
        A.ps("phbf", [128, 1024], BF16)
        P2 = P
        xkeys = [f"x{i}" for i in range(NDM)]
        def alloc_x():
            for i in range(NDM):
                x = A.sb(xkeys[i], [128, SEQ], F32)
                P.emit("sync", lambda e, x=x, i=i: e.dma_start(out=x[:], in_=xin[i * 128:(i + 1) * 128, :]),
                       writes=[xkeys[i]], dma=True)
        sl_l1attn(P, A, g, Wd, xkeys, xres, xout, alloc_x=alloc_x)
        P.barrier_read("gpsimd", ["mb_xout"])
        finalize(P, nc, A)
        print(f"SBUF bytes: {A.bytes/1e6:.2f}MB, ops: {len(P.ops)}", flush=True)
    return nc
